# version 06
# baseline (speedup 1.0000x reference)
import contextlib
import numpy as np
import concourse.bass as bass
import concourse.mybir as mybir
from concourse.bass_utils import run_bass_kernel_spmd

"""Shared kernel-building library (development copy; final code is inlined
into kernel.py). Implements the full megakernel program for
mk_seq_preln_d1_jaxref_1024_a72310.

Layouts:
- tm tile: [128 tokens, F feats] (tokens on partitions)
- fm tile: [128 feats(chunk), T tokens] (feature chunk on partitions)
- HBM fm arrays: [1024(or F), 8192] rows=features (contiguous along tokens)
- HBM tm arrays: [8192, F]
"""
import numpy as np
import concourse.bass as bass
import concourse.mybir as mybir

F32 = mybir.dt.float32
BF16 = mybir.dt.bfloat16
AF = mybir.ActivationFunctionType
ALU = mybir.AluOpType
AX = mybir.AxisListType

B, L, D = 32, 2048, 1024
NCORE = 8
SPC = 4
TOK = SPC * L
C = 8
NT = TOK // 512
HEADS, HD = 16, 64
INNER, FH, FHD = 512, 8, 64
HID = 2730
WIN = 512
EPS = 1e-5

ENGINES = ("sync", "scalar", "vector", "tensor", "gpsimd")


class Sem:
    __slots__ = ("name", "h", "count")

    def __init__(self, name):
        self.name = name
        self.h = None
        self.count = 0


class Prog:
    def __init__(self):
        self.q = {e: [] for e in ENGINES}
        self.sems = []

    def sem(self, name):
        s = Sem(name)
        self.sems.append(s)
        return s

    def on(self, eng, build=None, wait=(), inc=()):
        w = [(s, v) for (s, v) in wait]
        i = [(s, k) for (s, k) in inc]
        self.q[eng].append((build, w, i))
        for s, k in i:
            s.count += k

    def barrier(self, sems=None, skip=()):
        sems = [s for s in (sems if sems is not None else self.sems)
                if s.count > 0 and s not in skip]
        for e in ENGINES:
            self.on(e, None, wait=[(s, s.count) for s in sems])

    def emit(self, nc):
        import contextlib
        with contextlib.ExitStack() as stack:
            for s in self.sems:
                s.h = stack.enter_context(nc.semaphore(s.name))
            block = stack.enter_context(nc.Block())

            def replay(name):
                def run(eng):
                    for build, waits, incs in self.q[name]:
                        for s, v in waits:
                            if v > 0:
                                eng.wait_ge(s.h, v)
                        if build is not None:
                            ins = build(eng)
                            for s, k in incs:
                                ins = ins.then_inc(s.h, k)
                        elif incs:
                            ins = eng.nop()
                            for s, k in incs:
                                ins = ins.then_inc(s.h, k)
                return run

            block.sync(replay("sync"))
            block.scalar(replay("scalar"))
            block.vector(replay("vector"))
            block.tensor(replay("tensor"))
            block.gpsimd(replay("gpsimd"))


def pack8(v):
    return np.ascontiguousarray(v.reshape(8, 128).T).astype(np.float32)


def make_consts():
    c = {}
    c["identb"] = np.eye(128, dtype=np.float32)
    c["onesrow"] = np.ones((1, 128), np.float32)
    c["onescol"] = np.ones((128, 1), np.float32)
    sel = np.zeros((128, 2), np.float32)
    sel[0:64, 0] = 1.0
    sel[64:128, 1] = 1.0
    c["sel"] = sel
    c["sel2"] = np.ascontiguousarray(sel.T)
    i_idx = np.arange(128)[:, None]
    j_idx = np.arange(128)[None, :]
    NEG = np.float32(-1e30)
    c["mA_prev"] = np.where(j_idx >= i_idx, 0.0, NEG).astype(np.float32)
    c["mA_cur"] = np.where(j_idx <= i_idx, 0.0, NEG).astype(np.float32)
    c["mB_prev"] = np.ascontiguousarray(c["mA_prev"].T)
    c["mB_cur"] = np.ascontiguousarray(c["mA_cur"].T)
    d = np.arange(64)
    inv_freq = 1.0 / 10000.0 ** (np.arange(0, 64, 2, dtype=np.float64) / 64.0)
    t = np.arange(L, dtype=np.float64)
    freqs = t[None, :] * inv_freq[d % 32][:, None]
    base = (np.arange(0, 64, 2, dtype=np.float64) + 0.4 * 64) / (1.4 * 64)
    power = (t - L // 2) / float(WIN // 2)
    scale = base[d % 32][:, None] ** power[None, :]
    cos, sin = np.cos(freqs), np.sin(freqs)
    sgn = np.where(d % 64 < 32, -1.0, 1.0)[:, None]
    CSQ = (cos * scale)
    SSQ = (sgn * sin * scale)
    CSK = (cos / scale)
    SSK = (sgn * sin / scale)
    for nm, a in (("CSQ", CSQ), ("SSQ", SSQ), ("CSK", CSK), ("SSK", SSK)):
        c[nm] = np.ascontiguousarray(np.tile(a, (2, 1))).astype(np.float32)
    return c


def prep_inputs(inp):
    g = {}
    X = np.ascontiguousarray(np.asarray(inp["L0_attn_x"], np.float32).reshape(B * L, D))
    g["X"] = X
    for w in ("Wq", "Wk", "Wv", "Wp"):
        g[w + "T"] = np.ascontiguousarray(np.asarray(inp["L0_attn_" + w], np.float32).T)
    for w in ("Wq", "Wk", "Wv"):
        g[w + "U"] = np.ascontiguousarray(np.asarray(inp["L0_attn_" + w], np.float32))
    for v in ("gq", "bq", "gk", "bk", "gv", "bv", "bp"):
        g[v] = pack8(np.asarray(inp["L0_attn_" + v], np.float32))
    wqkv = np.asarray(inp["L0_ffn_wqkv"], np.float32)
    g["wqkv0"], g["wqkv1"] = np.ascontiguousarray(wqkv[0]), np.ascontiguousarray(wqkv[1])
    wo = np.asarray(inp["L0_ffn_wo"], np.float32)
    g["wo0"], g["wo1"] = np.ascontiguousarray(wo[0]), np.ascontiguousarray(wo[1])
    w1 = np.asarray(inp["L0_ffn_w1"], np.float32)
    g["w1_0"], g["w1_1"] = np.ascontiguousarray(w1[0]), np.ascontiguousarray(w1[1])
    w2 = np.asarray(inp["L0_ffn_w2"], np.float32)
    w2p = np.zeros((2, 22 * 128, D), np.float32)
    w2p[:, :HID, :] = w2
    g["w2_0"], g["w2_1"] = np.ascontiguousarray(w2p[0]), np.ascontiguousarray(w2p[1])
    for l in range(2):
        qs = np.asarray(inp["L0_ffn_q_scale"], np.float32)[l]
        ks = np.asarray(inp["L0_ffn_k_scale"], np.float32)[l]
        g[f"qs{l}"] = np.tile(qs, 2)[:, None].astype(np.float32)
        g[f"ks{l}"] = np.tile(ks, 2)[:, None].astype(np.float32)
        g[f"lng{l}"] = pack8(np.asarray(inp["L0_ffn_ln_g"], np.float32)[l])
        g[f"lnb{l}"] = pack8(np.asarray(inp["L0_ffn_ln_b"], np.float32)[l])
        b1 = np.asarray(inp["L0_ffn_b1"], np.float32)[l]
        b1a = np.zeros(22 * 128, np.float32)
        b1a[:HID] = b1[:HID]
        b1g = np.zeros(22 * 128, np.float32)
        b1g[:HID] = b1[HID:]
        g[f"b1a{l}"] = np.ascontiguousarray(b1a.reshape(22, 128).T)
        g[f"b1g{l}"] = np.ascontiguousarray(b1g.reshape(22, 128).T)
        g[f"b2_{l}"] = pack8(np.asarray(inp["L0_ffn_b2"], np.float32)[l])
    g.update(make_consts())
    return g


"""Program builders for the megakernel (development copy).

Conventions:
- Prog.on(engine, build, wait=[(sem, val)], inc=[(sem, k)])
- PSUM bank rotation via Rot: acquire() returns (bank_tile, wait_list); after
  the last reader of the bank's contents, call rot.release(reader_sem_count).
"""
import contextlib
import numpy as np
import concourse.bass as bass
import concourse.mybir as mybir


class Rot:

    def __init__(self, banks):
        self.banks = banks
        self.rel = [None] * len(banks)
        self.i = 0

    def acquire(self):
        k = self.i % len(self.banks)
        self.i += 1
        w = [self.rel[k]] if self.rel[k] else []
        return k, self.banks[k], w

    def release(self, k, sem):
        self.rel[k] = (sem, sem.count)


class Ctx:
    def __init__(self, nc, p):
        self.nc = nc
        self.p = p
        self.stack = contextlib.ExitStack()
        for name in ("qL", "qS", "qW", "vC", "sC", "tC", "gC"):
            setattr(self, name, p.sem(name))

    def sb(self, name, shape, dt=F32):
        return self.stack.enter_context(self.nc.sbuf_tensor(name, shape, dt))

    def ps(self, name, shape, dt=F32):
        return self.stack.enter_context(self.nc.psum_tensor(name, shape, dt))

    def close(self):
        self.stack.close()


def cast_w(e, dst, src):
    R, F = src.shape
    nch = R // 128
    return e.dma_start(
        out=dst[:, 0:nch * F].rearrange("p (c f) -> p c f", c=nch),
        in_=src.rearrange("(c p) f -> p c f", p=128))


def fm_store(e, dram, t0, T, sb, nch=8):
    return e.dma_start(
        out=dram[:, t0:t0 + T].rearrange("(c p) t -> p c t", p=128),
        in_=sb[:, 0:nch * T].rearrange("p (c t) -> p c t", c=nch))


def fm_load(e, dram, t0, T, sb, nch=8):
    return e.dma_start(
        out=sb[:, 0:nch * T].rearrange("p (c t) -> p c t", c=nch),
        in_=dram[:, t0:t0 + T].rearrange("(c p) t -> p c t", p=128))


def build_launch_a(nc, stages="all", ntiles=NT):
    p = Prog()
    cx = Ctx(nc, p)
    d = {}
    d["x"] = nc.dram_tensor("x", [TOK, D], F32, kind="ExternalInput").ap()
    for w in ("WqT", "WkT", "WvT"):
        d[w] = nc.dram_tensor(w, [D, D], F32, kind="ExternalInput").ap()
    d["identb"] = nc.dram_tensor("identb", [128, 128], F32, kind="ExternalInput").ap()
    d["bnp"] = nc.dram_tensor("bnp", [128, 48], F32, kind="ExternalOutput").ap()
    d["rows"] = nc.dram_tensor("rows", [2 * TOK], F32, kind="ExternalOutput").ap()
    d["cols"] = nc.dram_tensor("cols", [128, 128], F32, kind="ExternalOutput").ap()

    idb = cx.sb("idb", [128, 128], BF16)
    idf = cx.sb("idf", [128, 128], F32)
    W = {w: cx.sb("s" + w, [128, C * D], BF16) for w in ("WqT", "WkT", "WvT")}
    xin = [cx.sb(f"xin{i}", [128, 4 * D], F32) for i in range(2)]
    xlnf = [cx.sb(f"xlnf{i}", [128, C * 512], BF16) for i in range(2)]
    bnbuf = cx.sb("bnbuf", [128, 12], F32)
    mv = cx.sb("mv", [128, 2], F32)
    rcol = cx.sb("rcol", [128, 1], F32)
    nmr = cx.sb("nmr", [128, 1], F32)
    stash = cx.sb("stash", [128, 8], F32)
    rowb = cx.sb("rowb", [8, 128], F32)
    acc = cx.sb("acc", [128, 48], F32)
    tmpc = cx.sb("tmpc", [128, 1], F32)
    sqc = cx.sb("sqc", [128, 1], F32)
    epsc = cx.sb("epsc", [128, 1], F32)
    trash = cx.sb("trash", [128, 512], BF16)
    xln_tm = cx.sb("xln_tm", [128, D], BF16)
    psT = cx.ps("psT", [128, D], BF16)
    psH = Rot([cx.ps(f"psH{i}", [128, 512], F32) for i in range(2)])
    psR = cx.ps("psR", [8, 128], F32)

    p.on("gpsimd", lambda e: e.dma_start(out=idb[:], in_=d["identb"][:, :]), inc=[(cx.qW, 16)])
    p.on("sync", lambda e: e.dma_start(out=idf[:], in_=d["identb"][:, :]), inc=[(cx.qL, 16)])
    for w in ("WqT", "WkT", "WvT"):
        p.on("gpsimd", lambda e, w=w: cast_w(e, W[w], d[w]), inc=[(cx.qW, 16)])
    wl = cx.qW.count
    p.on("vector", lambda e: e.memset(acc[:], 0.0))
    p.on("vector", lambda e: e.memset(epsc[:], EPS), inc=[(cx.vC, 1)])

    xin_free = [None, None]
    xlnf_free = [None, None]
    for i in range(ntiles):
        bi = i % 2
        t0 = i * 512
        p.on("sync", lambda e, t0=t0, bi=bi: e.dma_start(
            out=xin[bi][:].rearrange("p (g d) -> p g d", g=4),
            in_=d["x"][t0:t0 + 512, :].rearrange("(g p) d -> p g d", p=128)),
            wait=([xin_free[bi]] if xin_free[bi] else []), inc=[(cx.qL, 16)])
        ld_t = cx.qL.count
        if xlnf_free[bi]:
            p.on("vector", None, wait=[xlnf_free[bi]])
        for gi in range(4 if stages != "ld" else 0):
            xg = xin[bi][:, gi * D:(gi + 1) * D]
            p.on("vector", lambda e, xg=xg: e.bn_stats(bnbuf[:, 0:6], xg[:, 0:512]),
                 wait=[(cx.qL, ld_t)])
            p.on("vector", lambda e, xg=xg: e.bn_stats(bnbuf[:, 6:12], xg[:, 512:1024]),
                 inc=[(cx.vC, 1)])
            p.on("vector", lambda e: e.bn_aggr(mv[:, 0:2], bnbuf[:, 0:12]),
                 wait=[(cx.sC, cx.sC.count), (cx.vC, cx.vC.count)], inc=[(cx.vC, 1)])
            p.on("scalar", lambda e: e.activation(rcol[:], mv[:, 1:2], AF.Ln, bias=epsc[:, 0:1]),
                 wait=[(cx.vC, cx.vC.count)], inc=[(cx.sC, 1)])
            p.on("scalar", lambda e: e.activation(rcol[:], rcol[:], AF.Exp, scale=-0.5),
                 inc=[(cx.sC, 1)])
            p.on("vector", lambda e: e.tensor_scalar(nmr[:], mv[:, 0:1], rcol[:, 0:1], -1.0,
                                                     ALU.mult, ALU.mult),
                 wait=[(cx.sC, cx.sC.count)])
            p.on("vector", lambda e, gi=gi: e.tensor_copy(stash[:, gi:gi + 1], mv[:, 0:1]))
            p.on("vector", lambda e, gi=gi: e.tensor_copy(stash[:, 4 + gi:5 + gi],
                                                          rcol[:, 0:1]), inc=[(cx.vC, 1)])
            p.on("vector", lambda e, xg=xg: e.tensor_scalar(
                xln_tm[:], xg, rcol[:, 0:1], nmr[:, 0:1], ALU.mult, ALU.add),
                wait=[(cx.tC, cx.tC.count)], inc=[(cx.vC, 1)])
            if stages == "ln":
                continue
            p.on("tensor", None, wait=[(cx.vC, cx.vC.count)])
            for cc in range(C):
                last = cc == C - 1
                p.on("tensor", lambda e, cc=cc: e.transpose(
                    psT[:, cc * 128:(cc + 1) * 128], xln_tm[:, cc * 128:(cc + 1) * 128],
                    idb[:]), inc=([(cx.tC, 1)] if last else []))
            p.on("vector", lambda e, bi=bi, gi=gi: e.tensor_copy(
                xlnf[bi][:].rearrange("p (c t) -> p c t", c=C)[:, :, gi * 128:(gi + 1) * 128],
                psT[:].rearrange("p (c t) -> p c t", c=C)),
                wait=[(cx.tC, cx.tC.count)], inc=[(cx.vC, 1)])
        xvread = cx.vC.count
        p.on("tensor", None, wait=[(cx.vC, xvread), (cx.qW, wl)])
        for br, w in (enumerate(("WqT", "WkT", "WvT")) if stages in ("all", "mm", "mm_nosq") else []):
            for ec in range(C):
                kb, pb, wv = psH.acquire()
                if wv:
                    p.on("tensor", None, wait=wv)
                for dc in range(C):
                    p.on("tensor", lambda e, w=w, ec=ec, dc=dc, pb=pb, bi=bi: e.matmul(
                        pb[:], W[w][:, dc * D + ec * 128:dc * D + (ec + 1) * 128],
                        xlnf[bi][:, dc * 512:(dc + 1) * 512],
                        start=(dc == 0), stop=(dc == C - 1)),
                        inc=([(cx.tC, 1)] if dc == C - 1 else []))
                tw = cx.tC.count
                col = br * 16 + ec
                p.on("vector", lambda e, pb=pb: e.reduce_sum(tmpc[:], pb[:], axis=AX.X),
                     wait=[(cx.tC, tw)], inc=[(cx.vC, 1)])
                p.on("vector", lambda e, col=col: e.tensor_add(
                    acc[:, col:col + 1], acc[:, col:col + 1], tmpc[:]),
                    wait=[(cx.vC, cx.vC.count)], inc=[(cx.vC, 1)])
                p.on("scalar", lambda e, pb=pb: e.activation(
                    trash[:], pb[:], AF.Square, accum_out=sqc[:]),
                    wait=[(cx.tC, tw), (cx.vC, cx.vC.count)], inc=[(cx.sC, 1)])
                p.on("vector", lambda e, col=col: e.tensor_add(
                    acc[:, col + 8:col + 9], acc[:, col + 8:col + 9], sqc[:]),
                    wait=[(cx.sC, cx.sC.count)], inc=[(cx.vC, 1)])
                psH.release(kb, cx.vC)
                psH.rel[kb] = (cx.vC, cx.vC.count)
        xlnf_free[bi] = (cx.tC, cx.tC.count)
        xin_free[bi] = (cx.vC, xvread)
        if stages in ("ld",):
            xin_free[bi] = (cx.qL, cx.qL.count)
            continue
        p.on("tensor", lambda e: e.transpose(psR[:], stash[:], idf[:]),
             wait=[(cx.vC, cx.vC.count)], inc=[(cx.tC, 1)])
        p.on("vector", lambda e: e.tensor_copy(rowb[:], psR[:]),
             wait=[(cx.tC, cx.tC.count)], inc=[(cx.vC, 1)])
        rv = (cx.vC, cx.vC.count)
        p.on("sync", lambda e, t0=t0: e.dma_start(
            out=d["rows"][t0:t0 + 512].rearrange("(g q) -> g q", g=4),
            in_=rowb[0:4, :]), wait=[rv], inc=[(cx.qS, 16)])
        p.on("sync", lambda e, t0=t0: e.dma_start(
            out=d["rows"][TOK + t0:TOK + t0 + 512].rearrange("(g q) -> g q", g=4),
            in_=rowb[4:8, :]), inc=[(cx.qS, 16)])
        p.on("sync", lambda e, i=i: e.dma_start(
            out=d["cols"][:, i * 4:(i + 1) * 4], in_=stash[:, 0:4]),
            wait=[rv], inc=[(cx.qS, 16)])
        p.on("sync", lambda e, i=i: e.dma_start(
            out=d["cols"][:, 64 + i * 4:64 + (i + 1) * 4], in_=stash[:, 4:8]),
            inc=[(cx.qS, 16)])
        stash_free = (cx.qS, cx.qS.count)
        p.on("vector", None, wait=[stash_free])
    p.on("sync", lambda e: e.dma_start(out=d["bnp"][:, :], in_=acc[:]),
         wait=[(cx.vC, cx.vC.count)], inc=[(cx.qS, 16)])
    p.on("sync", None, wait=[(cx.qS, cx.qS.count)])
    p.emit(nc)
    cx.close()
    return d



class Main:

    def __init__(self, nc, fused=False, debug=False,
                 phases=("S", "P23", "Fa0", "Fbc0", "Fa1", "Fbc1"), nseq=SPC):
        self.nseq = nseq
        self.nc = nc
        self.fused = fused
        self.debug = debug
        self.phases = phases
        self.p = Prog()
        self.cx = Ctx(nc, self.p)
        self.d = {}
        self._ysb_free = [None, None]
        self._otm_free = None
        self._tab_free = None
        self._avfree = None
        self._nmfree = None
        self._expw2 = [[None, None], [None, None]]
        self._attfree = [[None, None], [None, None]]

    def sb(self, name, shape, dt=F32):
        return self.cx.sb(name, shape, dt)

    def ps(self, name, shape, dt=F32):
        return self.cx.ps(name, shape, dt)

    def on(self, eng, build=None, wait=(), inc=()):
        self.p.on(eng, build, wait=wait, inc=inc)

    def w(self, sem):
        return (sem, sem.count)

    def declare_io(self):
        nc, d = self.nc, self.d
        DI = lambda n, sh: nc.dram_tensor(n, sh, F32, kind="ExternalInput").ap()
        d["x"] = DI("x", [(B * L) if self.fused else TOK, D])
        for w in ("WqT", "WkT", "WvT", "WpT"):
            d[w] = DI(w, [D, D])
        for v in ("gq", "bq", "gk", "bk", "gv", "bv", "bp"):
            d[v] = DI(v, [128, 8])
        for l in range(2):
            d[f"wqkv{l}"] = DI(f"wqkv{l}", [D, 1536])
            d[f"wo{l}"] = DI(f"wo{l}", [512, D])
            d[f"w1_{l}"] = DI(f"w1_{l}", [D, 5460])
            d[f"w2_{l}"] = DI(f"w2_{l}", [22 * 128, D])
            d[f"qs{l}"] = DI(f"qs{l}", [128, 1])
            d[f"ks{l}"] = DI(f"ks{l}", [128, 1])
            d[f"lng{l}"] = DI(f"lng{l}", [128, 8])
            d[f"lnb{l}"] = DI(f"lnb{l}", [128, 8])
            d[f"b1a{l}"] = DI(f"b1a{l}", [128, 22])
            d[f"b1g{l}"] = DI(f"b1g{l}", [128, 22])
            d[f"b2_{l}"] = DI(f"b2_{l}", [128, 8])
        d["identb"] = DI("identb", [128, 128])
        d["onesrow"] = DI("onesrow", [1, 128])
        d["onescol"] = DI("onescol", [128, 1])
        d["sel"] = DI("sel", [128, 2])
        d["sel2"] = DI("sel2", [2, 128])
        for m in ("mA_prev", "mA_cur", "mB_prev", "mB_cur"):
            d[m] = DI(m, [128, 128])
        for t in ("CSQ", "SSQ", "CSK", "SSK"):
            d[t] = DI(t, [128, L])
        if not self.fused:
            d["bnps"] = DI("bnps", [8 * 128, 48])
            d["ln_rows"] = DI("ln_rows", [2 * TOK])
            d["ln_cols"] = DI("ln_cols", [128, 128])
        else:
            for wn in ("Wq", "Wk", "Wv"):
                d[wn + "U"] = DI(wn + "U", [D, D])
        d["out"] = nc.dram_tensor("out", [D, TOK], F32, kind="ExternalOutput").ap()
        kind = "ExternalOutput" if self.debug else "Internal"
        DSC = lambda n, sh, dt=F32: nc.dram_tensor(n, sh, dt, kind=kind).ap()
        d["XLN"] = DSC("XLN", [D, TOK], BF16)
        d["X0F"] = DSC("X0F", [D, TOK])
        d["X1"] = DSC("X1", [D, TOK])
        d["S1"] = DSC("S1", [2 * TOK])
        d["Y1"] = DSC("Y1", [D, TOK])
        d["HLN0"] = DSC("HLN0", [D, TOK], BF16)
        d["Y2"] = DSC("Y2", [D, TOK])
        d["Y3"] = DSC("Y3", [D, TOK])
        d["HLN1"] = DSC("HLN1", [D, TOK], BF16)
        if self.fused:
            d["YTM"] = DSC("YTM", [B * L, D], BF16)
        if self.debug:
            d["DBGA"] = DSC("DBGA", [128, 48])
            d["DBGS"] = DSC("DBGS", [128, 64])
            d["DBGSS"] = DSC("DBGSS", [128, C * D], BF16)
            d["DBGM"] = DSC("DBGM", [128, C * D], BF16)
            d["DBGQ"] = DSC("DBGQ", [128, C * 512], BF16)
            d["DBGK"] = DSC("DBGK", [128, C * 512], BF16)

    def persistent(self):
        cx, p, d = self.cx, self.p, self.d
        self.idb = self.sb("idb", [128, 128], BF16)
        self.idf = self.sb("idf", [128, 128], F32)
        self.ones_r = self.sb("ones_r", [1, 128], BF16)
        self.ones_c = self.sb("ones_c", [128, 1], BF16)
        self.sel_b = self.sb("sel_b", [128, 2], BF16)
        self.sel2_b = self.sb("sel2_b", [2, 128], BF16)
        self.mA = {"prev": self.sb("mAp", [128, 128], BF16),
                   "cur": self.sb("mAc", [128, 128], BF16)}
        self.mB = {"prev": self.sb("mBp", [128, 128], BF16),
                   "cur": self.sb("mBc", [128, 128], BF16)}
        self.epsc = self.sb("epsc", [128, 1], F32)
        self.AB = {n: self.sb("AB" + n, [128, 8], F32)
                   for n in ("Aq", "Bq", "Ak", "Bk", "Av", "Bv")}
        self.bpc = self.sb("bpc", [128, 8], F32)
        self.nmrc = self.sb("nmrc", [128, 64], F32)
        self.rc = self.sb("rc", [128, 128], F32)
        for dst, src in ((self.idb, "identb"), (self.ones_r, "onesrow"),
                         (self.ones_c, "onescol"), (self.sel_b, "sel"),
                         (self.sel2_b, "sel2"),
                         (self.mA["prev"], "mA_prev"), (self.mA["cur"], "mA_cur"),
                         (self.mB["prev"], "mB_prev"), (self.mB["cur"], "mB_cur")):
            p.on("gpsimd", lambda e, dst=dst, src=src: e.dma_start(
                out=dst[:], in_=d[src][:, :]), inc=[(cx.qW, 16)])
        p.on("sync", lambda e: e.dma_start(out=self.idf[:], in_=d["identb"][:, :]),
             inc=[(cx.qL, 16)])
        p.on("sync", lambda e: e.dma_start(out=self.bpc[:], in_=d["bp"][:, :]),
             inc=[(cx.qL, 16)])
        p.on("vector", lambda e: e.memset(self.epsc[:], EPS), inc=[(cx.vC, 1)])

    def phase_stats(self):
        cx, p, d = self.cx, self.p, self.d
        bn8 = self.sb("bn8", [128, 8 * 48], F32)
        accs = self.sb("accs", [128, 48], F32)
        t1 = self.sb("St1", [128, 8], F32)
        t2 = self.sb("St2", [128, 8], F32)
        t3 = self.sb("St3", [128, 8], F32)
        gcol = self.sb("gcol", [128, 8], F32)
        bcol = self.sb("bcol", [128, 8], F32)
        p.on("sync", lambda e: e.dma_start(
            out=bn8[:].rearrange("p (c f) -> p c f", c=8),
            in_=d["bnps"][:, :].rearrange("(c p) f -> p c f", p=128)), inc=[(cx.qL, 16)])
        p.on("sync", lambda e: e.dma_start(out=self.rc[:], in_=d["ln_cols"][:, :]),
             inc=[(cx.qL, 16)])
        lded = cx.qL.count
        p.on("vector", lambda e: e.tensor_add(accs[:], bn8[:, 0:48], bn8[:, 48:96]),
             wait=[(cx.qL, lded)], inc=[(cx.vC, 1)])
        for cc in range(2, 8):
            p.on("vector", lambda e, cc=cc: e.tensor_add(
                accs[:], accs[:], bn8[:, cc * 48:(cc + 1) * 48]),
                wait=[self.w(cx.vC)], inc=[(cx.vC, 1)])
        NTOT = float(B * L)
        for br, (gn, bnm, An, Bn) in enumerate((("gq", "bq", "Aq", "Bq"),
                                                ("gk", "bk", "Ak", "Bk"),
                                                ("gv", "bv", "Av", "Bv"))):
            s0, s1_ = br * 16, br * 16 + 8
            p.on("sync", lambda e, gn=gn: e.dma_start(out=gcol[:], in_=d[gn][:, :]),
                 wait=[self.w(cx.vC), self.w(cx.qS)], inc=[(cx.qL, 16)])
            p.on("sync", lambda e, bnm=bnm: e.dma_start(out=bcol[:], in_=d[bnm][:, :]),
                 inc=[(cx.qL, 16)])
            lb = cx.qL.count
            p.on("vector", lambda e, s0=s0: e.tensor_scalar_mul(
                t1[:], accs[:, s0:s0 + 8], 1.0 / NTOT),
                wait=[(cx.qL, lb), self.w(cx.vC), self.w(cx.sC)], inc=[(cx.vC, 1)])
            p.on("vector", lambda e, s1_=s1_: e.tensor_scalar_mul(
                t2[:], accs[:, s1_:s1_ + 8], 1.0 / NTOT), inc=[(cx.vC, 1)])
            p.on("vector", lambda e: e.tensor_mul(t3[:], t1[:], t1[:]),
                 wait=[self.w(cx.vC)], inc=[(cx.vC, 1)])
            p.on("vector", lambda e: e.tensor_sub(t2[:], t2[:], t3[:]),
                 wait=[self.w(cx.vC)], inc=[(cx.vC, 1)])
            p.on("scalar", lambda e: e.activation(t2[:], t2[:], AF.Ln, bias=self.epsc[:, 0:1]),
                 wait=[self.w(cx.vC)], inc=[(cx.sC, 1)])
            p.on("scalar", lambda e: e.activation(t2[:], t2[:], AF.Exp, scale=-0.5),
                 wait=[self.w(cx.sC)], inc=[(cx.sC, 1)])
            p.on("vector", lambda e, An=An: e.tensor_mul(self.AB[An][:], gcol[:], t2[:]),
                 wait=[self.w(cx.sC)], inc=[(cx.vC, 1)])
            p.on("vector", lambda e, An=An: e.tensor_scalar_mul(
                self.AB[An][:], self.AB[An][:], 4.0), wait=[self.w(cx.vC)], inc=[(cx.vC, 1)])
            p.on("vector", lambda e, An=An: e.tensor_mul(t3[:], t1[:], self.AB[An][:]),
                 wait=[self.w(cx.vC)], inc=[(cx.vC, 1)])
            p.on("vector", lambda e, Bn=Bn: e.tensor_scalar(
                self.AB[Bn][:], bcol[:], 4.0, -4.0, ALU.mult, ALU.add), inc=[(cx.vC, 1)])
            p.on("vector", lambda e, Bn=Bn: e.tensor_sub(self.AB[Bn][:], self.AB[Bn][:], t3[:]),
                 wait=[self.w(cx.vC)], inc=[(cx.vC, 1)])
            if self.debug and br == 0:
                p.on("sync", lambda e: e.dma_start(out=d["DBGS"][:, 48:56], in_=t2[:]),
                     wait=[self.w(cx.vC), self.w(cx.sC)], inc=[(cx.qS, 16)])
                p.on("sync", lambda e: e.dma_start(out=d["DBGS"][:, 56:64], in_=gcol[:]),
                     inc=[(cx.qS, 16)])
                p.on("sync", lambda e, An=An: e.dma_start(out=d["DBGS"][:, 40:48],
                     in_=self.AB[An][:]), inc=[(cx.qS, 16)])
        p.on("vector", lambda e: e.tensor_mul(self.nmrc[:], self.rc[:, 0:64],
                                              self.rc[:, 64:128]),
             wait=[(cx.qL, lded)], inc=[(cx.vC, 1)])
        p.on("vector", lambda e: e.tensor_scalar_mul(self.nmrc[:], self.nmrc[:], -1.0),
             wait=[self.w(cx.vC)], inc=[(cx.vC, 1)])
        if self.debug:
            p.on("sync", lambda e: e.dma_start(out=d["DBGS"][:, 0:48], in_=accs[:]),
                 wait=[self.w(cx.vC), self.w(cx.sC)], inc=[(cx.qS, 16)])
            for i, n in enumerate(("Aq", "Bq", "Ak", "Bk", "Av", "Bv")):
                p.on("sync", lambda e, i=i, n=n: e.dma_start(
                    out=d["DBGA"][:, i * 8:(i + 1) * 8], in_=self.AB[n][:]),
                    wait=[self.w(cx.vC), self.w(cx.sC)], inc=[(cx.qS, 16)])
        self.p.barrier()

    def phase_p1(self):
        cx, p, d = self.cx, self.p, self.d
        vC, sC, tC = cx.vC, cx.sC, cx.tC
        W = self.w
        NALL = (B * L) // 512
        with contextlib.ExitStack() as st:
            sb = lambda n, sh, dt=F32: st.enter_context(self.nc.sbuf_tensor(n, sh, dt))
            ps = lambda n, sh, dt=F32: st.enter_context(self.nc.psum_tensor(n, sh, dt))
            xin = [sb(f"p1xin{i}", [128, 4 * D], F32) for i in range(2)]
            ytile = [sb(f"p1y{i}", [128, 4 * D], BF16) for i in range(2)]
            bnbuf = [sb(f"p1bn{i}", [128, 12], F32) for i in range(2)]
            mv = [sb(f"p1mv{i}", [128, 2], F32) for i in range(2)]
            rcol = [sb(f"p1rcol{i}", [128, 1], F32) for i in range(2)]
            nmr = [sb(f"p1nmr{i}", [128, 1], F32) for i in range(2)]
            onebc = sb("p1oneb", [128, 1], BF16)
            psG = {mi: ps(f"p1g{mi}", [128, min(1024, ((8 - mi) * 128 + 511) // 512 * 512)])
                   for mi in (0, 1, 6, 7)}
            psMean = ps("p1mean", [1, 1024])
            inS = [p.sem("p1in0"), p.sem("p1in1")]
            ysS = [p.sem("p1ys0"), p.sem("p1ys1")]
            p.on("vector", lambda e: e.memset(onebc[:], 1.0), inc=[(vC, 1)])
            xin_free = [None, None]
            yt_free = [None, None]
            chain_free = [None, None]
            for i in range(NALL):
                bi = i % 2
                t0 = i * 512
                p.on("sync", lambda e, t0=t0, bi=bi: e.dma_start(
                    out=xin[bi][:].rearrange("p (g dd) -> p g dd", g=4),
                    in_=d["x"][t0:t0 + 512, :].rearrange("(g p) dd -> p g dd", p=128)),
                    wait=(xin_free[bi] or []), inc=[(inS[bi], 16)])
                ld_t = (inS[bi], inS[bi].count)
                if yt_free[bi]:
                    p.on("scalar", None, wait=yt_free[bi])
                    p.on("vector", None, wait=yt_free[bi])
                for gi in range(4):
                    par = gi % 2
                    xg = xin[bi][:, gi * D:(gi + 1) * D]
                    yg = ytile[bi][:, gi * D:(gi + 1) * D]
                    p.on("vector", lambda e, xg=xg, par=par: e.bn_stats(
                        bnbuf[par][:, 0:6], xg[:, 0:512]), wait=[ld_t])
                    p.on("vector", lambda e, xg=xg, par=par: e.bn_stats(
                        bnbuf[par][:, 6:12], xg[:, 512:1024]), inc=[(vC, 1)])
                    bnw = (vC, vC.count)
                    p.on("vector", lambda e, par=par: e.bn_aggr(
                        mv[par][:, 0:2], bnbuf[par][:, 0:12]),
                        wait=[bnw] + ([chain_free[par]] if chain_free[par] else []),
                        inc=[(vC, 1)])
                    aggw = (vC, vC.count)
                    p.on("scalar", lambda e, par=par: e.activation(
                        rcol[par][:], mv[par][:, 1:2], AF.Ln, bias=self.epsc[:, 0:1]),
                        wait=[aggw], inc=[(sC, 1)])
                    p.on("scalar", lambda e, par=par: e.activation(
                        rcol[par][:], rcol[par][:], AF.Exp, scale=-0.5),
                        wait=[W(sC)], inc=[(sC, 1)])
                    rw = (sC, sC.count)
                    p.on("vector", lambda e, par=par: e.tensor_scalar(
                        nmr[par][:], mv[par][:, 0:1], rcol[par][:, 0:1], -1.0,
                        ALU.mult, ALU.mult), wait=[rw], inc=[(vC, 1)])
                    nw = (vC, vC.count)
                    if i < NT:
                        idx = i * 4 + gi
                        p.on("vector", lambda e, idx=idx, par=par: e.tensor_copy(
                            self.rc[:, idx:idx + 1], mv[par][:, 0:1]),
                            wait=[aggw], inc=[(vC, 1)])
                        p.on("vector", lambda e, idx=idx, par=par: e.tensor_copy(
                            self.rc[:, 64 + idx:65 + idx], rcol[par][:, 0:1]),
                            wait=[rw], inc=[(vC, 1)])
                    if par == 0:
                        p.on("scalar", lambda e, xg=xg, yg=yg, par=par: e.activation(
                            yg, xg, AF.Identity, bias=nmr[par][:, 0:1],
                            scale=rcol[par][:, 0:1]), wait=[nw], inc=[(sC, 1)])
                        chain_free[par] = (sC, sC.count)
                    else:
                        p.on("vector", lambda e, xg=xg, yg=yg, par=par: e.tensor_scalar(
                            yg, xg, rcol[par][:, 0:1], nmr[par][:, 0:1],
                            ALU.mult, ALU.add), wait=[nw, rw], inc=[(vC, 1)])
                        chain_free[par] = (vC, vC.count)
                yrdy = [(sC, sC.count), (vC, vC.count)]
                p.on("gpsimd", lambda e, bi=bi, t0=t0: e.dma_start(
                    out=d["YTM"][t0:t0 + 512, :].rearrange("(g p) dd -> p g dd", p=128),
                    in_=ytile[bi][:].rearrange("p (g dd) -> p g dd", g=4)),
                    wait=yrdy, inc=[(ysS[bi], 16)])
                p.on("tensor", None, wait=yrdy + ([(vC, 1)] if i == 0 else []))
                for gi in range(4):
                    yg = ytile[bi][:, gi * D:(gi + 1) * D]
                    first = (i == 0 and gi == 0)
                    last = (i == NALL - 1 and gi == 3)
                    for mi in (0, 1, 6, 7):
                        w0 = mi * 128
                        wdt = 1024 - w0
                        for (off, wp) in ([(0, 512), (512, wdt - 512)] if wdt > 512
                                          else [(0, wdt)]):
                            p.on("tensor", lambda e, yg=yg, mi=mi, w0=w0, off=off, wp=wp,
                                 first=first, last=last: e.matmul(
                                psG[mi][:, off:off + wp],
                                yg[:, mi * 128:(mi + 1) * 128],
                                yg[:, w0 + off:w0 + off + wp],
                                start=first, stop=last))
                    for half in range(2):
                        p.on("tensor", lambda e, yg=yg, half=half, first=first, last=last:
                             e.matmul(psMean[0:1, half * 512:(half + 1) * 512],
                                      onebc[:], yg[:, half * 512:(half + 1) * 512],
                                      start=first, stop=last),
                             inc=([(tC, 1)] if (half == 1) else []))
                gdone = W(tC)
                xin_free[bi] = yrdy
                yt_free[bi] = [gdone, (ysS[bi], ysS[bi].count)]
            p.on("scalar", lambda e: e.copy(self.Ssb[:, 0 * D:1 * D], psG[0][:, 0:1024]),
                 wait=[gdone], inc=[(sC, 1)])
            p.on("scalar", lambda e: e.copy(self.Ssb[:, 1 * D + 128:2 * D], psG[1][:, 0:896]),
                 inc=[(sC, 1)])
            p.on("scalar", lambda e: e.copy(self.Ssb[:, 6 * D + 768:7 * D], psG[6][:, 0:256]),
                 inc=[(sC, 1)])
            p.on("scalar", lambda e: e.copy(self.Ssb[:, 7 * D + 896:8 * D], psG[7][:, 0:128]),
                 inc=[(sC, 1)])
            p.on("vector", lambda e: e.tensor_copy(self.syrow[:], psMean[0:1, :]),
                 wait=[gdone], inc=[(vC, 1)])
            self.p.barrier()

    def phase_gramB(self):
        cx, p, d = self.cx, self.p, self.d
        vC, sC, tC = cx.vC, cx.sC, cx.tC
        W = self.w
        NALL = (B * L) // 512
        with contextlib.ExitStack() as st:
            sb = lambda n, sh, dt=F32: st.enter_context(self.nc.sbuf_tensor(n, sh, dt))
            ps = lambda n, sh, dt=F32: st.enter_context(self.nc.psum_tensor(n, sh, dt))
            ytile = [sb(f"gBy{i}", [128, 4 * D], BF16) for i in range(2)]
            psG = {mi: ps(f"gBg{mi}", [128, min(1024, ((8 - mi) * 128 + 511) // 512 * 512)])
                   for mi in (2, 3, 4, 5)}
            inS = [p.sem("gBin0"), p.sem("gBin1")]
            yt_free = [None, None]
            for i in range(NALL):
                bi = i % 2
                t0 = i * 512
                p.on("sync", lambda e, t0=t0, bi=bi: e.dma_start(
                    out=ytile[bi][:].rearrange("p (g dd) -> p g dd", g=4),
                    in_=d["YTM"][t0:t0 + 512, :].rearrange("(g p) dd -> p g dd", p=128)),
                    wait=([yt_free[bi]] if yt_free[bi] else []), inc=[(inS[bi], 16)])
                ld_t = (inS[bi], inS[bi].count)
                p.on("tensor", None, wait=[ld_t])
                for gi in range(4):
                    yg = ytile[bi][:, gi * D:(gi + 1) * D]
                    first = (i == 0 and gi == 0)
                    last = (i == NALL - 1 and gi == 3)
                    for mi in (2, 3, 4, 5):
                        w0 = mi * 128
                        wdt = 1024 - w0
                        for (off, wp) in ([(0, 512), (512, wdt - 512)] if wdt > 512
                                          else [(0, wdt)]):
                            p.on("tensor", lambda e, yg=yg, mi=mi, w0=w0, off=off, wp=wp,
                                 first=first, last=last: e.matmul(
                                psG[mi][:, off:off + wp],
                                yg[:, mi * 128:(mi + 1) * 128],
                                yg[:, w0 + off:w0 + off + wp],
                                start=first, stop=last),
                                inc=([(tC, 1)] if (mi == 5 and gi == 3) else []))
                yt_free[bi] = W(tC)
            gdone = W(tC)
            p.on("scalar", lambda e: e.copy(self.Ssb[:, 2 * D + 256:3 * D], psG[2][:, 0:768]),
                 wait=[gdone], inc=[(sC, 1)])
            p.on("scalar", lambda e: e.copy(self.Ssb[:, 3 * D + 384:4 * D], psG[3][:, 0:640]),
                 inc=[(sC, 1)])
            p.on("scalar", lambda e: e.copy(self.Ssb[:, 4 * D + 512:5 * D], psG[4][:, 0:512]),
                 inc=[(sC, 1)])
            p.on("scalar", lambda e: e.copy(self.Ssb[:, 5 * D + 640:6 * D], psG[5][:, 0:384]),
                 inc=[(sC, 1)])
            self.p.barrier()

    def phase_stats_fused(self):
        cx, p, d = self.cx, self.p, self.d
        vC, sC, tC = cx.vC, cx.sC, cx.tC
        W = self.w
        NTOT = float(B * L)
        with contextlib.ExitStack() as st:
            sb = lambda n, sh, dt=F32: st.enter_context(self.nc.sbuf_tensor(n, sh, dt))
            ps = lambda n, sh, dt=F32: st.enter_context(self.nc.psum_tensor(n, sh, dt))
            Wq2 = sb("sfWq2", [128, C * D], BF16)
            WT = sb("sfWT", [128, C * D], BF16)
            sycol = sb("sfsycol", [128, 8], BF16)
            gcol = sb("sfgcol", [128, 8], F32)
            bcol = sb("sfbcol", [128, 8], F32)
            t1 = sb("sft1", [128, 8], F32)
            t2 = sb("sft2", [128, 8], F32)
            t3 = sb("sft3", [128, 8], F32)
            ucol = sb("sfu", [128, 1024], F32)
            psT = ps("sfpsT", [128, 1024])
            psR = ps("sfpsR", [128, 128])
            psRb = ps("sfpsRb", [128, 128], BF16)
            psE = ps("sfpsE", [1, 1024])
            ldS = p.sem("sfld")
            for dc in range(1, C):
                for dj in range(dc):
                    p.on("tensor", lambda e, dc=dc, dj=dj: e.transpose(
                        psRb[:], self.Ssb[:, dj * D + dc * 128:dj * D + (dc + 1) * 128],
                        self.idb[:]), wait=[W(sC), W(vC)], inc=[(tC, 1)])
                    p.on("vector", lambda e, dc=dc, dj=dj: e.tensor_copy(
                        self.Ssb[:, dc * D + dj * 128:dc * D + (dj + 1) * 128], psRb[:]),
                        wait=[W(tC)], inc=[(vC, 1)])
            mirr = [W(vC)]
            p.on("vector", lambda e: e.tensor_copy(ucol[0:1, :], self.syrow[:]),
                 inc=[(vC, 1)])
            for cc2 in range(C):
                p.on("tensor", lambda e, cc2=cc2: e.transpose(
                    psR[:, 0:1], ucol[0:1, cc2 * 128:(cc2 + 1) * 128], self.idf[0:1, 0:1]),
                    wait=[W(vC)], inc=[(tC, 1)])
                p.on("vector", lambda e, cc2=cc2: e.tensor_copy(
                    sycol[:, cc2:cc2 + 1], psR[:, 0:1]), wait=[W(tC)], inc=[(vC, 1)])
            for br, (wn, gn, bnm, An, Bn) in enumerate((
                    ("Wq", "gq", "bq", "Aq", "Bq"),
                    ("Wk", "gk", "bk", "Ak", "Bk"),
                    ("Wv", "gv", "bv", "Av", "Bv"))):
                p.on("gpsimd", lambda e, wn=wn: cast_w(e, Wq2, d[wn + "U"]),
                     wait=[W(sC), W(vC)], inc=[(cx.qW, 16)])
                p.on("gpsimd", lambda e, wn=wn: cast_w(e, WT, d[wn + "T"]),
                     inc=[(cx.qW, 16)])
                p.on("sync", lambda e, gn=gn: e.dma_start(out=gcol[:], in_=d[gn][:, :]),
                     wait=[W(vC)], inc=[(ldS, 16)])
                p.on("sync", lambda e, bnm=bnm: e.dma_start(out=bcol[:], in_=d[bnm][:, :]),
                     inc=[(ldS, 16)])
                wl = [(cx.qW, cx.qW.count), (ldS, ldS.count)]
                for ec in range(C):
                    p.on("tensor", None, wait=wl + mirr + [W(vC)])
                    for dc in range(C):
                        for half in range(2):
                            p.on("tensor", lambda e, ec=ec, dc=dc, half=half: e.matmul(
                                psT[:, half * 512:(half + 1) * 512],
                                WT[:, dc * D + ec * 128:dc * D + (ec + 1) * 128],
                                self.Ssb[:, dc * D + half * 512:dc * D + (half + 1) * 512],
                                start=(dc == 0), stop=(dc == C - 1)),
                                inc=([(tC, 1)] if dc == C - 1 and half == 1 else []))
                    p.on("vector", lambda e, ec=ec: e.tensor_mul(
                        ucol[:], psT[:, 0:1024], Wq2[:, ec * D:(ec + 1) * D]),
                        wait=[W(tC), W(vC)], inc=[(vC, 1)])
                    p.on("vector", lambda e, ec=ec: e.reduce_sum(
                        t2[:, ec:ec + 1], ucol[:], axis=AX.X), wait=[W(vC)], inc=[(vC, 1)])
                p.on("tensor", None, wait=[W(vC)])
                for dc in range(C):
                    for half in range(2):
                        p.on("tensor", lambda e, dc=dc, half=half: e.matmul(
                            psE[0:1, half * 512:(half + 1) * 512], sycol[:, dc:dc + 1],
                            WT[:, dc * D + half * 512:dc * D + (half + 1) * 512],
                            start=(dc == 0), stop=(dc == C - 1)),
                            inc=([(tC, 1)] if dc == C - 1 and half == 1 else []))
                p.on("vector", lambda e: e.tensor_copy(ucol[0:1, :], psE[0:1, :]),
                     wait=[W(tC)], inc=[(vC, 1)])
                for cc2 in range(C):
                    p.on("tensor", lambda e, cc2=cc2: e.transpose(
                        psR[:, 0:1], ucol[0:1, cc2 * 128:(cc2 + 1) * 128], self.idf[0:1, 0:1]),
                        wait=[W(vC)], inc=[(tC, 1)])
                    p.on("vector", lambda e, cc2=cc2: e.tensor_copy(
                        t1[:, cc2:cc2 + 1], psR[:, 0:1]), wait=[W(tC)], inc=[(vC, 1)])
                p.on("vector", lambda e: e.tensor_scalar_mul(t1[:], t1[:], 1.0 / NTOT),
                     wait=[W(vC)], inc=[(vC, 1)])
                p.on("vector", lambda e: e.tensor_scalar_mul(t2[:], t2[:], 1.0 / NTOT),
                     wait=[W(vC)], inc=[(vC, 1)])
                p.on("vector", lambda e: e.tensor_mul(t3[:], t1[:], t1[:]),
                     wait=[W(vC)], inc=[(vC, 1)])
                p.on("vector", lambda e: e.tensor_sub(t2[:], t2[:], t3[:]),
                     wait=[W(vC)], inc=[(vC, 1)])
                p.on("scalar", lambda e: e.activation(t2[:], t2[:], AF.Ln,
                                                      bias=self.epsc[:, 0:1]),
                     wait=[W(vC)], inc=[(sC, 1)])
                p.on("scalar", lambda e: e.activation(t2[:], t2[:], AF.Exp, scale=-0.5),
                     wait=[W(sC)], inc=[(sC, 1)])
                p.on("vector", lambda e, An=An: e.tensor_mul(self.AB[An][:], gcol[:], t2[:]),
                     wait=[W(sC)], inc=[(vC, 1)])
                p.on("vector", lambda e, An=An: e.tensor_scalar_mul(
                    self.AB[An][:], self.AB[An][:], 4.0), wait=[W(vC)], inc=[(vC, 1)])
                p.on("vector", lambda e, An=An: e.tensor_mul(t3[:], t1[:], self.AB[An][:]),
                     wait=[W(vC)], inc=[(vC, 1)])
                p.on("vector", lambda e, Bn=Bn: e.tensor_scalar(
                    self.AB[Bn][:], bcol[:], 4.0, -4.0, ALU.mult, ALU.add), inc=[(vC, 1)])
                p.on("vector", lambda e, Bn=Bn: e.tensor_sub(
                    self.AB[Bn][:], self.AB[Bn][:], t3[:]), wait=[W(vC)], inc=[(vC, 1)])
                if self.debug and br == 0:
                    p.on("sync", lambda e: e.dma_start(out=d["DBGS"][:, 0:8], in_=t1[:]),
                         wait=[W(vC), W(sC)], inc=[(cx.qS, 16)])
                    p.on("sync", lambda e: e.dma_start(out=d["DBGS"][:, 8:16], in_=t2[:]),
                         inc=[(cx.qS, 16)])
                    p.on("vector", None, wait=[(cx.qS, cx.qS.count)])
            if self.debug:
                p.on("gpsimd", lambda e: e.dma_start(out=d["DBGSS"][:, :], in_=self.Ssb[:]),
                     wait=[W(vC), W(sC)], inc=[(cx.qW, 16)])
                for i, n in enumerate(("Aq", "Bq", "Ak", "Bk", "Av", "Bv")):
                    p.on("sync", lambda e, i=i, n=n: e.dma_start(
                        out=d["DBGA"][:, i * 8:(i + 1) * 8], in_=self.AB[n][:]),
                        wait=[W(vC), W(sC)], inc=[(cx.qS, 16)])
            p.on("vector", lambda e: e.tensor_mul(self.nmrc[:], self.rc[:, 0:64],
                                                  self.rc[:, 64:128]),
                 wait=[W(vC)], inc=[(vC, 1)])
            p.on("vector", lambda e: e.tensor_scalar_mul(self.nmrc[:], self.nmrc[:], -1.0),
                 wait=[W(vC)], inc=[(vC, 1)])
            self.p.barrier()

    def phase_p23a(self):
        cx, p, d = self.cx, self.p, self.d
        qL, qS, qW, vC, sC, tC = cx.qL, cx.qS, cx.qW, cx.vC, cx.sC, cx.tC
        with contextlib.ExitStack() as st:
            sb = lambda n, sh, dt=F32: st.enter_context(self.nc.sbuf_tensor(n, sh, dt))
            ps = lambda n, sh, dt=F32: st.enter_context(self.nc.psum_tensor(n, sh, dt))
            Wv = sb("aWv", [128, C * D], BF16)
            xin = [sb(f"axin{i}", [128, 4 * D], F32) for i in range(2)]
            xlnf = [sb(f"axlnf{i}", [128, C * 512], BF16) for i in range(2)]
            x0f = [sb(f"ax0f{i}", [128, C * 512], F32) for i in range(2)]
            xlt = sb("axlt", [128, D], BF16)
            vfm = sb("avfm", [128, C * 512], BF16)
            vtm = sb("avtm", [128, 1024], BF16)
            psTB = ps("psTB", [128, 1024], BF16)
            psTF = ps("psTF", [128, 1024], F32)
            psH = Rot([ps(f"apsH{i}", [128, 512], F32) for i in range(2)])
            psV = ps("apsV", [128, 1024], BF16)
            psVV = ps("apsVV", [128, 512], F32)
            vvacc = sb("avvacc", [128, 512], F32)
            spillS = [p.sem("aspill0"), p.sem("aspill1")]
            p.on("gpsimd", lambda e: cast_w(e, Wv, d["WvT"]), inc=[(qW, 16)])
            wld = qW.count
            xin_free = [None, None]
            xlnf_free = [None, None]
            x0f_free = [None, None]
            vv_read = None
            vfm_free = None
            for s in range(self.nseq):
                for t in range(4):
                    ti = s * 4 + t
                    bi = ti % 2
                    t0 = ti * 512
                    p.on("sync", lambda e, t0=t0, bi=bi: e.dma_start(
                        out=xin[bi][:].rearrange("p (g d) -> p g d", g=4),
                        in_=d["x"][t0:t0 + 512, :].rearrange("(g p) d -> p g d", p=128)),
                        wait=([xin_free[bi]] if xin_free[bi] else []), inc=[(qL, 16)])
                    ld_t = qL.count
                    if xlnf_free[bi]:
                        p.on("vector", None, wait=[xlnf_free[bi], x0f_free[bi]])
                    for g in range(4):
                        xg = xin[bi][:, g * D:(g + 1) * D]
                        idx = ti * 4 + g
                        p.on("vector", lambda e, xg=xg, idx=idx: e.tensor_scalar(
                            xlt[:], xg, self.rc[:, 64 + idx:65 + idx],
                            self.nmrc[:, idx:idx + 1], ALU.mult, ALU.add),
                            wait=[(qL, ld_t), self.w(tC)], inc=[(vC, 1)])
                        p.on("tensor", None, wait=[self.w(vC)])
                        for cc in range(C):
                            p.on("tensor", lambda e, cc=cc: e.transpose(
                                psTB[:, cc * 128:(cc + 1) * 128],
                                xlt[:, cc * 128:(cc + 1) * 128], self.idb[:]),
                                inc=([(tC, 1)] if cc == C - 1 else []))
                        for cc in range(C):
                            p.on("tensor", lambda e, xg=xg, cc=cc: e.transpose(
                                psTF[:, cc * 128:(cc + 1) * 128],
                                xg[:, cc * 128:(cc + 1) * 128], self.idf[:]),
                                inc=([(tC, 1)] if cc == C - 1 else []))
                        tcw = tC.count
                        p.on("vector", lambda e, bi=bi, g=g: e.tensor_copy(
                            xlnf[bi][:].rearrange("p (c t) -> p c t", c=C)[:, :, g * 128:(g + 1) * 128],
                            psTB[:].rearrange("p (c t) -> p c t", c=C)),
                            wait=[(tC, tcw)], inc=[(vC, 1)])
                        p.on("vector", lambda e, bi=bi, g=g: e.tensor_copy(
                            x0f[bi][:].rearrange("p (c t) -> p c t", c=C)[:, :, g * 128:(g + 1) * 128],
                            psTF[:].rearrange("p (c t) -> p c t", c=C)),
                            inc=[(vC, 1)])
                    xready = vC.count
                    p.on("sync", lambda e, bi=bi, t0=t0: fm_store(e, d["XLN"], t0, 512, xlnf[bi]),
                         wait=[(vC, xready)], inc=[(spillS[bi], 16)])
                    p.on("sync", lambda e, bi=bi, t0=t0: fm_store(e, d["X0F"], t0, 512, x0f[bi]),
                         inc=[(spillS[bi], 16)])
                    x0f_free[bi] = (spillS[bi], spillS[bi].count)
                    p.on("tensor", None, wait=[(vC, xready), (qW, wld)]
                         + ([vfm_free] if vfm_free else []))
                    for ec in range(C):
                        kb, pb, wv = psH.acquire()
                        if wv:
                            p.on("tensor", None, wait=wv)
                        for dc in range(C):
                            p.on("tensor", lambda e, ec=ec, dc=dc, pb=pb, bi=bi: e.matmul(
                                pb[:], Wv[:, dc * D + ec * 128:dc * D + (ec + 1) * 128],
                                xlnf[bi][:, dc * 512:(dc + 1) * 512],
                                start=(dc == 0), stop=(dc == C - 1)),
                                inc=([(tC, 1)] if dc == C - 1 else []))
                        p.on("scalar", lambda e, ec=ec, pb=pb: e.activation(
                            vfm[:, ec * 512:(ec + 1) * 512], pb[:], AF.Sigmoid,
                            scale=self.AB["Av"][:, ec:ec + 1],
                            bias=self.AB["Bv"][:, ec:ec + 1]),
                            wait=[(tC, tC.count)], inc=[(sC, 1)])
                        psH.rel[kb] = (sC, sC.count)
                    xlnf_free[bi] = (tC, tC.count)
                    xin_free[bi] = (vC, xready)
                    sig_done = sC.count
                    for g2 in range(4):
                        p.on("tensor", None, wait=[(sC, sig_done)])
                        for cc in range(C):
                            p.on("tensor", lambda e, cc=cc, g2=g2: e.transpose(
                                psV[:, cc * 128:(cc + 1) * 128],
                                vfm[:, cc * 512 + g2 * 128:cc * 512 + (g2 + 1) * 128],
                                self.idb[:]), inc=([(tC, 1)] if cc == C - 1 else []))
                        p.on("scalar", lambda e: e.copy(vtm[:], psV[:]),
                             wait=[(tC, tC.count)], inc=[(sC, 1)])
                        p.on("tensor", None, wait=[(sC, sC.count), self.w(vC)])
                        for h in range(HEADS):
                            p.on("tensor", lambda e, h=h: e.matmul(
                                psVV[(h % 2) * 64:(h % 2) * 64 + 64,
                                     (h // 2) * 64:(h // 2) * 64 + 64],
                                vtm[:, h * HD:(h + 1) * HD], vtm[:, h * HD:(h + 1) * HD],
                                start=True, stop=True,
                                tile_position=(0, (h % 2) * 64)),
                                inc=([(tC, 1)] if h == HEADS - 1 else []))
                        first = (t == 0 and g2 == 0)
                        if first:
                            p.on("vector", lambda e: e.tensor_copy(vvacc[:], psVV[:]),
                                 wait=[self.w(tC)], inc=[(vC, 1)])
                        else:
                            p.on("vector", lambda e: e.tensor_add(vvacc[:], vvacc[:], psVV[:]),
                                 wait=[self.w(tC), self.w(vC)], inc=[(vC, 1)])
                    vfm_free = (tC, tC.count)
                p.on("scalar", lambda e, s=s: e.mul(self.vvs[s][:], vvacc[:], 0.125),
                     wait=[self.w(vC)], inc=[(sC, 1)])
                vv_read = (sC, sC.count)
            self.p.barrier()

    def phase_p23b(self):
        cx, p, d = self.cx, self.p, self.d
        qL, qS, qW, vC, sC, tC = cx.qL, cx.qS, cx.qW, cx.vC, cx.sC, cx.tC
        with contextlib.ExitStack() as st:
            sb = lambda n, sh, dt=F32: st.enter_context(self.nc.sbuf_tensor(n, sh, dt))
            ps = lambda n, sh, dt=F32: st.enter_context(self.nc.psum_tensor(n, sh, dt))
            Wq = sb("bWq", [128, C * D], BF16)
            Wk = sb("bWk", [128, C * D], BF16)
            Wp = sb("bWp", [128, C * D], BF16)
            Msb = sb("bM", [128, C * D], BF16)
            xl_in = [sb(f"bxl{i}", [128, C * 512], BF16) for i in range(2)]
            x0_in = [sb(f"bx0{i}", [128, C * 512], F32) for i in range(2)]
            qf = sb("bqf", [128, C * 512], BF16)
            kf = sb("bkf", [128, C * 512], BF16)
            qkt = sb("bqkt", [128, C * 512], BF16)
            x1t = sb("bx1t", [128, C * 512], F32)
            x1b = sb("bx1b", [128, C * 512], BF16)
            x1sq = sb("bx1sq", [128, C * 512], BF16)
            s1row = sb("bs1row", [1, L], F32)
            s2row = sb("bs2row", [1, L], F32)
            psH = Rot([ps(f"bpsH{i}", [128, 512], F32) for i in range(2)])
            psO = Rot([ps(f"bpsO{i}", [128, 512], F32) for i in range(2)])
            psM = ps("bpsM", [128, 1024], F32)
            psRS = ps("bpsRS", [1, 512], F32)
            psRS2 = ps("bpsRS2", [1, 512], F32)
            binS = [p.sem("bin0"), p.sem("bin1")]
            x1S = p.sem("bx1S")
            rowS = p.sem("browS")
            for w, dst in (("WqT", Wq), ("WkT", Wk), ("WpT", Wp)):
                p.on("gpsimd", lambda e, w=w, dst=dst: cast_w(e, dst, d[w]), inc=[(qW, 16)])
            wld = qW.count
            in_free = [None, None]
            x1b_free = None
            row_free = None
            for s in range(self.nseq):
                for j in range(C):
                    p.on("tensor", None, wait=[self.w(sC), (qW, wld)])
                    for half in range(2):
                        for hh in range(2):
                            h = 2 * j + hh
                            base = (h % 2) * 64
                            p.on("tensor", lambda e, s=s, j=j, half=half, hh=hh, base=base: e.matmul(
                                psM[base:base + 64, half * 512:(half + 1) * 512],
                                self.vvs[s][base:base + 64, j * 64:(j + 1) * 64],
                                Wp[base:base + 64,
                                   j * D + half * 512:j * D + (half + 1) * 512],
                                start=True, stop=True,
                                tile_position=(base, base)),
                                inc=([(tC, 1)] if half == 1 and hh == 1 else []))
                    p.on("scalar", lambda e, j=j: e.copy(Msb[:, j * D:(j + 1) * D], psM[:]),
                         wait=[self.w(tC)], inc=[(sC, 1)])
                mdone = self.w(sC)
                if self.debug and s == 0:
                    p.on("sync", lambda e: e.dma_start(out=d["DBGM"][:, :], in_=Msb[:]),
                         wait=[mdone], inc=[(qS, 16)])
                for t in range(4):
                    ti = s * 4 + t
                    bi = ti % 2
                    t0 = ti * 512
                    p.on("sync", lambda e, bi=bi, t0=t0: fm_load(e, d["XLN"], t0, 512, xl_in[bi]),
                         wait=([in_free[bi]] if in_free[bi] else []), inc=[(binS[bi], 16)])
                    p.on("sync", lambda e, bi=bi, t0=t0: fm_load(e, d["X0F"], t0, 512, x0_in[bi]),
                         inc=[(binS[bi], 16)])
                    ld_t = (binS[bi], binS[bi].count)
                    if getattr(self, "_dbgqk_free", None):
                        p.on("scalar", None, wait=[self._dbgqk_free])
                        self._dbgqk_free = None
                    p.on("tensor", None, wait=[ld_t, (qW, wld), self.w(vC)])
                    for which, Wt, dst, Acol, Bcol in (("q", Wq, qf, "Aq", "Bq"),
                                                       ("k", Wk, kf, "Ak", "Bk")):
                        for ec in range(C):
                            kb, pb, wv = psH.acquire()
                            if wv:
                                p.on("tensor", None, wait=wv)
                            for dc in range(C):
                                p.on("tensor", lambda e, Wt=Wt, ec=ec, dc=dc, pb=pb, bi=bi: e.matmul(
                                    pb[:], Wt[:, dc * D + ec * 128:dc * D + (ec + 1) * 128],
                                    xl_in[bi][:, dc * 512:(dc + 1) * 512],
                                    start=(dc == 0), stop=(dc == C - 1)),
                                    inc=([(tC, 1)] if dc == C - 1 else []))
                            p.on("scalar", lambda e, dst=dst, ec=ec, pb=pb, Acol=Acol, Bcol=Bcol:
                                 e.activation(dst[:, ec * 512:(ec + 1) * 512], pb[:], AF.Sigmoid,
                                              scale=self.AB[Acol][:, ec:ec + 1],
                                              bias=self.AB[Bcol][:, ec:ec + 1]),
                                 wait=[(tC, tC.count)], inc=[(sC, 1)])
                            psH.rel[kb] = (sC, sC.count)
                    if self.debug and s == 0 and t == 0:
                        p.on("sync", lambda e: e.dma_start(out=d["DBGQ"][:, :], in_=qf[:]),
                             wait=[self.w(sC)], inc=[(qS, 16)])
                        p.on("sync", lambda e: e.dma_start(out=d["DBGK"][:, :], in_=kf[:]),
                             inc=[(qS, 16)])
                        self._dbgqk_free = (qS, qS.count)
                    p.on("vector", lambda e: e.tensor_mul(qkt[:], qf[:], kf[:]),
                         wait=[self.w(sC)], inc=[(vC, 1)])
                    qk_ready = vC.count
                    if getattr(self, "_x1t_free", None):
                        p.on("vector", None, wait=self._x1t_free)
                    p.on("tensor", None, wait=[(vC, qk_ready), mdone])
                    for ec in range(C):
                        kb, pb, wv = psO.acquire()
                        if wv:
                            p.on("tensor", None, wait=wv)
                        for dc in range(C):
                            p.on("tensor", lambda e, ec=ec, dc=dc, pb=pb: e.matmul(
                                pb[:], Msb[:, dc * D + ec * 128:dc * D + (ec + 1) * 128],
                                qkt[:, dc * 512:(dc + 1) * 512],
                                start=(dc == 0), stop=(dc == C - 1)),
                                inc=([(tC, 1)] if dc == C - 1 else []))
                        p.on("vector", lambda e, ec=ec, pb=pb, bi=bi: e.scalar_tensor_tensor(
                            x1t[:, ec * 512:(ec + 1) * 512], pb[:],
                            self.bpc[:, ec:ec + 1],
                            x0_in[bi][:, ec * 512:(ec + 1) * 512], ALU.add, ALU.add),
                            wait=[(tC, tC.count)], inc=[(vC, 1)])
                        psO.rel[kb] = (vC, vC.count)
                    x1_ready = vC.count
                    in_free[bi] = (vC, x1_ready)
                    p.on("sync", lambda e, t0=t0: fm_store(e, d["X1"], t0, 512, x1t),
                         wait=[(vC, x1_ready)], inc=[(x1S, 16)])
                    p.on("vector", lambda e: e.tensor_copy(x1b[:], x1t[:]),
                         wait=[self.w(vC)] + ([x1b_free] if x1b_free else []), inc=[(vC, 1)])
                    p.on("scalar", lambda e: e.activation(x1sq[:], x1b[:], AF.Square),
                         wait=[self.w(vC)], inc=[(sC, 1)])
                    p.on("tensor", None, wait=[self.w(sC)] + ([row_free] if row_free else []))
                    for dc in range(C):
                        p.on("tensor", lambda e, dc=dc: e.matmul(
                            psRS[:], self.ones_c[:], x1b[:, dc * 512:(dc + 1) * 512],
                            start=(dc == 0), stop=(dc == C - 1)),
                            inc=([(tC, 1)] if dc == C - 1 else []))
                    for dc in range(C):
                        p.on("tensor", lambda e, dc=dc: e.matmul(
                            psRS2[:], self.ones_c[:], x1sq[:, dc * 512:(dc + 1) * 512],
                            start=(dc == 0), stop=(dc == C - 1)),
                            inc=([(tC, 1)] if dc == C - 1 else []))
                    rdone = tC.count
                    p.on("vector", lambda e, t=t: e.tensor_copy(
                        s1row[:, t * 512:(t + 1) * 512], psRS[:]), wait=[(tC, rdone)])
                    p.on("vector", lambda e, t=t: e.tensor_copy(
                        s2row[:, t * 512:(t + 1) * 512], psRS2[:]), inc=[(vC, 1)])
                    row_free = (vC, vC.count)
                    x1b_free = (vC, vC.count)
                    self._x1t_free = [(x1S, x1S.count), self.w(vC)]
                p.on("sync", lambda e, s=s: e.dma_start(
                    out=d["S1"][s * L:(s + 1) * L].rearrange("(a b) -> a b", a=1),
                    in_=s1row[:, :]), wait=[self.w(vC)], inc=[(rowS, 16)])
                p.on("sync", lambda e, s=s: e.dma_start(
                    out=d["S1"][TOK + s * L:TOK + (s + 1) * L].rearrange("(a b) -> a b", a=1),
                    in_=s2row[:, :]), inc=[(rowS, 16)])
                row_free = (rowS, rowS.count)
            self.p.barrier()

    def phase_fa(self, l):
        cx, p, d = self.cx, self.p, self.d
        qW, vC, sC, tC = cx.qW, cx.vC, cx.sC, cx.tC
        YIN = d["X1"] if l == 0 else d["Y2"]
        YOUT = d["Y1"] if l == 0 else d["Y3"]
        HOUT = d["HLN0"] if l == 0 else d["HLN1"]
        W = self.w
        with contextlib.ExitStack() as st:
            sb = lambda n, sh, dt=F32: st.enter_context(self.nc.sbuf_tensor(n, sh, dt))
            ps = lambda n, sh, dt=F32: st.enter_context(self.nc.psum_tensor(n, sh, dt))
            pre = f"f{l}"
            Wqkv = sb(pre + "Wqkv", [128, C * 1536], BF16)
            Wo = sb(pre + "Wo", [128, 4 * D], BF16)
            ones_rf = sb(pre + "onesrf", [1, 128], F32)
            tabs = {n: sb(pre + n, [128, 512], F32) for n in ("CSQ", "SSQ", "CSK", "SSK")}
            qscol = sb(pre + "qscol", [128, 1], F32)
            kscol = sb(pre + "kscol", [128, 1], F32)
            lgc = sb(pre + "lgc", [128, 8], F32)
            lbc = sb(pre + "lbc", [128, 8], F32)
            yin = [sb(pre + f"yin{i}", [128, C * 512], F32) for i in range(2)] if l == 0 else None
            ysb = [sb(pre + f"ysb{i}", [128, C * 513], F32) for i in range(2)]
            ybb = [sb(pre + f"ybb{i}", [128, C * 513], BF16) for i in range(2)]
            ybnd = sb(pre + "ybnd", [128, 4], F32)
            qraw = sb(pre + "qraw", [128, 4 * 512], F32)
            kraw = sb(pre + "kraw", [128, 4 * 512], F32)
            sqt = sb(pre + "sqt", [128, 512], BF16)
            lr = sb(pre + "lr", [2, 512], F32)
            rhi = sb(pre + "rhi", [2, 512], BF16)
            rhi32 = sb(pre + "rhi32", [2, 512], F32)
            rlo = sb(pre + "rlo", [2, 512], BF16)
            qn = sb(pre + "qn", [128, 512], F32)
            s1t = sb(pre + "s1t", [128, 512], F32)
            swp = sb(pre + "swp", [128, 512], F32)
            rt1 = sb(pre + "rt1", [128, 512], F32)
            qpair = sb(pre + "qpair", [128, 4 * 512], BF16)
            kring = sb(pre + "kring", [128, 4 * 1024], BF16)
            vring = [sb(pre + f"vr{i}", [128, 512], BF16) for i in range(9)]
            nmc = sb(pre + "nmc", [128, 8], F32)
            nmb = sb(pre + "nmb", [128, 8], BF16)
            nmf = sb(pre + "nmf", [128, 8], F32)
            nmrow0 = sb(pre + "nmrow0", [1, 512], BF16)
            onesm = sb(pre + "onesm", [128, 64], BF16)
            nmrow1 = sb(pre + "nmrow1", [1, 512], BF16)
            rdrow = sb(pre + "rdrow", [1, 512], F32)
            rdhi = sb(pre + "rdhi", [1, 512], BF16)
            rdhi32 = sb(pre + "rdhi32", [1, 512], F32)
            rdlo = sb(pre + "rdlo", [1, 512], BF16)
            rdsb = sb(pre + "rdsb", [128, 512], F32)
            attnT = [[sb(pre + f"attnT{i}{k}", [128, 512], BF16) for k in range(2)]
                     for i in range(2)]
            osb = sb(pre + "osb", [128, 4 * 512], BF16)
            hsq = sb(pre + "hsq", [128, 512], BF16)
            rowa = sb(pre + "rowa", [1, 512], F32)
            rowb2 = sb(pre + "rowb2", [1, 512], F32)
            m3row = rowa
            v3row = rowb2
            psH = Rot([ps(pre + f"psH{i}", [128, 512], F32) for i in range(2)])
            psSA = [ps(pre + "psSAa", [128, 1024], F32), ps(pre + "psSAb", [128, 1024], F32)]
            psSB = [ps(pre + "psSBa", [128, 512], F32), ps(pre + "psSBb", [128, 512], F32)]
            psRR = psSA[1][0:1, 512:1024]
            psL2 = psSA[0][0:2, 512:1024]

            inS = [p.sem(pre + "in0"), p.sem(pre + "in1")]
            outSp = [p.sem(pre + "out0"), p.sem(pre + "out1")]
            hSp = [p.sem(pre + "hS0"), p.sem(pre + "hS1")]
            faC = p.sem(pre + "faC")
            rowLS = p.sem(pre + "rowLS")
            self._tab_free = None

            p.on("gpsimd", lambda e: cast_w(e, Wqkv, d[f"wqkv{l}"]), inc=[(qW, 16)])
            p.on("gpsimd", lambda e: cast_w(e, Wo, d[f"wo{l}"]), inc=[(qW, 16)])
            for dst, src in ((ones_rf, "onesrow"), (qscol, f"qs{l}"), (kscol, f"ks{l}"),
                             (lgc, f"lng{l}"), (lbc, f"lnb{l}")):
                p.on("sync", lambda e, dst=dst, src=src: e.dma_start(
                    out=dst[:], in_=d[src][:, :]), inc=[(faC, 16)])
            tabS = p.sem(pre + "tabS")
            wld = (qW, qW.count)
            cl = (faC, faC.count)

            in_free = [None, None]
            p.on("vector", lambda e: e.memset(onesm[:], 1.0), inc=[(cx.vC, 1)])
            for bb2 in range(2):
                for cc in range(4):
                    p.on("vector", lambda e, bb2=bb2, cc=cc: e.memset(
                        ysb[bb2][:, cc * 513:cc * 513 + 1], 0.0), inc=[(vC, 1)])
            for s in range(self.nseq):
                for t in range(4):
                    ti = s * 4 + t
                    bi = ti % 2
                    cur, prv = ti % 2, 1 - ti % 2
                    t0 = ti * 512
                    pos0 = t * 512

                    if l == 0:
                        p.on("sync", lambda e, bi=bi, t0=t0: fm_load(e, YIN, t0, 512, yin[bi]),
                             wait=([in_free[bi]] if in_free[bi] else []), inc=[(inS[bi], 16)])
                        ldw = (inS[bi], inS[bi].count)
                        p.on("sync", lambda e, t0=t0: e.dma_start(
                            out=rowa[:], in_=d["S1"][t0:t0 + 512].rearrange("(a b) -> a b", a=1)),
                            wait=[W(tC), W(vC)], inc=[(rowLS, 16)])
                        p.on("sync", lambda e, t0=t0: e.dma_start(
                            out=rowb2[:], in_=d["S1"][TOK + t0:TOK + t0 + 512].rearrange("(a b) -> a b", a=1)),
                            inc=[(rowLS, 16)])
                        lw = (rowLS, rowLS.count)
                        p.on("vector", lambda e: e.tensor_scalar_mul(rowa[:], rowa[:], 1.0 / D),
                             wait=[lw, W(sC), W(vC)], inc=[(vC, 1)])
                        p.on("vector", lambda e: e.tensor_scalar_mul(rowb2[:], rowb2[:], 1.0 / D),
                             wait=[W(vC)], inc=[(vC, 1)])
                        p.on("vector", lambda e: e.tensor_mul(swp[0:1, :], rowa[:], rowa[:]),
                             wait=[W(vC)], inc=[(vC, 1)])
                        p.on("vector", lambda e: e.tensor_sub(rowb2[:], rowb2[:], swp[0:1, :]),
                             wait=[W(vC)], inc=[(vC, 1)])
                        p.on("scalar", lambda e: e.activation(rowb2[:], rowb2[:], AF.Ln,
                                                              bias=self.epsc[0:1, 0:1]),
                             wait=[W(vC)], inc=[(sC, 1)])
                        p.on("scalar", lambda e: e.activation(rowb2[:], rowb2[:], AF.Exp,
                                                              scale=-0.5),
                             wait=[W(sC)], inc=[(sC, 1)])
                        p.on("vector", lambda e: e.tensor_mul(rowa[:], rowa[:], rowb2[:]),
                             wait=[W(sC)], inc=[(vC, 1)])
                        p.on("vector", lambda e: e.tensor_scalar_mul(rowa[:], rowa[:], -1.0),
                             wait=[W(vC)], inc=[(vC, 1)])
                        rows_rdy = W(vC)
                        kb0, pb0, wv0 = psH.acquire()
                        kb1, pb1, wv1 = psH.acquire()
                        p.on("tensor", None, wait=wv0 + wv1 + [rows_rdy, cl])
                        p.on("tensor", lambda e, pb0=pb0: e.matmul(
                            pb0[:], ones_rf[:, :], rowb2[0:1, :],
                            start=True, stop=True))
                        p.on("tensor", lambda e, pb1=pb1: e.matmul(
                            pb1[:], ones_rf[:, :], rowa[0:1, :],
                            start=True, stop=True), inc=[(tC, 1)])
                        bdone = W(tC)
                        if self._ysb_free[cur]:
                            p.on("vector", None, wait=self._ysb_free[cur])
                        for cc in range(C):
                            p.on("vector", lambda e, cc=cc, bi=bi, pb0=pb0: e.tensor_mul(
                                s1t[:], yin[bi][:, cc * 512:(cc + 1) * 512], pb0[:]),
                                wait=[bdone, ldw, W(vC)], inc=[(vC, 1)])
                            p.on("vector", lambda e, cc=cc, cur=cur, pb1=pb1: e.tensor_add(
                                ysb[cur][:, cc * 513 + 1:cc * 513 + 513], s1t[:], pb1[:]),
                                wait=[W(vC)], inc=[(vC, 1)])
                        psH.rel[kb0] = (vC, vC.count)
                        psH.rel[kb1] = (vC, vC.count)
                        in_free[bi] = (vC, vC.count)
                        yload = W(vC)
                    else:
                        if self._ysb_free[cur]:
                            p.on("vector", None, wait=self._ysb_free[cur])
                            p.on("sync", lambda e, cur=cur, t0=t0: e.dma_start(
                                out=ysb[cur][:].rearrange("p (c t) -> p c t", c=C)[:, :, 1:513],
                                in_=YIN[:, t0:t0 + 512].rearrange("(c p) t -> p c t", p=128)),
                                wait=self._ysb_free[cur], inc=[(inS[bi], 16)])
                        else:
                            p.on("sync", lambda e, cur=cur, t0=t0: e.dma_start(
                                out=ysb[cur][:].rearrange("p (c t) -> p c t", c=C)[:, :, 1:513],
                                in_=YIN[:, t0:t0 + 512].rearrange("(c p) t -> p c t", p=128)),
                                inc=[(inS[bi], 16)])
                        yload = (inS[bi], inS[bi].count)
                    if t == 0:
                        for cc in range(4, C):
                            p.on("vector", lambda e, cc=cc, cur=cur: e.memset(
                                ysb[cur][:, cc * 513:cc * 513 + 1], 0.0),
                                wait=[yload] + ([W(vC)] if cc == 4 else []), inc=[(vC, 1)])
                    else:
                        for cc in range(4, C):
                            p.on("vector", lambda e, cc=cc, cur=cur: e.tensor_copy(
                                ysb[cur][:, cc * 513:cc * 513 + 1], ybnd[:, cc - 4:cc - 3]),
                                wait=[yload] + ([W(vC)] if cc == 4 else []), inc=[(vC, 1)])
                    yrdy = W(vC)
                    for cc in range(C):
                        p.on("vector", lambda e, cc=cc, cur=cur: e.tensor_copy(
                            ybb[cur][:, cc * 513:(cc + 1) * 513],
                            ysb[cur][:, cc * 513:(cc + 1) * 513]),
                            wait=[yrdy] + ([W(tC)] if cc == 0 else []), inc=[(vC, 1)])
                    ybrdy = (vC, vC.count)
                    for n in ("CSQ", "SSQ", "CSK", "SSK"):
                        p.on("sync", lambda e, n=n, pos0=pos0: e.dma_start(
                            out=tabs[n][:], in_=d[n][:, pos0:pos0 + 512]),
                            wait=([self._tab_free] if self._tab_free else []),
                            inc=[(tabS, 16)])
                    tab_rdy = (tabS, tabS.count)

                    def ysl(cc, off, width, cur):
                        shf = 1 if cc >= 4 else 1
                        base = cc * 513 + (0 if cc >= 4 else 1)
                        return ybb[cur][:, base + off:base + off + width]

                    p.on("tensor", None, wait=[ybrdy, wld])
                    for which, dst in (("q", qraw), ("k", kraw)):
                        for ecl in range(4):
                            ec = ecl if which == "q" else 4 + ecl
                            kb, pb, wv = psH.acquire()
                            if wv:
                                p.on("tensor", None, wait=wv)
                            for dc in range(C):
                                p.on("tensor", lambda e, ec=ec, dc=dc, pb=pb, cur=cur: e.matmul(
                                    pb[:], Wqkv[:, dc * 1536 + ec * 128:dc * 1536 + (ec + 1) * 128],
                                    ysl(dc, 0, 512, cur),
                                    start=(dc == 0), stop=(dc == C - 1)),
                                    inc=([(tC, 1)] if dc == C - 1 else []))
                            p.on("scalar", lambda e, dst=dst, ecl=ecl, pb=pb: e.copy(
                                dst[:, ecl * 512:(ecl + 1) * 512], pb[:]),
                                wait=[W(tC), W(vC), W(hSp[0]), W(hSp[1])], inc=[(sC, 1)])
                            psH.rel[kb] = (sC, sC.count)
                    for g2 in range(4):
                        slot = (ti * 4 + g2) % 9
                        kb, pb, wv = psH.acquire()
                        if wv:
                            p.on("tensor", None, wait=wv)
                        for dc in range(C):
                            p.on("tensor", lambda e, dc=dc, g2=g2, pb=pb, cur=cur: e.matmul(
                                pb[:], ysl(dc, g2 * 128, 128, cur),
                                Wqkv[:, dc * 1536 + 1024:dc * 1536 + 1536],
                                start=(dc == 0), stop=(dc == C - 1)),
                                inc=([(tC, 1)] if dc == C - 1 else []))
                        p.on("scalar", lambda e, slot=slot, pb=pb: e.copy(
                            vring[slot][:], pb[:]), wait=[W(tC)], inc=[(sC, 1)])
                        psH.rel[kb] = (sC, sC.count)
                    if t > 0:
                        for cc in range(4):
                            p.on("vector", lambda e, cc=cc: e.tensor_copy(
                                kring[:, cc * 1024:cc * 1024 + 512],
                                kring[:, cc * 1024 + 512:(cc + 1) * 1024]),
                                wait=([W(tC), W(vC)] if cc == 0 else [W(vC)]), inc=[(vC, 1)])
                    else:
                        p.on("vector", None, wait=[W(tC)])
                    for which in ("q", "k"):
                        src = qraw if which == "q" else kraw
                        scol = qscol if which == "q" else kscol
                        CS = tabs["CSQ" if which == "q" else "CSK"]
                        SS = tabs["SSQ" if which == "q" else "SSK"]
                        for cc in range(4):
                            p.on("scalar", lambda e, src=src, cc=cc: e.activation(
                                sqt[:], src[:, cc * 512:(cc + 1) * 512], AF.Square),
                                wait=[W(sC), W(vC)], inc=[(sC, 1)])
                            p.on("tensor", lambda e: e.matmul(
                                psL2, self.sel_b[:], sqt[:], start=True, stop=True),
                                wait=[W(sC), W(vC)], inc=[(tC, 1)])
                            p.on("scalar", lambda e: e.activation(lr[:], psL2, AF.Ln),
                                 wait=[W(tC)], inc=[(sC, 1)])
                            p.on("scalar", lambda e: e.activation(lr[:], lr[:], AF.Exp,
                                                                  scale=-0.5),
                                 wait=[W(sC)], inc=[(sC, 1)])
                            p.on("vector", lambda e: e.tensor_scalar_min(lr[:], lr[:], 1.0e12),
                                 wait=[W(sC)], inc=[(vC, 1)])
                            p.on("vector", lambda e: e.tensor_copy(rhi[:], lr[:]),
                                 wait=[W(vC)], inc=[(vC, 1)])
                            p.on("vector", lambda e: e.tensor_copy(rhi32[:], rhi[:]),
                                 wait=[W(vC)], inc=[(vC, 1)])
                            p.on("vector", lambda e: e.tensor_sub(lr[:], lr[:], rhi32[:]),
                                 wait=[W(vC)], inc=[(vC, 1)])
                            p.on("vector", lambda e: e.tensor_copy(rlo[:], lr[:]),
                                 wait=[W(vC)], inc=[(vC, 1)])
                            kb, pb, wv = psH.acquire()
                            if wv:
                                p.on("tensor", None, wait=wv)
                            p.on("tensor", lambda e, pb=pb: e.matmul(
                                pb[:], self.sel2_b[:], rhi[:], start=True, stop=False),
                                wait=[W(vC)])
                            p.on("tensor", lambda e, pb=pb: e.matmul(
                                pb[:], self.sel2_b[:], rlo[:], start=False, stop=True),
                                inc=[(tC, 1)])
                            p.on("vector", lambda e, src=src, cc=cc, pb=pb: e.tensor_mul(
                                qn[:], src[:, cc * 512:(cc + 1) * 512], pb[:]),
                                wait=[W(tC), W(vC)], inc=[(vC, 1)])
                            psH.rel[kb] = (vC, vC.count)
                            p.on("vector", lambda e, scol=scol: e.tensor_scalar_mul(
                                s1t[:], qn[:], scol[:, 0:1]), wait=[W(vC), cl], inc=[(vC, 1)])
                            for bb in range(4):
                                srcr = (bb // 2) * 64 + (1 - bb % 2) * 32
                                dstr = (bb // 2) * 64 + (bb % 2) * 32
                                p.on("vector", lambda e, srcr=srcr, dstr=dstr: e.tensor_copy(
                                    swp[dstr:dstr + 32, :], s1t[srcr:srcr + 32, :]),
                                    wait=[W(vC)], inc=[(vC, 1)])
                            p.on("vector", lambda e, CS=CS: e.tensor_mul(
                                rt1[:], s1t[:], CS[:, 0:512]),
                                wait=[W(vC), tab_rdy], inc=[(vC, 1)])
                            p.on("vector", lambda e, SS=SS: e.tensor_mul(
                                swp[:], swp[:], SS[:, 0:512]),
                                wait=[W(vC)], inc=[(vC, 1)])
                            if which == "q":
                                p.on("vector", lambda e, cc=cc: e.tensor_add(
                                    qpair[:, cc * 512:(cc + 1) * 512], rt1[:], swp[:]),
                                    wait=[W(vC)], inc=[(vC, 1)])
                            else:
                                p.on("vector", lambda e, cc=cc: e.tensor_add(
                                    kring[:, cc * 1024 + 512:(cc + 1) * 1024], rt1[:], swp[:]),
                                    wait=[W(vC)], inc=[(vC, 1)])
                    qk_rdy = (vC, vC.count)
                    self._tab_free = qk_rdy
                    for j in range(4):
                        SAa, SAb = psSA
                        for g in range(4):
                            if t == 0:
                                k0, wdt = 512, 128 * (g + 1)
                            else:
                                k0, wdt = 128 * g, 640
                            gw = [qk_rdy]
                            if g == 0:
                                gw += ([self._avfree] if self._avfree else [])
                            else:
                                gw += [(vC, vC.count)]
                            p.on("tensor", None, wait=gw)
                            for hh, (SA, tp) in enumerate(((SAa, (0, 0)), (SAb, (64, 0)))):
                                rb = hh * 64

                                def simmm(e, off, wpart, st, sp, SA=None, rb=None, j=None,
                                          g=None, k0=None, tp=None):
                                    return e.matmul(
                                        SA[:, off:off + wpart],
                                        qpair[rb:rb + 64, j * 512 + g * 128:j * 512 + (g + 1) * 128],
                                        kring[rb:rb + 64, j * 1024 + k0 + off:j * 1024 + k0 + off + wpart],
                                        start=st, stop=sp, tile_position=tp)
                                kw = dict(SA=SA, rb=rb, j=j, g=g, k0=k0, tp=tp)
                                lastinc = [(tC, 1)] if hh == 1 else []
                                if t == 0:
                                    if g > 0:
                                        p.on("tensor", lambda e, kw=kw, g=g: simmm(
                                            e, 0, g * 128, True, True, **kw))
                                    p.on("tensor", lambda e, kw=kw, g=g: simmm(
                                        e, g * 128, 128, True, False, **kw))
                                    p.on("tensor", lambda e, SA=SA, g=g: e.matmul(
                                        SA[:, g * 128:(g + 1) * 128], self.idb[:],
                                        self.mA["cur"][:, :], start=False, stop=True),
                                        inc=lastinc)
                                else:
                                    p.on("tensor", lambda e, kw=kw: simmm(
                                        e, 0, 128, True, False, **kw))
                                    p.on("tensor", lambda e, SA=SA: e.matmul(
                                        SA[:, 0:128], self.idb[:], self.mA["prev"][:, :],
                                        start=False, stop=True))
                                    p.on("tensor", lambda e, kw=kw: simmm(
                                        e, 128, 384, True, True, **kw))
                                    p.on("tensor", lambda e, kw=kw: simmm(
                                        e, 512, 128, True, False, **kw))
                                    p.on("tensor", lambda e, SA=SA: e.matmul(
                                        SA[:, 512:640], self.idb[:], self.mA["cur"][:, :],
                                        start=False, stop=True), inc=lastinc)
                            simw = W(tC)
                            for hh, SA in enumerate(psSA):
                                col = g if hh == 0 else 4 + g
                                mw = [simw] if hh == 0 else []
                                if g == 0 and hh == 0:
                                    mw += [self._nmfree] if self._nmfree else []
                                p.on("vector", lambda e, SA=SA, wdt=wdt, col=col: e.reduce_max(
                                    nmc[:, col:col + 1], SA[:, 0:wdt], axis=AX.X, negate=True),
                                    wait=mw, inc=[(vC, 1)])
                                p.on("vector", lambda e, col=col: e.tensor_copy(
                                    nmb[:, col:col + 1], nmc[:, col:col + 1]),
                                    wait=[W(vC)], inc=[(vC, 1)])
                                p.on("vector", lambda e, col=col: e.tensor_copy(
                                    nmf[:, col:col + 1], nmb[:, col:col + 1]),
                                    wait=[W(vC)], inc=[(vC, 1)])
                        maxdone = (vC, vC.count)
                        for hh in range(2):
                            cb = 0 if hh == 0 else 4
                            kb, pb, wv = psH.acquire()
                            p.on("tensor", None, wait=wv + [maxdone])
                            for g in range(4):
                                p.on("tensor", lambda e, cb=cb, g=g, pb=pb: e.transpose(
                                    pb[0:1, g * 128:(g + 1) * 128],
                                    nmf[:, cb + g:cb + g + 1], self.idf[:]),
                                    inc=([(tC, 1)] if g == 3 else []))
                            nr = nmrow0 if hh == 0 else nmrow1
                            p.on("vector", lambda e, nr=nr, pb=pb: e.tensor_copy(
                                nr[:], pb[0:1, 0:512]), wait=[W(tC)], inc=[(vC, 1)])
                            psH.rel[kb] = (vC, vC.count)
                        self._nmfree = (tC, tC.count)
                        rowsdone = (vC, vC.count)
                        kcs = [4, 0, 1, 2, 3, 5, 6, 7] if t > 0 else [4, 5, 6, 7]
                        for ki, kc in enumerate(kcs):
                            if kc < 4:
                                q0, qw = 0, 128 * (kc + 1)
                            else:
                                q0, qw = 128 * (kc - 4), 512 - 128 * (kc - 4)
                            if kc >= 4:
                                kslot = 4 * t + (kc - 4)
                            else:
                                kslot = 4 * (t - 1) + kc
                            vs = vring[(s * 16 + kslot) % 9]
                            first_av = ki == 0
                            is_last = ki == len(kcs) - 1
                            par = ki % 2
                            simc = [None, None]
                            for hh in range(2):
                                rb = hh * 64
                                base = 0 if par == 0 else 512
                                SBps = psSB[hh] if par == 0 else psSA[hh]
                                nr = nmrow0 if hh == 0 else nmrow1
                                bw = []
                                if self._expw2[hh][par]:
                                    bw += [self._expw2[hh][par]]
                                if ki == 0:
                                    bw += [rowsdone]
                                p.on("tensor", None, wait=bw)
                                p.on("tensor", lambda e, j=j, rb=rb, kc=kc, q0=q0, qw=qw,
                                     SBps=SBps, base=base: e.matmul(
                                    SBps[:, base + q0:base + q0 + qw],
                                    kring[rb:rb + 64, j * 1024 + kc * 128:j * 1024 + (kc + 1) * 128],
                                    qpair[rb:rb + 64, j * 512 + q0:j * 512 + q0 + qw],
                                    start=True, stop=False, tile_position=(rb, 0)))
                                if kc >= 4:
                                    p.on("tensor", lambda e, SBps=SBps, q0=q0, base=base: e.matmul(
                                        SBps[:, base + q0:base + q0 + 128], self.idb[:],
                                        self.mB["cur"][:, :], start=False, stop=False))
                                else:
                                    p.on("tensor", lambda e, SBps=SBps, kc=kc, base=base: e.matmul(
                                        SBps[:, base + kc * 128:base + (kc + 1) * 128], self.idb[:],
                                        self.mB["prev"][:, :], start=False, stop=False))
                                p.on("tensor", lambda e, SBps=SBps, q0=q0, qw=qw, nr=nr,
                                     base=base: e.matmul(
                                    SBps[:, base + q0:base + q0 + qw], self.ones_r[:, :],
                                    nr[0:1, q0:q0 + qw], start=False, stop=True),
                                    inc=[(tC, 1)])
                                simc[hh] = (tC, tC.count)
                            for hh in range(2):
                                att = attnT[hh][par]
                                base = 0 if par == 0 else 512
                                SBps = psSB[hh] if par == 0 else psSA[hh]
                                aw = [simc[hh]]
                                if self._attfree[hh][par]:
                                    aw += [self._attfree[hh][par]]
                                p.on("scalar", lambda e, SBps=SBps, att=att, q0=q0, qw=qw,
                                     base=base: e.activation(
                                    att[:, q0:q0 + qw], SBps[:, base + q0:base + q0 + qw],
                                    AF.Exp),
                                     wait=aw, inc=[(sC, 1)])
                                self._expw2[hh][par] = (sC, sC.count)
                            for hh in range(2):
                                att = attnT[hh][par]
                                h = 2 * j + hh
                                AVps = psSA[hh]
                                p.on("tensor", None, wait=[self._expw2[hh][par]])
                                p.on("tensor", lambda e, AVps=AVps, att=att, vs=vs, h=h,
                                     q0=q0, qw=qw, fa=first_av, sp=is_last: e.matmul(
                                    AVps[0:64, q0:q0 + qw], vs[:, h * 64:(h + 1) * 64],
                                    att[:, q0:q0 + qw], start=fa, stop=sp,
                                    tile_position=(0, 0)))
                                p.on("tensor", lambda e, AVps=AVps, att=att,
                                     q0=q0, qw=qw, fa=first_av, sp=is_last: e.matmul(
                                    AVps[64:128, q0:q0 + qw], onesm[:, :],
                                    att[:, q0:q0 + qw], start=fa, stop=sp,
                                    tile_position=(0, 64), skip_group_check=True),
                                    inc=[(tC, 1)])
                                self._attfree[hh][par] = (tC, tC.count)
                        avw = [self._attfree[0][(len(kcs) - 1) % 2],
                               self._attfree[1][(len(kcs) - 1) % 2]]
                        for hh in range(2):
                            rb = hh * 64
                            AVps = psSA[hh]
                            p.on("vector", lambda e, AVps=AVps: e.reciprocal(
                                rdrow[:], AVps[64:65, 0:512]),
                                wait=[avw[hh], W(tC), W(sC), W(vC)], inc=[(vC, 1)])
                            p.on("vector", lambda e: e.tensor_copy(rdhi[:], rdrow[:]),
                                 wait=[W(vC)] + ([W(tC)] if hh == 1 else []), inc=[(vC, 1)])
                            p.on("vector", lambda e: e.tensor_copy(rdhi32[:], rdhi[:]),
                                 wait=[W(vC)], inc=[(vC, 1)])
                            p.on("vector", lambda e: e.tensor_sub(rdrow[:], rdrow[:], rdhi32[:]),
                                 wait=[W(vC)], inc=[(vC, 1)])
                            p.on("vector", lambda e: e.tensor_copy(rdlo[:], rdrow[:]),
                                 wait=[W(vC)], inc=[(vC, 1)])
                            rowsw = W(vC)
                            kb, pb, wv = psH.acquire()
                            p.on("tensor", None, wait=wv + [rowsw])
                            p.on("tensor", lambda e, pb=pb: e.matmul(
                                pb[:], self.ones_r[:, :], rdhi[:, :], start=True, stop=False))
                            p.on("tensor", lambda e, pb=pb: e.matmul(
                                pb[:], self.ones_r[:, :], rdlo[:, :], start=False, stop=True),
                                inc=[(tC, 1)])
                            p.on("scalar", lambda e, pb=pb: e.copy(rdsb[:], pb[:]),
                                 wait=[W(tC)] + ([W(vC)] if hh == 1 else []), inc=[(sC, 1)])
                            psH.rel[kb] = (sC, sC.count)
                            p.on("vector", lambda e, AVps=AVps, j=j, rb=rb: e.tensor_mul(
                                osb[rb:rb + 64, j * 512:(j + 1) * 512],
                                AVps[0:64, 0:512], rdsb[0:64, 0:512]),
                                wait=[W(sC), W(vC)], inc=[(vC, 1)])
                        self._avfree = (vC, vC.count)
                    attn_done = (vC, vC.count)
                    for cc in range(4, C):
                        p.on("vector", lambda e, cc=cc, cur=cur: e.tensor_copy(
                            ybnd[:, cc - 4:cc - 3], ysb[cur][:, cc * 513 + 512:cc * 513 + 513]),
                            wait=[W(vC)], inc=[(vC, 1)])
                    bnd_saved = (vC, vC.count)
                    p.on("tensor", None, wait=[attn_done])
                    for ec in range(C):
                        kb, pb, wv = psH.acquire()
                        if wv:
                            p.on("tensor", None, wait=wv)
                        for dc in range(4):
                            p.on("tensor", lambda e, ec=ec, dc=dc, pb=pb: e.matmul(
                                pb[:], Wo[:, dc * D + ec * 128:dc * D + (ec + 1) * 128],
                                osb[:, dc * 512:(dc + 1) * 512],
                                start=(dc == 0), stop=(dc == 3)),
                                inc=([(tC, 1)] if dc == 3 else []))
                        p.on("vector", lambda e, ec=ec, pb=pb, cur=cur: e.tensor_add(
                            ysb[cur][:, ec * 513 + 1:ec * 513 + 513],
                            ysb[cur][:, ec * 513 + 1:ec * 513 + 513], pb[:]),
                            wait=[W(tC), bnd_saved, W(vC)], inc=[(vC, 1)])
                        psH.rel[kb] = (vC, vC.count)
                    yp_done = (vC, vC.count)
                    if t == 0:
                        for cc in range(4, C):
                            p.on("vector", lambda e, cc=cc, cur=cur: e.memset(
                                ysb[cur][:, cc * 513:cc * 513 + 1], 0.0),
                                wait=[W(vC)], inc=[(vC, 1)])
                    else:
                        for cc in range(4, C):
                            p.on("vector", lambda e, cc=cc, cur=cur, prv=prv: e.tensor_copy(
                                ysb[cur][:, cc * 513:cc * 513 + 1],
                                ysb[prv][:, cc * 513 + 512:cc * 513 + 513]),
                                wait=[W(vC)], inc=[(vC, 1)])
                        if self._ysb_free[prv]:
                            self._ysb_free[prv] = self._ysb_free[prv] + [(vC, vC.count)]
                    for cc in range(C):
                        p.on("vector", lambda e, cc=cc, cur=cur: e.tensor_copy(
                            ybb[cur][:, cc * 513:(cc + 1) * 513],
                            ysb[cur][:, cc * 513:(cc + 1) * 513]),
                            wait=[W(vC)], inc=[(vC, 1)])
                    yb2 = (vC, vC.count)
                    p.on("sync", lambda e, cur=cur, t0=t0: e.dma_start(
                        out=YOUT[:, t0:t0 + 512].rearrange("(c p) t -> p c t", p=128),
                        in_=ysb[cur][:].rearrange("p (c t) -> p c t", c=C)[:, :, 1:513]),
                        wait=[yp_done], inc=[(outSp[cur], 16)])
                    def ypsl(cc, off, width, cur):
                        base = cc * 513 + (0 if cc >= 4 else 1)
                        return ybb[cur][:, base + off:base + off + width]
                    p.on("tensor", None, wait=[yb2])
                    for dc in range(C):
                        p.on("tensor", lambda e, dc=dc, cur=cur: e.matmul(
                            psRR, self.ones_c[:], ypsl(dc, 0, 512, cur),
                            start=(dc == 0), stop=(dc == C - 1)),
                            inc=([(tC, 1)] if dc == C - 1 else []))
                    p.on("vector", lambda e: e.tensor_copy(m3row[:], psRR[0:1, 0:512]),
                         wait=[W(tC)], inc=[(vC, 1)])
                    for dc in range(C):
                        p.on("scalar", lambda e, dc=dc, cur=cur: e.activation(
                            hsq[:], ypsl(dc, 0, 512, cur), AF.Square),
                            wait=([W(vC)] if dc == 0 else []) + [W(tC)], inc=[(sC, 1)])
                        p.on("tensor", lambda e, dc=dc: e.matmul(
                            psRR, self.ones_c[:],
                            hsq[:], start=(dc == 0), stop=(dc == C - 1)),
                            wait=[W(sC)], inc=[(tC, 1)])
                    p.on("vector", lambda e: e.tensor_copy(v3row[:], psRR[0:1, 0:512]),
                         wait=[W(tC)], inc=[(vC, 1)])
                    p.on("vector", lambda e: e.tensor_scalar_mul(m3row[:], m3row[:], 1.0 / D),
                         wait=[W(vC)], inc=[(vC, 1)])
                    p.on("vector", lambda e: e.tensor_scalar_mul(v3row[:], v3row[:], 1.0 / D),
                         wait=[W(vC)], inc=[(vC, 1)])
                    p.on("vector", lambda e: e.tensor_mul(s1t[0:1, :], m3row[:], m3row[:]),
                         wait=[W(vC)], inc=[(vC, 1)])
                    p.on("vector", lambda e: e.tensor_sub(v3row[:], v3row[:], s1t[0:1, :]),
                         wait=[W(vC)], inc=[(vC, 1)])
                    p.on("scalar", lambda e: e.activation(v3row[:], v3row[:], AF.Ln,
                                                          bias=self.epsc[0:1, 0:1]),
                         wait=[W(vC)], inc=[(sC, 1)])
                    p.on("scalar", lambda e: e.activation(v3row[:], v3row[:], AF.Exp,
                                                          scale=-0.5),
                         wait=[W(sC)], inc=[(sC, 1)])
                    p.on("vector", lambda e: e.tensor_mul(m3row[:], m3row[:], v3row[:]),
                         wait=[W(sC)], inc=[(vC, 1)])
                    p.on("vector", lambda e: e.tensor_scalar_mul(m3row[:], m3row[:], -1.0),
                         wait=[W(vC)], inc=[(vC, 1)])
                    rows3 = W(vC)
                    kb0, pb0, wv0 = psH.acquire()
                    kb1, pb1, wv1 = psH.acquire()
                    p.on("tensor", None, wait=wv0 + wv1 + [rows3])
                    p.on("tensor", lambda e, pb0=pb0: e.matmul(
                        pb0[:], ones_rf[:, :], v3row[:, :], start=True, stop=True))
                    p.on("tensor", lambda e, pb1=pb1: e.matmul(
                        pb1[:], ones_rf[:, :], m3row[:, :], start=True, stop=True),
                        inc=[(tC, 1)])
                    bc3 = W(tC)
                    for cc in range(C):
                        p.on("vector", lambda e, cc=cc, cur=cur, pb0=pb0: e.tensor_mul(
                            rt1[:], ysb[cur][:, cc * 513 + (0 if cc >= 4 else 1):
                                             cc * 513 + (0 if cc >= 4 else 1) + 512], pb0[:]),
                            wait=[bc3, W(vC)], inc=[(vC, 1)])
                        p.on("vector", lambda e, pb1=pb1: e.tensor_add(rt1[:], rt1[:], pb1[:]),
                             wait=[W(vC)], inc=[(vC, 1)])
                        hdst = qraw if cc < 4 else kraw
                        p.on("vector", lambda e, cc=cc, hdst=hdst: e.tensor_scalar(
                            hdst[:, (cc % 4) * 512:(cc % 4 + 1) * 512], rt1[:],
                            lgc[:, cc:cc + 1], lbc[:, cc:cc + 1], ALU.mult, ALU.add),
                            wait=[W(vC), cl], inc=[(vC, 1)])
                    psH.rel[kb0] = (vC, vC.count)
                    psH.rel[kb1] = (vC, vC.count)
                    hrdy = (vC, vC.count)
                    p.on("gpsimd", lambda e, t0=t0: e.dma_start(
                        out=HOUT[0:512, t0:t0 + 512].rearrange("(c p) t -> p c t", p=128),
                        in_=qraw[:].rearrange("p (c t) -> p c t", c=4)),
                        wait=[hrdy], inc=[(hSp[cur], 16)])
                    p.on("gpsimd", lambda e, t0=t0: e.dma_start(
                        out=HOUT[512:1024, t0:t0 + 512].rearrange("(c p) t -> p c t", p=128),
                        in_=kraw[:].rearrange("p (c t) -> p c t", c=4)),
                        inc=[(hSp[cur], 16)])
                    self._ysb_free[cur] = [(outSp[cur], outSp[cur].count),
                                           (hSp[cur], hSp[cur].count), hrdy]
            self.p.barrier()
            self._ysb_free = [None, None]
            self._avfree = None
            self._nmfree = None
            self._expw2 = [[None, None], [None, None]]
            self._attfree = [[None, None], [None, None]]

    def phase_fbc(self, l):
        cx, p, d = self.cx, self.p, self.d
        qW, vC, sC, tC = cx.qW, cx.vC, cx.sC, cx.tC
        W = self.w
        YIN = d["Y1"] if l == 0 else d["Y3"]
        HIN = d["HLN0"] if l == 0 else d["HLN1"]
        T = 512
        NTF = (self.nseq * L) // T
        with contextlib.ExitStack() as st:
            sb = lambda n, sh, dt=F32: st.enter_context(self.nc.sbuf_tensor(n, sh, dt))
            ps = lambda n, sh, dt=F32: st.enter_context(self.nc.psum_tensor(n, sh, dt))
            pre = f"g{l}"
            W1 = sb(pre + "W1", [128, C * 5460], BF16)
            W2 = sb(pre + "W2", [128, 22 * D], BF16)
            b1a = sb(pre + "b1a", [128, 22], F32)
            b1g = sb(pre + "b1g", [128, 22], F32)
            b2c = sb(pre + "b2c", [128, 8], F32)
            hin = [sb(pre + f"hin{i}", [128, C * T], BF16) for i in range(2)]
            yin = sb(pre + "yin", [128, C * T], F32)
            x1in = sb(pre + "x1in", [128, C * T], F32) if l == 1 else None
            gel = sb(pre + "gel", [128, T], BF16)
            gated = sb(pre + "gated", [128, 22 * T], BF16)
            psA = Rot([ps(pre + f"psA{i}", [128, T], F32) for i in range(2)])
            psG = Rot([ps(pre + f"psG{i}", [128, T], F32) for i in range(2)])
            psO = Rot([ps(pre + f"psO{i}", [128, T], F32) for i in range(2)])
            inS = [p.sem(pre + "in0"), p.sem(pre + "in1")]
            yS = p.sem(pre + "yS")
            xS = p.sem(pre + "xS")
            outS = p.sem(pre + "outS")
            p.on("gpsimd", lambda e: cast_w(e, W1, d[f"w1_{l}"]), inc=[(qW, 16)])
            p.on("gpsimd", lambda e: cast_w(e, W2, d[f"w2_{l}"]), inc=[(qW, 16)])
            for dst, src in ((b1a, f"b1a{l}"), (b1g, f"b1g{l}"), (b2c, f"b2_{l}")):
                p.on("sync", lambda e, dst=dst, src=src: e.dma_start(
                    out=dst[:], in_=d[src][:, :]), inc=[(cx.qL, 16)])
            wld = (qW, qW.count)
            cl = (cx.qL, cx.qL.count)
            hin_free = [None, None]
            y_free = None
            x_free = None
            for ti in range(NTF):
                bi = ti % 2
                t0 = ti * T
                p.on("sync", lambda e, bi=bi, t0=t0: fm_load(e, HIN, t0, T, hin[bi]),
                     wait=([hin_free[bi]] if hin_free[bi] else []), inc=[(inS[bi], 16)])
                p.on("sync", lambda e, t0=t0: fm_load(e, YIN, t0, T, yin),
                     wait=([y_free] if y_free else []), inc=[(yS, 16)])
                if l == 1:
                    p.on("sync", lambda e, t0=t0: fm_load(e, d["X1"], t0, T, x1in),
                         wait=([x_free] if x_free else []), inc=[(xS, 16)])
                ldw = (inS[bi], inS[bi].count)
                ydw = (yS, yS.count)
                xdw = (xS, xS.count)
                p.on("tensor", None, wait=[ldw, wld, cl])
                for c in range(22):
                    rows = 128 if c < 21 else 42
                    ka, pa, wva = psA.acquire()
                    if wva:
                        p.on("tensor", None, wait=wva)
                    for dc in range(C):
                        p.on("tensor", lambda e, c=c, dc=dc, pa=pa, bi=bi, rows=rows: e.matmul(
                            pa[0:rows, :], W1[:, dc * 5460 + c * 128:dc * 5460 + c * 128 + rows],
                            hin[bi][:, dc * T:(dc + 1) * T],
                            start=(dc == 0), stop=(dc == C - 1)),
                            inc=([(tC, 1)] if dc == C - 1 else []))
                    kg, pg, wvg = psG.acquire()
                    if wvg:
                        p.on("tensor", None, wait=wvg)
                    for dc in range(C):
                        p.on("tensor", lambda e, c=c, dc=dc, pg=pg, bi=bi, rows=rows: e.matmul(
                            pg[0:rows, :],
                            W1[:, dc * 5460 + 2730 + c * 128:dc * 5460 + 2730 + c * 128 + rows],
                            hin[bi][:, dc * T:(dc + 1) * T],
                            start=(dc == 0), stop=(dc == C - 1)),
                            inc=([(tC, 1)] if dc == C - 1 else []))
                    mmd = W(tC)
                    p.on("scalar", lambda e, c=c, pg=pg, rows=rows: e.activation(
                        gel[0:rows, :], pg[0:rows, :], AF.Gelu, bias=b1g[0:rows, c:c + 1]),
                        wait=[mmd, W(vC)], inc=[(sC, 1)])
                    p.on("vector", lambda e, c=c, pa=pa, rows=rows: e.scalar_tensor_tensor(
                        gated[0:rows, c * T:(c + 1) * T], pa[0:rows, :],
                        b1a[0:rows, c:c + 1], gel[0:rows, :], ALU.add, ALU.mult),
                        wait=[W(sC), W(vC)], inc=[(vC, 1)])
                    psA.rel[ka] = (vC, vC.count)
                    psG.rel[kg] = (vC, vC.count)
                gdone = (vC, vC.count)
                hin_free[bi] = (tC, tC.count)
                p.on("tensor", None, wait=[gdone])
                for ec in range(C):
                    ko, po, wvo = psO.acquire()
                    if wvo:
                        p.on("tensor", None, wait=wvo)
                    for c in range(22):
                        rows = 128 if c < 21 else 42
                        p.on("tensor", lambda e, ec=ec, c=c, po=po, rows=rows: e.matmul(
                            po[:], W2[0:rows, c * D + ec * 128:c * D + (ec + 1) * 128],
                            gated[0:rows, c * T:(c + 1) * T],
                            start=(c == 0), stop=(c == 21)),
                            inc=([(tC, 1)] if c == 21 else []))
                    if l == 1:
                        p.on("vector", lambda e, ec=ec: e.tensor_add(
                            yin[:, ec * T:(ec + 1) * T], yin[:, ec * T:(ec + 1) * T],
                            x1in[:, ec * T:(ec + 1) * T]),
                            wait=[W(vC), ydw, xdw], inc=[(vC, 1)])
                    p.on("vector", lambda e, ec=ec, po=po: e.scalar_tensor_tensor(
                        yin[:, ec * T:(ec + 1) * T], po[:], b2c[:, ec:ec + 1],
                        yin[:, ec * T:(ec + 1) * T], ALU.add, ALU.add),
                        wait=[W(tC), W(vC)] + ([ydw] if l == 0 else []),
                        inc=[(vC, 1)])
                    psO.rel[ko] = (vC, vC.count)
                ydone = (vC, vC.count)
                x_free = ydone
                OUTD = d["Y2"] if l == 0 else d["out"]
                p.on("sync", lambda e, t0=t0, OUTD=OUTD: fm_store(e, OUTD, t0, T, yin, nch=C),
                     wait=[ydone], inc=[(outS, 16)])
                y_free = (outS, outS.count)
            self.p.barrier()

    def build(self):
        self.declare_io()
        self.persistent()
        if "S" in self.phases and not self.fused:
            self.phase_stats()
        if self.fused and "S" in self.phases:
            with contextlib.ExitStack() as sst:
                self.Ssb = sst.enter_context(
                    self.nc.sbuf_tensor("Ssb", [128, C * D], BF16))
                self.syrow = sst.enter_context(
                    self.nc.sbuf_tensor("syrow", [1, 1024], F32))
                self.phase_p1()
                self.phase_gramB()
                self.phase_stats_fused()
        if "P23" in self.phases:
            with contextlib.ExitStack() as pst:
                self.vvs = [pst.enter_context(
                    self.nc.sbuf_tensor(f"vvs{s}", [128, 512], BF16))
                    for s in range(SPC)]
                self.phase_p23a()
                self.phase_p23b()
        if "Fa0" in self.phases:
            self.phase_fa(0)
        if "Fbc0" in self.phases:
            self.phase_fbc(0)
        if "Fa1" in self.phases:
            self.phase_fa(1)
        if "Fbc1" in self.phases:
            self.phase_fbc(1)
        self.p.on("sync", None, wait=[self.w(self.cx.qS)])
        self.p.emit(self.nc)
        self.cx.close()
        return self.d


def kernel(**inputs):
    g = prep_inputs(inputs)
    X = g["X"]
    nc = bass.Bass("TRN2", target_bir_lowering=False)
    m = Main(nc, fused=True, debug=False)
    m.build()
    in_maps = []
    for c in range(NCORE):
        im = {k: g[k] for k in g if k != "X" and k in m.d}
        im["x"] = np.ascontiguousarray(np.concatenate(
            [X[c * TOK:(c + 1) * TOK], X[:c * TOK], X[(c + 1) * TOK:]], 0))
        in_maps.append(im)
    res = run_bass_kernel_spmd(nc, in_maps, core_ids=list(range(NCORE)))
    out = np.concatenate([np.asarray(res.results[c]["out"]).T
                          for c in range(NCORE)], 0)
    return np.ascontiguousarray(out).reshape(B, L, D)


# version 07
# speedup vs baseline: 1.2485x; 1.2485x over previous
import contextlib
import numpy as np
import concourse.bass as bass
import concourse.mybir as mybir
from concourse.bass_utils import run_bass_kernel_spmd

"""Shared kernel-building library (development copy; final code is inlined
into kernel.py). Implements the full megakernel program for
mk_seq_preln_d1_jaxref_1024_a72310.

Layouts:
- tm tile: [128 tokens, F feats] (tokens on partitions)
- fm tile: [128 feats(chunk), T tokens] (feature chunk on partitions)
- HBM fm arrays: [1024(or F), 8192] rows=features (contiguous along tokens)
- HBM tm arrays: [8192, F]
"""
import numpy as np
import concourse.bass as bass
import concourse.mybir as mybir

F32 = mybir.dt.float32
BF16 = mybir.dt.bfloat16
AF = mybir.ActivationFunctionType
ALU = mybir.AluOpType
AX = mybir.AxisListType

B, L, D = 32, 2048, 1024
NCORE = 8
SPC = 4
TOK = SPC * L
C = 8
NT = TOK // 512
HEADS, HD = 16, 64
INNER, FH, FHD = 512, 8, 64
HID = 2730
WIN = 512
EPS = 1e-5

ENGINES = ("sync", "scalar", "vector", "tensor", "gpsimd")


class Sem:
    __slots__ = ("name", "h", "count")

    def __init__(self, name):
        self.name = name
        self.h = None
        self.count = 0


class Prog:
    def __init__(self):
        self.q = {e: [] for e in ENGINES}
        self.sems = []
        self._scope = None

    def scope(self, name):
        self._scope = name

    def sem(self, name):
        s = Sem(name)
        self.sems.append(s)
        return s

    def on(self, eng, build=None, wait=(), inc=()):
        w = [(s, v) for (s, v) in wait]
        i = [(s, k) for (s, k) in inc]
        self.q[eng].append((build, w, i, self._scope))
        for s, k in i:
            s.count += k

    def barrier(self, sems=None, skip=()):
        sems = [s for s in (sems if sems is not None else self.sems)
                if s.count > 0 and s not in skip]
        for e in ENGINES:
            self.on(e, None, wait=[(s, s.count) for s in sems])

    def emit(self, nc):
        import contextlib
        with contextlib.ExitStack() as stack:
            for s in self.sems:
                s.h = stack.enter_context(nc.semaphore(s.name))
            block = stack.enter_context(nc.Block())

            def replay(name):
                def run(eng):
                    cur = [None]
                    cm = [None]
                    for build, waits, incs, scope in self.q[name]:
                        if scope != cur[0]:
                            if cm[0] is not None:
                                cm[0].__exit__(None, None, None)
                                cm[0] = None
                            if scope is not None:
                                cm[0] = nc.named_scope(scope)
                                cm[0].__enter__()
                            cur[0] = scope
                        for s, v in waits:
                            if v > 0:
                                eng.wait_ge(s.h, v)
                        if build is not None:
                            ins = build(eng)
                            for s, k in incs:
                                ins = ins.then_inc(s.h, k)
                        elif incs:
                            ins = eng.nop()
                            for s, k in incs:
                                ins = ins.then_inc(s.h, k)
                    if cm[0] is not None:
                        cm[0].__exit__(None, None, None)
                return run

            block.sync(replay("sync"))
            block.scalar(replay("scalar"))
            block.vector(replay("vector"))
            block.tensor(replay("tensor"))
            block.gpsimd(replay("gpsimd"))


def pack8(v):
    return np.ascontiguousarray(v.reshape(8, 128).T).astype(np.float32)


def make_consts():
    c = {}
    c["identb"] = np.eye(128, dtype=np.float32)
    c["onesrow"] = np.ones((1, 128), np.float32)
    c["onescol"] = np.ones((128, 1), np.float32)
    sel = np.zeros((128, 2), np.float32)
    sel[0:64, 0] = 1.0
    sel[64:128, 1] = 1.0
    c["sel"] = sel
    c["sel2"] = np.ascontiguousarray(sel.T)
    i_idx = np.arange(128)[:, None]
    j_idx = np.arange(128)[None, :]
    NEG = np.float32(-1e30)
    c["mA_prev"] = np.where(j_idx >= i_idx, 0.0, NEG).astype(np.float32)
    c["mA_cur"] = np.where(j_idx <= i_idx, 0.0, NEG).astype(np.float32)
    c["mB_prev"] = np.ascontiguousarray(c["mA_prev"].T)
    c["mB_cur"] = np.ascontiguousarray(c["mA_cur"].T)
    d = np.arange(64)
    inv_freq = 1.0 / 10000.0 ** (np.arange(0, 64, 2, dtype=np.float64) / 64.0)
    t = np.arange(L, dtype=np.float64)
    freqs = t[None, :] * inv_freq[d % 32][:, None]
    base = (np.arange(0, 64, 2, dtype=np.float64) + 0.4 * 64) / (1.4 * 64)
    power = (t - L // 2) / float(WIN // 2)
    scale = base[d % 32][:, None] ** power[None, :]
    cos, sin = np.cos(freqs), np.sin(freqs)
    sgn = np.where(d % 64 < 32, -1.0, 1.0)[:, None]
    CSQ = (cos * scale)
    SSQ = (sgn * sin * scale)
    CSK = (cos / scale)
    SSK = (sgn * sin / scale)
    for nm, a in (("CSQ", CSQ), ("SSQ", SSQ), ("CSK", CSK), ("SSK", SSK)):
        c[nm] = np.ascontiguousarray(np.tile(a, (2, 1))).astype(np.float32)
    return c


def prep_inputs(inp):
    g = {}
    X = np.ascontiguousarray(np.asarray(inp["L0_attn_x"], np.float32).reshape(B * L, D))
    g["X"] = X
    for w in ("Wq", "Wk", "Wv", "Wp"):
        g[w + "T"] = np.ascontiguousarray(np.asarray(inp["L0_attn_" + w], np.float32).T)
    for w in ("Wq", "Wk", "Wv"):
        g[w + "U"] = np.ascontiguousarray(np.asarray(inp["L0_attn_" + w], np.float32))
    for v in ("gq", "bq", "gk", "bk", "gv", "bv", "bp"):
        g[v] = pack8(np.asarray(inp["L0_attn_" + v], np.float32))
    wqkv = np.asarray(inp["L0_ffn_wqkv"], np.float32)
    g["wqkv0"], g["wqkv1"] = np.ascontiguousarray(wqkv[0]), np.ascontiguousarray(wqkv[1])
    wo = np.asarray(inp["L0_ffn_wo"], np.float32)
    g["wo0"], g["wo1"] = np.ascontiguousarray(wo[0]), np.ascontiguousarray(wo[1])
    w1 = np.asarray(inp["L0_ffn_w1"], np.float32)
    g["w1_0"], g["w1_1"] = np.ascontiguousarray(w1[0]), np.ascontiguousarray(w1[1])
    w2 = np.asarray(inp["L0_ffn_w2"], np.float32)
    w2p = np.zeros((2, 22 * 128, D), np.float32)
    w2p[:, :HID, :] = w2
    g["w2_0"], g["w2_1"] = np.ascontiguousarray(w2p[0]), np.ascontiguousarray(w2p[1])
    for l in range(2):
        qs = np.asarray(inp["L0_ffn_q_scale"], np.float32)[l]
        ks = np.asarray(inp["L0_ffn_k_scale"], np.float32)[l]
        g[f"qs{l}"] = np.tile(qs, 2)[:, None].astype(np.float32)
        g[f"ks{l}"] = np.tile(ks, 2)[:, None].astype(np.float32)
        g[f"lng{l}"] = pack8(np.asarray(inp["L0_ffn_ln_g"], np.float32)[l])
        g[f"lnb{l}"] = pack8(np.asarray(inp["L0_ffn_ln_b"], np.float32)[l])
        b1 = np.asarray(inp["L0_ffn_b1"], np.float32)[l]
        b1a = np.zeros(22 * 128, np.float32)
        b1a[:HID] = b1[:HID]
        b1g = np.zeros(22 * 128, np.float32)
        b1g[:HID] = b1[HID:]
        g[f"b1a{l}"] = np.ascontiguousarray(b1a.reshape(22, 128).T)
        g[f"b1g{l}"] = np.ascontiguousarray(b1g.reshape(22, 128).T)
        g[f"b2_{l}"] = pack8(np.asarray(inp["L0_ffn_b2"], np.float32)[l])
    g.update(make_consts())
    return g


"""Program builders for the megakernel (development copy).

Conventions:
- Prog.on(engine, build, wait=[(sem, val)], inc=[(sem, k)])
- PSUM bank rotation via Rot: acquire() returns (bank_tile, wait_list); after
  the last reader of the bank's contents, call rot.release(reader_sem_count).
"""
import contextlib
import numpy as np
import concourse.bass as bass
import concourse.mybir as mybir


class Rot:

    def __init__(self, banks):
        self.banks = banks
        self.rel = [None] * len(banks)
        self.i = 0

    def acquire(self):
        k = self.i % len(self.banks)
        self.i += 1
        w = [self.rel[k]] if self.rel[k] else []
        return k, self.banks[k], w

    def release(self, k, sem):
        self.rel[k] = (sem, sem.count)


class Ctx:
    def __init__(self, nc, p):
        self.nc = nc
        self.p = p
        self.stack = contextlib.ExitStack()
        for name in ("qL", "qS", "qW", "vC", "sC", "tC", "gC"):
            setattr(self, name, p.sem(name))

    def sb(self, name, shape, dt=F32):
        return self.stack.enter_context(self.nc.sbuf_tensor(name, shape, dt))

    def ps(self, name, shape, dt=F32):
        return self.stack.enter_context(self.nc.psum_tensor(name, shape, dt))

    def close(self):
        self.stack.close()


def cast_w(e, dst, src):
    R, F = src.shape
    nch = R // 128
    return e.dma_start(
        out=dst[:, 0:nch * F].rearrange("p (c f) -> p c f", c=nch),
        in_=src.rearrange("(c p) f -> p c f", p=128))


def fm_store(e, dram, t0, T, sb, nch=8):
    return e.dma_start(
        out=dram[:, t0:t0 + T].rearrange("(c p) t -> p c t", p=128),
        in_=sb[:, 0:nch * T].rearrange("p (c t) -> p c t", c=nch))


def fm_load(e, dram, t0, T, sb, nch=8):
    return e.dma_start(
        out=sb[:, 0:nch * T].rearrange("p (c t) -> p c t", c=nch),
        in_=dram[:, t0:t0 + T].rearrange("(c p) t -> p c t", p=128))


def build_launch_a(nc, stages="all", ntiles=NT):
    p = Prog()
    cx = Ctx(nc, p)
    d = {}
    d["x"] = nc.dram_tensor("x", [TOK, D], F32, kind="ExternalInput").ap()
    for w in ("WqT", "WkT", "WvT"):
        d[w] = nc.dram_tensor(w, [D, D], F32, kind="ExternalInput").ap()
    d["identb"] = nc.dram_tensor("identb", [128, 128], F32, kind="ExternalInput").ap()
    d["bnp"] = nc.dram_tensor("bnp", [128, 48], F32, kind="ExternalOutput").ap()
    d["rows"] = nc.dram_tensor("rows", [2 * TOK], F32, kind="ExternalOutput").ap()
    d["cols"] = nc.dram_tensor("cols", [128, 128], F32, kind="ExternalOutput").ap()

    idb = cx.sb("idb", [128, 128], BF16)
    idf = cx.sb("idf", [128, 128], F32)
    W = {w: cx.sb("s" + w, [128, C * D], BF16) for w in ("WqT", "WkT", "WvT")}
    xin = [cx.sb(f"xin{i}", [128, 4 * D], F32) for i in range(2)]
    xlnf = [cx.sb(f"xlnf{i}", [128, C * 512], BF16) for i in range(2)]
    bnbuf = cx.sb("bnbuf", [128, 12], F32)
    mv = cx.sb("mv", [128, 2], F32)
    rcol = cx.sb("rcol", [128, 1], F32)
    nmr = cx.sb("nmr", [128, 1], F32)
    stash = cx.sb("stash", [128, 8], F32)
    rowb = cx.sb("rowb", [8, 128], F32)
    acc = cx.sb("acc", [128, 48], F32)
    tmpc = cx.sb("tmpc", [128, 1], F32)
    sqc = cx.sb("sqc", [128, 1], F32)
    epsc = cx.sb("epsc", [128, 1], F32)
    trash = cx.sb("trash", [128, 512], BF16)
    xln_tm = cx.sb("xln_tm", [128, D], BF16)
    psT = cx.ps("psT", [128, D], BF16)
    psH = Rot([cx.ps(f"psH{i}", [128, 512], F32) for i in range(2)])
    psR = cx.ps("psR", [8, 128], F32)

    p.on("gpsimd", lambda e: e.dma_start(out=idb[:], in_=d["identb"][:, :]), inc=[(cx.qW, 16)])
    p.on("sync", lambda e: e.dma_start(out=idf[:], in_=d["identb"][:, :]), inc=[(cx.qL, 16)])
    for w in ("WqT", "WkT", "WvT"):
        p.on("gpsimd", lambda e, w=w: cast_w(e, W[w], d[w]), inc=[(cx.qW, 16)])
    wl = cx.qW.count
    p.on("vector", lambda e: e.memset(acc[:], 0.0))
    p.on("vector", lambda e: e.memset(epsc[:], EPS), inc=[(cx.vC, 1)])

    xin_free = [None, None]
    xlnf_free = [None, None]
    for i in range(ntiles):
        bi = i % 2
        t0 = i * 512
        p.on("sync", lambda e, t0=t0, bi=bi: e.dma_start(
            out=xin[bi][:].rearrange("p (g d) -> p g d", g=4),
            in_=d["x"][t0:t0 + 512, :].rearrange("(g p) d -> p g d", p=128)),
            wait=([xin_free[bi]] if xin_free[bi] else []), inc=[(cx.qL, 16)])
        ld_t = cx.qL.count
        if xlnf_free[bi]:
            p.on("vector", None, wait=[xlnf_free[bi]])
        for gi in range(4 if stages != "ld" else 0):
            xg = xin[bi][:, gi * D:(gi + 1) * D]
            p.on("vector", lambda e, xg=xg: e.bn_stats(bnbuf[:, 0:6], xg[:, 0:512]),
                 wait=[(cx.qL, ld_t)])
            p.on("vector", lambda e, xg=xg: e.bn_stats(bnbuf[:, 6:12], xg[:, 512:1024]),
                 inc=[(cx.vC, 1)])
            p.on("vector", lambda e: e.bn_aggr(mv[:, 0:2], bnbuf[:, 0:12]),
                 wait=[(cx.sC, cx.sC.count), (cx.vC, cx.vC.count)], inc=[(cx.vC, 1)])
            p.on("scalar", lambda e: e.activation(rcol[:], mv[:, 1:2], AF.Ln, bias=epsc[:, 0:1]),
                 wait=[(cx.vC, cx.vC.count)], inc=[(cx.sC, 1)])
            p.on("scalar", lambda e: e.activation(rcol[:], rcol[:], AF.Exp, scale=-0.5),
                 inc=[(cx.sC, 1)])
            p.on("vector", lambda e: e.tensor_scalar(nmr[:], mv[:, 0:1], rcol[:, 0:1], -1.0,
                                                     ALU.mult, ALU.mult),
                 wait=[(cx.sC, cx.sC.count)])
            p.on("vector", lambda e, gi=gi: e.tensor_copy(stash[:, gi:gi + 1], mv[:, 0:1]))
            p.on("vector", lambda e, gi=gi: e.tensor_copy(stash[:, 4 + gi:5 + gi],
                                                          rcol[:, 0:1]), inc=[(cx.vC, 1)])
            p.on("vector", lambda e, xg=xg: e.tensor_scalar(
                xln_tm[:], xg, rcol[:, 0:1], nmr[:, 0:1], ALU.mult, ALU.add),
                wait=[(cx.tC, cx.tC.count)], inc=[(cx.vC, 1)])
            if stages == "ln":
                continue
            p.on("tensor", None, wait=[(cx.vC, cx.vC.count)])
            for cc in range(C):
                last = cc == C - 1
                p.on("tensor", lambda e, cc=cc: e.transpose(
                    psT[:, cc * 128:(cc + 1) * 128], xln_tm[:, cc * 128:(cc + 1) * 128],
                    idb[:]), inc=([(cx.tC, 1)] if last else []))
            p.on("vector", lambda e, bi=bi, gi=gi: e.tensor_copy(
                xlnf[bi][:].rearrange("p (c t) -> p c t", c=C)[:, :, gi * 128:(gi + 1) * 128],
                psT[:].rearrange("p (c t) -> p c t", c=C)),
                wait=[(cx.tC, cx.tC.count)], inc=[(cx.vC, 1)])
        xvread = cx.vC.count
        p.on("tensor", None, wait=[(cx.vC, xvread), (cx.qW, wl)])
        for br, w in (enumerate(("WqT", "WkT", "WvT")) if stages in ("all", "mm", "mm_nosq") else []):
            for ec in range(C):
                kb, pb, wv = psH.acquire()
                if wv:
                    p.on("tensor", None, wait=wv)
                for dc in range(C):
                    p.on("tensor", lambda e, w=w, ec=ec, dc=dc, pb=pb, bi=bi: e.matmul(
                        pb[:], W[w][:, dc * D + ec * 128:dc * D + (ec + 1) * 128],
                        xlnf[bi][:, dc * 512:(dc + 1) * 512],
                        start=(dc == 0), stop=(dc == C - 1)),
                        inc=([(cx.tC, 1)] if dc == C - 1 else []))
                tw = cx.tC.count
                col = br * 16 + ec
                p.on("vector", lambda e, pb=pb: e.reduce_sum(tmpc[:], pb[:], axis=AX.X),
                     wait=[(cx.tC, tw)], inc=[(cx.vC, 1)])
                p.on("vector", lambda e, col=col: e.tensor_add(
                    acc[:, col:col + 1], acc[:, col:col + 1], tmpc[:]),
                    wait=[(cx.vC, cx.vC.count)], inc=[(cx.vC, 1)])
                p.on("scalar", lambda e, pb=pb: e.activation(
                    trash[:], pb[:], AF.Square, accum_out=sqc[:]),
                    wait=[(cx.tC, tw), (cx.vC, cx.vC.count)], inc=[(cx.sC, 1)])
                p.on("vector", lambda e, col=col: e.tensor_add(
                    acc[:, col + 8:col + 9], acc[:, col + 8:col + 9], sqc[:]),
                    wait=[(cx.sC, cx.sC.count)], inc=[(cx.vC, 1)])
                psH.release(kb, cx.vC)
                psH.rel[kb] = (cx.vC, cx.vC.count)
        xlnf_free[bi] = (cx.tC, cx.tC.count)
        xin_free[bi] = (cx.vC, xvread)
        if stages in ("ld",):
            xin_free[bi] = (cx.qL, cx.qL.count)
            continue
        p.on("tensor", lambda e: e.transpose(psR[:], stash[:], idf[:]),
             wait=[(cx.vC, cx.vC.count)], inc=[(cx.tC, 1)])
        p.on("vector", lambda e: e.tensor_copy(rowb[:], psR[:]),
             wait=[(cx.tC, cx.tC.count)], inc=[(cx.vC, 1)])
        rv = (cx.vC, cx.vC.count)
        p.on("sync", lambda e, t0=t0: e.dma_start(
            out=d["rows"][t0:t0 + 512].rearrange("(g q) -> g q", g=4),
            in_=rowb[0:4, :]), wait=[rv], inc=[(cx.qS, 16)])
        p.on("sync", lambda e, t0=t0: e.dma_start(
            out=d["rows"][TOK + t0:TOK + t0 + 512].rearrange("(g q) -> g q", g=4),
            in_=rowb[4:8, :]), inc=[(cx.qS, 16)])
        p.on("sync", lambda e, i=i: e.dma_start(
            out=d["cols"][:, i * 4:(i + 1) * 4], in_=stash[:, 0:4]),
            wait=[rv], inc=[(cx.qS, 16)])
        p.on("sync", lambda e, i=i: e.dma_start(
            out=d["cols"][:, 64 + i * 4:64 + (i + 1) * 4], in_=stash[:, 4:8]),
            inc=[(cx.qS, 16)])
        stash_free = (cx.qS, cx.qS.count)
        p.on("vector", None, wait=[stash_free])
    p.on("sync", lambda e: e.dma_start(out=d["bnp"][:, :], in_=acc[:]),
         wait=[(cx.vC, cx.vC.count)], inc=[(cx.qS, 16)])
    p.on("sync", None, wait=[(cx.qS, cx.qS.count)])
    p.emit(nc)
    cx.close()
    return d



class Main:

    def __init__(self, nc, fused=False, debug=False,
                 phases=("S", "P23", "Fa0", "Fbc0", "Fa1", "Fbc1"), nseq=SPC):
        self.nseq = nseq
        self.nc = nc
        self.fused = fused
        self.debug = debug
        self.phases = phases
        self.p = Prog()
        self.cx = Ctx(nc, self.p)
        self.d = {}
        self._ysb_free = [None, None]
        self._otm_free = None
        self._tab_free = None
        self._avfree = None
        self._nmfree = None
        self._expw2 = [[None, None], [None, None]]
        self._attfree = [[None, None], [None, None]]

    def sb(self, name, shape, dt=F32):
        return self.cx.sb(name, shape, dt)

    def ps(self, name, shape, dt=F32):
        return self.cx.ps(name, shape, dt)

    def on(self, eng, build=None, wait=(), inc=()):
        self.p.on(eng, build, wait=wait, inc=inc)

    def w(self, sem):
        return (sem, sem.count)

    def declare_io(self):
        nc, d = self.nc, self.d
        DI = lambda n, sh: nc.dram_tensor(n, sh, F32, kind="ExternalInput").ap()
        d["x"] = DI("x", [(B * L) if self.fused else TOK, D])
        for w in ("WqT", "WkT", "WvT", "WpT"):
            d[w] = DI(w, [D, D])
        for v in ("gq", "bq", "gk", "bk", "gv", "bv", "bp"):
            d[v] = DI(v, [128, 8])
        for l in range(2):
            d[f"wqkv{l}"] = DI(f"wqkv{l}", [D, 1536])
            d[f"wo{l}"] = DI(f"wo{l}", [512, D])
            d[f"w1_{l}"] = DI(f"w1_{l}", [D, 5460])
            d[f"w2_{l}"] = DI(f"w2_{l}", [22 * 128, D])
            d[f"qs{l}"] = DI(f"qs{l}", [128, 1])
            d[f"ks{l}"] = DI(f"ks{l}", [128, 1])
            d[f"lng{l}"] = DI(f"lng{l}", [128, 8])
            d[f"lnb{l}"] = DI(f"lnb{l}", [128, 8])
            d[f"b1a{l}"] = DI(f"b1a{l}", [128, 22])
            d[f"b1g{l}"] = DI(f"b1g{l}", [128, 22])
            d[f"b2_{l}"] = DI(f"b2_{l}", [128, 8])
        d["identb"] = DI("identb", [128, 128])
        d["onesrow"] = DI("onesrow", [1, 128])
        d["onescol"] = DI("onescol", [128, 1])
        d["sel"] = DI("sel", [128, 2])
        d["sel2"] = DI("sel2", [2, 128])
        for m in ("mA_prev", "mA_cur", "mB_prev", "mB_cur"):
            d[m] = DI(m, [128, 128])
        for t in ("CSQ", "SSQ", "CSK", "SSK"):
            d[t] = DI(t, [128, L])
        if not self.fused:
            d["bnps"] = DI("bnps", [8 * 128, 48])
            d["ln_rows"] = DI("ln_rows", [2 * TOK])
            d["ln_cols"] = DI("ln_cols", [128, 128])
        else:
            for wn in ("Wq", "Wk", "Wv"):
                d[wn + "U"] = DI(wn + "U", [D, D])
        d["out"] = nc.dram_tensor("out", [D, TOK], F32, kind="ExternalOutput").ap()
        kind = "ExternalOutput" if self.debug else "Internal"
        DSC = lambda n, sh, dt=F32: nc.dram_tensor(n, sh, dt, kind=kind).ap()
        d["XLN"] = DSC("XLN", [D, TOK], BF16)
        d["X0F"] = DSC("X0F", [D, TOK])
        d["X1"] = DSC("X1", [D, TOK])
        d["S1"] = DSC("S1", [2 * TOK])
        d["Y1"] = DSC("Y1", [D, TOK])
        d["HLN0"] = DSC("HLN0", [D, TOK], BF16)
        d["Y2"] = DSC("Y2", [D, TOK])
        d["Y3"] = DSC("Y3", [D, TOK])
        d["HLN1"] = DSC("HLN1", [D, TOK], BF16)
        if self.fused:
            d["YTM"] = DSC("YTM", [B * L, D], BF16)
        if self.debug:
            d["DBGA"] = DSC("DBGA", [128, 48])
            d["DBGS"] = DSC("DBGS", [128, 64])
            d["DBGSS"] = DSC("DBGSS", [128, C * D], BF16)
            d["DBGM"] = DSC("DBGM", [128, C * D], BF16)
            d["DBGQ"] = DSC("DBGQ", [128, C * 512], BF16)
            d["DBGK"] = DSC("DBGK", [128, C * 512], BF16)

    def persistent(self):
        cx, p, d = self.cx, self.p, self.d
        self.idb = self.sb("idb", [128, 128], BF16)
        self.idf = self.sb("idf", [128, 128], F32)
        self.ones_r = self.sb("ones_r", [1, 128], BF16)
        self.ones_c = self.sb("ones_c", [128, 1], BF16)
        self.sel_b = self.sb("sel_b", [128, 2], BF16)
        self.sel2_b = self.sb("sel2_b", [2, 128], BF16)
        self.mA = {"prev": self.sb("mAp", [128, 128], BF16),
                   "cur": self.sb("mAc", [128, 128], BF16)}
        self.mB = {"prev": self.sb("mBp", [128, 128], BF16),
                   "cur": self.sb("mBc", [128, 128], BF16)}
        self.epsc = self.sb("epsc", [128, 1], F32)
        self.AB = {n: self.sb("AB" + n, [128, 8], F32)
                   for n in ("Aq", "Bq", "Ak", "Bk", "Av", "Bv")}
        self.bpc = self.sb("bpc", [128, 8], F32)
        self.nmrc = self.sb("nmrc", [128, 64], F32)
        self.rc = self.sb("rc", [128, 128], F32)
        for dst, src in ((self.idb, "identb"), (self.ones_r, "onesrow"),
                         (self.ones_c, "onescol"), (self.sel_b, "sel"),
                         (self.sel2_b, "sel2"),
                         (self.mA["prev"], "mA_prev"), (self.mA["cur"], "mA_cur"),
                         (self.mB["prev"], "mB_prev"), (self.mB["cur"], "mB_cur")):
            p.on("gpsimd", lambda e, dst=dst, src=src: e.dma_start(
                out=dst[:], in_=d[src][:, :]), inc=[(cx.qW, 16)])
        p.on("sync", lambda e: e.dma_start(out=self.idf[:], in_=d["identb"][:, :]),
             inc=[(cx.qL, 16)])
        p.on("sync", lambda e: e.dma_start(out=self.bpc[:], in_=d["bp"][:, :]),
             inc=[(cx.qL, 16)])
        p.on("vector", lambda e: e.memset(self.epsc[:], EPS), inc=[(cx.vC, 1)])

    def phase_stats(self):
        cx, p, d = self.cx, self.p, self.d
        bn8 = self.sb("bn8", [128, 8 * 48], F32)
        accs = self.sb("accs", [128, 48], F32)
        t1 = self.sb("St1", [128, 8], F32)
        t2 = self.sb("St2", [128, 8], F32)
        t3 = self.sb("St3", [128, 8], F32)
        gcol = self.sb("gcol", [128, 8], F32)
        bcol = self.sb("bcol", [128, 8], F32)
        p.on("sync", lambda e: e.dma_start(
            out=bn8[:].rearrange("p (c f) -> p c f", c=8),
            in_=d["bnps"][:, :].rearrange("(c p) f -> p c f", p=128)), inc=[(cx.qL, 16)])
        p.on("sync", lambda e: e.dma_start(out=self.rc[:], in_=d["ln_cols"][:, :]),
             inc=[(cx.qL, 16)])
        lded = cx.qL.count
        p.on("vector", lambda e: e.tensor_add(accs[:], bn8[:, 0:48], bn8[:, 48:96]),
             wait=[(cx.qL, lded)], inc=[(cx.vC, 1)])
        for cc in range(2, 8):
            p.on("vector", lambda e, cc=cc: e.tensor_add(
                accs[:], accs[:], bn8[:, cc * 48:(cc + 1) * 48]),
                wait=[self.w(cx.vC)], inc=[(cx.vC, 1)])
        NTOT = float(B * L)
        for br, (gn, bnm, An, Bn) in enumerate((("gq", "bq", "Aq", "Bq"),
                                                ("gk", "bk", "Ak", "Bk"),
                                                ("gv", "bv", "Av", "Bv"))):
            s0, s1_ = br * 16, br * 16 + 8
            p.on("sync", lambda e, gn=gn: e.dma_start(out=gcol[:], in_=d[gn][:, :]),
                 wait=[self.w(cx.vC), self.w(cx.qS)], inc=[(cx.qL, 16)])
            p.on("sync", lambda e, bnm=bnm: e.dma_start(out=bcol[:], in_=d[bnm][:, :]),
                 inc=[(cx.qL, 16)])
            lb = cx.qL.count
            p.on("vector", lambda e, s0=s0: e.tensor_scalar_mul(
                t1[:], accs[:, s0:s0 + 8], 1.0 / NTOT),
                wait=[(cx.qL, lb), self.w(cx.vC), self.w(cx.sC)], inc=[(cx.vC, 1)])
            p.on("vector", lambda e, s1_=s1_: e.tensor_scalar_mul(
                t2[:], accs[:, s1_:s1_ + 8], 1.0 / NTOT), inc=[(cx.vC, 1)])
            p.on("vector", lambda e: e.tensor_mul(t3[:], t1[:], t1[:]),
                 wait=[self.w(cx.vC)], inc=[(cx.vC, 1)])
            p.on("vector", lambda e: e.tensor_sub(t2[:], t2[:], t3[:]),
                 wait=[self.w(cx.vC)], inc=[(cx.vC, 1)])
            p.on("scalar", lambda e: e.activation(t2[:], t2[:], AF.Ln, bias=self.epsc[:, 0:1]),
                 wait=[self.w(cx.vC)], inc=[(cx.sC, 1)])
            p.on("scalar", lambda e: e.activation(t2[:], t2[:], AF.Exp, scale=-0.5),
                 wait=[self.w(cx.sC)], inc=[(cx.sC, 1)])
            p.on("vector", lambda e, An=An: e.tensor_mul(self.AB[An][:], gcol[:], t2[:]),
                 wait=[self.w(cx.sC)], inc=[(cx.vC, 1)])
            p.on("vector", lambda e, An=An: e.tensor_scalar_mul(
                self.AB[An][:], self.AB[An][:], 4.0), wait=[self.w(cx.vC)], inc=[(cx.vC, 1)])
            p.on("vector", lambda e, An=An: e.tensor_mul(t3[:], t1[:], self.AB[An][:]),
                 wait=[self.w(cx.vC)], inc=[(cx.vC, 1)])
            p.on("vector", lambda e, Bn=Bn: e.tensor_scalar(
                self.AB[Bn][:], bcol[:], 4.0, -4.0, ALU.mult, ALU.add), inc=[(cx.vC, 1)])
            p.on("vector", lambda e, Bn=Bn: e.tensor_sub(self.AB[Bn][:], self.AB[Bn][:], t3[:]),
                 wait=[self.w(cx.vC)], inc=[(cx.vC, 1)])
            if self.debug and br == 0:
                p.on("sync", lambda e: e.dma_start(out=d["DBGS"][:, 48:56], in_=t2[:]),
                     wait=[self.w(cx.vC), self.w(cx.sC)], inc=[(cx.qS, 16)])
                p.on("sync", lambda e: e.dma_start(out=d["DBGS"][:, 56:64], in_=gcol[:]),
                     inc=[(cx.qS, 16)])
                p.on("sync", lambda e, An=An: e.dma_start(out=d["DBGS"][:, 40:48],
                     in_=self.AB[An][:]), inc=[(cx.qS, 16)])
        p.on("vector", lambda e: e.tensor_mul(self.nmrc[:], self.rc[:, 0:64],
                                              self.rc[:, 64:128]),
             wait=[(cx.qL, lded)], inc=[(cx.vC, 1)])
        p.on("vector", lambda e: e.tensor_scalar_mul(self.nmrc[:], self.nmrc[:], -1.0),
             wait=[self.w(cx.vC)], inc=[(cx.vC, 1)])
        if self.debug:
            p.on("sync", lambda e: e.dma_start(out=d["DBGS"][:, 0:48], in_=accs[:]),
                 wait=[self.w(cx.vC), self.w(cx.sC)], inc=[(cx.qS, 16)])
            for i, n in enumerate(("Aq", "Bq", "Ak", "Bk", "Av", "Bv")):
                p.on("sync", lambda e, i=i, n=n: e.dma_start(
                    out=d["DBGA"][:, i * 8:(i + 1) * 8], in_=self.AB[n][:]),
                    wait=[self.w(cx.vC), self.w(cx.sC)], inc=[(cx.qS, 16)])
        self.p.barrier()

    def phase_p1(self):
        cx, p, d = self.cx, self.p, self.d
        vC, sC, tC = cx.vC, cx.sC, cx.tC
        W = self.w
        NALL = (B * L) // 512
        with contextlib.ExitStack() as st:
            sb = lambda n, sh, dt=F32: st.enter_context(self.nc.sbuf_tensor(n, sh, dt))
            ps = lambda n, sh, dt=F32: st.enter_context(self.nc.psum_tensor(n, sh, dt))
            xin = [sb(f"p1xin{i}", [128, 4 * D], F32) for i in range(2)]
            ytile = [sb(f"p1y{i}", [128, 4 * D], BF16) for i in range(2)]
            bnbuf = [sb(f"p1bn{i}", [128, 12], F32) for i in range(2)]
            mv = [sb(f"p1mv{i}", [128, 2], F32) for i in range(2)]
            rcol = [sb(f"p1rcol{i}", [128, 1], F32) for i in range(2)]
            nmr = [sb(f"p1nmr{i}", [128, 1], F32) for i in range(2)]
            onebc = sb("p1oneb", [128, 1], BF16)
            psG = {mi: ps(f"p1g{mi}", [128, min(1024, ((8 - mi) * 128 + 511) // 512 * 512)])
                   for mi in (0, 1, 6, 7)}
            psMean = ps("p1mean", [1, 1024])
            inS = [p.sem("p1in0"), p.sem("p1in1")]
            ysS = [p.sem("p1ys0"), p.sem("p1ys1")]
            p.on("vector", lambda e: e.memset(onebc[:], 1.0), inc=[(vC, 1)])
            xin_free = [None, None]
            yt_free = [None, None]
            chain_free = [None, None]
            for i in range(NALL):
                bi = i % 2
                t0 = i * 512
                p.on("sync", lambda e, t0=t0, bi=bi: e.dma_start(
                    out=xin[bi][:].rearrange("p (g dd) -> p g dd", g=4),
                    in_=d["x"][t0:t0 + 512, :].rearrange("(g p) dd -> p g dd", p=128)),
                    wait=(xin_free[bi] or []), inc=[(inS[bi], 16)])
                ld_t = (inS[bi], inS[bi].count)
                if yt_free[bi]:
                    p.on("scalar", None, wait=yt_free[bi])
                    p.on("vector", None, wait=yt_free[bi])
                for gi in range(4):
                    par = gi % 2
                    xg = xin[bi][:, gi * D:(gi + 1) * D]
                    yg = ytile[bi][:, gi * D:(gi + 1) * D]
                    p.on("vector", lambda e, xg=xg, par=par: e.bn_stats(
                        bnbuf[par][:, 0:6], xg[:, 0:512]), wait=[ld_t])
                    p.on("vector", lambda e, xg=xg, par=par: e.bn_stats(
                        bnbuf[par][:, 6:12], xg[:, 512:1024]), inc=[(vC, 1)])
                    bnw = (vC, vC.count)
                    p.on("vector", lambda e, par=par: e.bn_aggr(
                        mv[par][:, 0:2], bnbuf[par][:, 0:12]),
                        wait=[bnw] + ([chain_free[par]] if chain_free[par] else []),
                        inc=[(vC, 1)])
                    aggw = (vC, vC.count)
                    p.on("scalar", lambda e, par=par: e.activation(
                        rcol[par][:], mv[par][:, 1:2], AF.Ln, bias=self.epsc[:, 0:1]),
                        wait=[aggw], inc=[(sC, 1)])
                    p.on("scalar", lambda e, par=par: e.activation(
                        rcol[par][:], rcol[par][:], AF.Exp, scale=-0.5),
                        wait=[W(sC)], inc=[(sC, 1)])
                    rw = (sC, sC.count)
                    p.on("vector", lambda e, par=par: e.tensor_scalar(
                        nmr[par][:], mv[par][:, 0:1], rcol[par][:, 0:1], -1.0,
                        ALU.mult, ALU.mult), wait=[rw], inc=[(vC, 1)])
                    nw = (vC, vC.count)
                    if i < NT:
                        idx = i * 4 + gi
                        p.on("vector", lambda e, idx=idx, par=par: e.tensor_copy(
                            self.rc[:, idx:idx + 1], mv[par][:, 0:1]),
                            wait=[aggw], inc=[(vC, 1)])
                        p.on("vector", lambda e, idx=idx, par=par: e.tensor_copy(
                            self.rc[:, 64 + idx:65 + idx], rcol[par][:, 0:1]),
                            wait=[rw], inc=[(vC, 1)])
                    if par == 0:
                        p.on("scalar", lambda e, xg=xg, yg=yg, par=par: e.activation(
                            yg, xg, AF.Identity, bias=nmr[par][:, 0:1],
                            scale=rcol[par][:, 0:1]), wait=[nw], inc=[(sC, 1)])
                        chain_free[par] = (sC, sC.count)
                    else:
                        p.on("vector", lambda e, xg=xg, yg=yg, par=par: e.tensor_scalar(
                            yg, xg, rcol[par][:, 0:1], nmr[par][:, 0:1],
                            ALU.mult, ALU.add), wait=[nw, rw], inc=[(vC, 1)])
                        chain_free[par] = (vC, vC.count)
                yrdy = [(sC, sC.count), (vC, vC.count)]
                p.on("gpsimd", lambda e, bi=bi, t0=t0: e.dma_start(
                    out=d["YTM"][t0:t0 + 512, :].rearrange("(g p) dd -> p g dd", p=128),
                    in_=ytile[bi][:].rearrange("p (g dd) -> p g dd", g=4)),
                    wait=yrdy, inc=[(ysS[bi], 16)])
                p.on("tensor", None, wait=yrdy + ([(vC, 1)] if i == 0 else []))
                for gi in range(4):
                    yg = ytile[bi][:, gi * D:(gi + 1) * D]
                    first = (i == 0 and gi == 0)
                    last = (i == NALL - 1 and gi == 3)
                    for mi in (0, 1, 6, 7):
                        w0 = mi * 128
                        wdt = 1024 - w0
                        for (off, wp) in ([(0, 512), (512, wdt - 512)] if wdt > 512
                                          else [(0, wdt)]):
                            p.on("tensor", lambda e, yg=yg, mi=mi, w0=w0, off=off, wp=wp,
                                 first=first, last=last: e.matmul(
                                psG[mi][:, off:off + wp],
                                yg[:, mi * 128:(mi + 1) * 128],
                                yg[:, w0 + off:w0 + off + wp],
                                start=first, stop=last))
                    for half in range(2):
                        p.on("tensor", lambda e, yg=yg, half=half, first=first, last=last:
                             e.matmul(psMean[0:1, half * 512:(half + 1) * 512],
                                      onebc[:], yg[:, half * 512:(half + 1) * 512],
                                      start=first, stop=last),
                             inc=([(tC, 1)] if (half == 1) else []))
                gdone = W(tC)
                xin_free[bi] = yrdy
                yt_free[bi] = [gdone, (ysS[bi], ysS[bi].count)]
            p.on("scalar", lambda e: e.copy(self.Ssb[:, 0 * D:1 * D], psG[0][:, 0:1024]),
                 wait=[gdone], inc=[(sC, 1)])
            p.on("scalar", lambda e: e.copy(self.Ssb[:, 1 * D + 128:2 * D], psG[1][:, 0:896]),
                 inc=[(sC, 1)])
            p.on("scalar", lambda e: e.copy(self.Ssb[:, 6 * D + 768:7 * D], psG[6][:, 0:256]),
                 inc=[(sC, 1)])
            p.on("scalar", lambda e: e.copy(self.Ssb[:, 7 * D + 896:8 * D], psG[7][:, 0:128]),
                 inc=[(sC, 1)])
            p.on("vector", lambda e: e.tensor_copy(self.syrow[:], psMean[0:1, :]),
                 wait=[gdone], inc=[(vC, 1)])
            self.p.barrier()

    def phase_gramB(self):
        cx, p, d = self.cx, self.p, self.d
        vC, sC, tC = cx.vC, cx.sC, cx.tC
        W = self.w
        NALL = (B * L) // 512
        with contextlib.ExitStack() as st:
            sb = lambda n, sh, dt=F32: st.enter_context(self.nc.sbuf_tensor(n, sh, dt))
            ps = lambda n, sh, dt=F32: st.enter_context(self.nc.psum_tensor(n, sh, dt))
            ytile = [sb(f"gBy{i}", [128, 4 * D], BF16) for i in range(2)]
            psG = {mi: ps(f"gBg{mi}", [128, min(1024, ((8 - mi) * 128 + 511) // 512 * 512)])
                   for mi in (2, 3, 4, 5)}
            inS = [p.sem("gBin0"), p.sem("gBin1")]
            yt_free = [None, None]
            for i in range(NALL):
                bi = i % 2
                t0 = i * 512
                p.on("sync", lambda e, t0=t0, bi=bi: e.dma_start(
                    out=ytile[bi][:].rearrange("p (g dd) -> p g dd", g=4),
                    in_=d["YTM"][t0:t0 + 512, :].rearrange("(g p) dd -> p g dd", p=128)),
                    wait=([yt_free[bi]] if yt_free[bi] else []), inc=[(inS[bi], 16)])
                ld_t = (inS[bi], inS[bi].count)
                p.on("tensor", None, wait=[ld_t])
                for gi in range(4):
                    yg = ytile[bi][:, gi * D:(gi + 1) * D]
                    first = (i == 0 and gi == 0)
                    last = (i == NALL - 1 and gi == 3)
                    for mi in (2, 3, 4, 5):
                        w0 = mi * 128
                        wdt = 1024 - w0
                        for (off, wp) in ([(0, 512), (512, wdt - 512)] if wdt > 512
                                          else [(0, wdt)]):
                            p.on("tensor", lambda e, yg=yg, mi=mi, w0=w0, off=off, wp=wp,
                                 first=first, last=last: e.matmul(
                                psG[mi][:, off:off + wp],
                                yg[:, mi * 128:(mi + 1) * 128],
                                yg[:, w0 + off:w0 + off + wp],
                                start=first, stop=last),
                                inc=([(tC, 1)] if (mi == 5 and gi == 3) else []))
                yt_free[bi] = W(tC)
            gdone = W(tC)
            p.on("scalar", lambda e: e.copy(self.Ssb[:, 2 * D + 256:3 * D], psG[2][:, 0:768]),
                 wait=[gdone], inc=[(sC, 1)])
            p.on("scalar", lambda e: e.copy(self.Ssb[:, 3 * D + 384:4 * D], psG[3][:, 0:640]),
                 inc=[(sC, 1)])
            p.on("scalar", lambda e: e.copy(self.Ssb[:, 4 * D + 512:5 * D], psG[4][:, 0:512]),
                 inc=[(sC, 1)])
            p.on("scalar", lambda e: e.copy(self.Ssb[:, 5 * D + 640:6 * D], psG[5][:, 0:384]),
                 inc=[(sC, 1)])
            self.p.barrier()

    def phase_stats_fused(self):
        cx, p, d = self.cx, self.p, self.d
        vC, sC, tC = cx.vC, cx.sC, cx.tC
        W = self.w
        NTOT = float(B * L)
        with contextlib.ExitStack() as st:
            sb = lambda n, sh, dt=F32: st.enter_context(self.nc.sbuf_tensor(n, sh, dt))
            ps = lambda n, sh, dt=F32: st.enter_context(self.nc.psum_tensor(n, sh, dt))
            Wq2 = sb("sfWq2", [128, C * D], BF16)
            WT = sb("sfWT", [128, C * D], BF16)
            sycol = sb("sfsycol", [128, 8], BF16)
            gcol = sb("sfgcol", [128, 8], F32)
            bcol = sb("sfbcol", [128, 8], F32)
            t1 = sb("sft1", [128, 8], F32)
            t2 = sb("sft2", [128, 8], F32)
            t3 = sb("sft3", [128, 8], F32)
            ucol = sb("sfu", [128, 1024], F32)
            psT = ps("sfpsT", [128, 1024])
            psR = ps("sfpsR", [128, 128])
            psRb = ps("sfpsRb", [128, 128], BF16)
            psE = ps("sfpsE", [1, 1024])
            ldS = p.sem("sfld")
            for dc in range(1, C):
                for dj in range(dc):
                    p.on("tensor", lambda e, dc=dc, dj=dj: e.transpose(
                        psRb[:], self.Ssb[:, dj * D + dc * 128:dj * D + (dc + 1) * 128],
                        self.idb[:]), wait=[W(sC), W(vC)], inc=[(tC, 1)])
                    p.on("vector", lambda e, dc=dc, dj=dj: e.tensor_copy(
                        self.Ssb[:, dc * D + dj * 128:dc * D + (dj + 1) * 128], psRb[:]),
                        wait=[W(tC)], inc=[(vC, 1)])
            mirr = [W(vC)]
            p.on("vector", lambda e: e.tensor_copy(ucol[0:1, :], self.syrow[:]),
                 inc=[(vC, 1)])
            for cc2 in range(C):
                p.on("tensor", lambda e, cc2=cc2: e.transpose(
                    psR[:, 0:1], ucol[0:1, cc2 * 128:(cc2 + 1) * 128], self.idf[0:1, 0:1]),
                    wait=[W(vC)], inc=[(tC, 1)])
                p.on("vector", lambda e, cc2=cc2: e.tensor_copy(
                    sycol[:, cc2:cc2 + 1], psR[:, 0:1]), wait=[W(tC)], inc=[(vC, 1)])
            for br, (wn, gn, bnm, An, Bn) in enumerate((
                    ("Wq", "gq", "bq", "Aq", "Bq"),
                    ("Wk", "gk", "bk", "Ak", "Bk"),
                    ("Wv", "gv", "bv", "Av", "Bv"))):
                p.on("gpsimd", lambda e, wn=wn: cast_w(e, Wq2, d[wn + "U"]),
                     wait=[W(sC), W(vC)], inc=[(cx.qW, 16)])
                p.on("gpsimd", lambda e, wn=wn: cast_w(e, WT, d[wn + "T"]),
                     inc=[(cx.qW, 16)])
                p.on("sync", lambda e, gn=gn: e.dma_start(out=gcol[:], in_=d[gn][:, :]),
                     wait=[W(vC)], inc=[(ldS, 16)])
                p.on("sync", lambda e, bnm=bnm: e.dma_start(out=bcol[:], in_=d[bnm][:, :]),
                     inc=[(ldS, 16)])
                wl = [(cx.qW, cx.qW.count), (ldS, ldS.count)]
                for ec in range(C):
                    p.on("tensor", None, wait=wl + mirr + [W(vC)])
                    for dc in range(C):
                        for half in range(2):
                            p.on("tensor", lambda e, ec=ec, dc=dc, half=half: e.matmul(
                                psT[:, half * 512:(half + 1) * 512],
                                WT[:, dc * D + ec * 128:dc * D + (ec + 1) * 128],
                                self.Ssb[:, dc * D + half * 512:dc * D + (half + 1) * 512],
                                start=(dc == 0), stop=(dc == C - 1)),
                                inc=([(tC, 1)] if dc == C - 1 and half == 1 else []))
                    p.on("vector", lambda e, ec=ec: e.tensor_mul(
                        ucol[:], psT[:, 0:1024], Wq2[:, ec * D:(ec + 1) * D]),
                        wait=[W(tC), W(vC)], inc=[(vC, 1)])
                    p.on("vector", lambda e, ec=ec: e.reduce_sum(
                        t2[:, ec:ec + 1], ucol[:], axis=AX.X), wait=[W(vC)], inc=[(vC, 1)])
                p.on("tensor", None, wait=[W(vC)])
                for dc in range(C):
                    for half in range(2):
                        p.on("tensor", lambda e, dc=dc, half=half: e.matmul(
                            psE[0:1, half * 512:(half + 1) * 512], sycol[:, dc:dc + 1],
                            WT[:, dc * D + half * 512:dc * D + (half + 1) * 512],
                            start=(dc == 0), stop=(dc == C - 1)),
                            inc=([(tC, 1)] if dc == C - 1 and half == 1 else []))
                p.on("vector", lambda e: e.tensor_copy(ucol[0:1, :], psE[0:1, :]),
                     wait=[W(tC)], inc=[(vC, 1)])
                for cc2 in range(C):
                    p.on("tensor", lambda e, cc2=cc2: e.transpose(
                        psR[:, 0:1], ucol[0:1, cc2 * 128:(cc2 + 1) * 128], self.idf[0:1, 0:1]),
                        wait=[W(vC)], inc=[(tC, 1)])
                    p.on("vector", lambda e, cc2=cc2: e.tensor_copy(
                        t1[:, cc2:cc2 + 1], psR[:, 0:1]), wait=[W(tC)], inc=[(vC, 1)])
                p.on("vector", lambda e: e.tensor_scalar_mul(t1[:], t1[:], 1.0 / NTOT),
                     wait=[W(vC)], inc=[(vC, 1)])
                p.on("vector", lambda e: e.tensor_scalar_mul(t2[:], t2[:], 1.0 / NTOT),
                     wait=[W(vC)], inc=[(vC, 1)])
                p.on("vector", lambda e: e.tensor_mul(t3[:], t1[:], t1[:]),
                     wait=[W(vC)], inc=[(vC, 1)])
                p.on("vector", lambda e: e.tensor_sub(t2[:], t2[:], t3[:]),
                     wait=[W(vC)], inc=[(vC, 1)])
                p.on("scalar", lambda e: e.activation(t2[:], t2[:], AF.Ln,
                                                      bias=self.epsc[:, 0:1]),
                     wait=[W(vC)], inc=[(sC, 1)])
                p.on("scalar", lambda e: e.activation(t2[:], t2[:], AF.Exp, scale=-0.5),
                     wait=[W(sC)], inc=[(sC, 1)])
                p.on("vector", lambda e, An=An: e.tensor_mul(self.AB[An][:], gcol[:], t2[:]),
                     wait=[W(sC)], inc=[(vC, 1)])
                p.on("vector", lambda e, An=An: e.tensor_scalar_mul(
                    self.AB[An][:], self.AB[An][:], 4.0), wait=[W(vC)], inc=[(vC, 1)])
                p.on("vector", lambda e, An=An: e.tensor_mul(t3[:], t1[:], self.AB[An][:]),
                     wait=[W(vC)], inc=[(vC, 1)])
                p.on("vector", lambda e, Bn=Bn: e.tensor_scalar(
                    self.AB[Bn][:], bcol[:], 4.0, -4.0, ALU.mult, ALU.add), inc=[(vC, 1)])
                p.on("vector", lambda e, Bn=Bn: e.tensor_sub(
                    self.AB[Bn][:], self.AB[Bn][:], t3[:]), wait=[W(vC)], inc=[(vC, 1)])
                if self.debug and br == 0:
                    p.on("sync", lambda e: e.dma_start(out=d["DBGS"][:, 0:8], in_=t1[:]),
                         wait=[W(vC), W(sC)], inc=[(cx.qS, 16)])
                    p.on("sync", lambda e: e.dma_start(out=d["DBGS"][:, 8:16], in_=t2[:]),
                         inc=[(cx.qS, 16)])
                    p.on("vector", None, wait=[(cx.qS, cx.qS.count)])
            if self.debug:
                p.on("gpsimd", lambda e: e.dma_start(out=d["DBGSS"][:, :], in_=self.Ssb[:]),
                     wait=[W(vC), W(sC)], inc=[(cx.qW, 16)])
                for i, n in enumerate(("Aq", "Bq", "Ak", "Bk", "Av", "Bv")):
                    p.on("sync", lambda e, i=i, n=n: e.dma_start(
                        out=d["DBGA"][:, i * 8:(i + 1) * 8], in_=self.AB[n][:]),
                        wait=[W(vC), W(sC)], inc=[(cx.qS, 16)])
            p.on("vector", lambda e: e.tensor_mul(self.nmrc[:], self.rc[:, 0:64],
                                                  self.rc[:, 64:128]),
                 wait=[W(vC)], inc=[(vC, 1)])
            p.on("vector", lambda e: e.tensor_scalar_mul(self.nmrc[:], self.nmrc[:], -1.0),
                 wait=[W(vC)], inc=[(vC, 1)])
            self.p.barrier()

    def phase_p23a(self):
        cx, p, d = self.cx, self.p, self.d
        qL, qS, qW, vC, sC, tC = cx.qL, cx.qS, cx.qW, cx.vC, cx.sC, cx.tC
        with contextlib.ExitStack() as st:
            sb = lambda n, sh, dt=F32: st.enter_context(self.nc.sbuf_tensor(n, sh, dt))
            ps = lambda n, sh, dt=F32: st.enter_context(self.nc.psum_tensor(n, sh, dt))
            Wv = sb("aWv", [128, C * D], BF16)
            xin = [sb(f"axin{i}", [128, 4 * D], F32) for i in range(2)]
            xlnf = [sb(f"axlnf{i}", [128, C * 512], BF16) for i in range(2)]
            x0f = [sb(f"ax0f{i}", [128, C * 512], F32) for i in range(2)]
            xlt = sb("axlt", [128, D], BF16)
            vfm = sb("avfm", [128, C * 512], BF16)
            vtm = sb("avtm", [128, 1024], BF16)
            psTB = ps("psTB", [128, 1024], BF16)
            psTF = ps("psTF", [128, 1024], F32)
            psH = Rot([ps(f"apsH{i}", [128, 512], F32) for i in range(2)])
            psV = ps("apsV", [128, 1024], BF16)
            psVV = ps("apsVV", [128, 512], F32)
            vvacc = sb("avvacc", [128, 512], F32)
            spillS = [p.sem("aspill0"), p.sem("aspill1")]
            p.on("gpsimd", lambda e: cast_w(e, Wv, d["WvT"]), inc=[(qW, 16)])
            wld = qW.count
            xin_free = [None, None]
            xlnf_free = [None, None]
            x0f_free = [None, None]
            vv_read = None
            vfm_free = None
            for s in range(self.nseq):
                for t in range(4):
                    ti = s * 4 + t
                    bi = ti % 2
                    t0 = ti * 512
                    p.on("sync", lambda e, t0=t0, bi=bi: e.dma_start(
                        out=xin[bi][:].rearrange("p (g d) -> p g d", g=4),
                        in_=d["x"][t0:t0 + 512, :].rearrange("(g p) d -> p g d", p=128)),
                        wait=([xin_free[bi]] if xin_free[bi] else []), inc=[(qL, 16)])
                    ld_t = qL.count
                    if xlnf_free[bi]:
                        p.on("vector", None, wait=[xlnf_free[bi], x0f_free[bi]])
                    for g in range(4):
                        xg = xin[bi][:, g * D:(g + 1) * D]
                        idx = ti * 4 + g
                        p.on("vector", lambda e, xg=xg, idx=idx: e.tensor_scalar(
                            xlt[:], xg, self.rc[:, 64 + idx:65 + idx],
                            self.nmrc[:, idx:idx + 1], ALU.mult, ALU.add),
                            wait=[(qL, ld_t), self.w(tC)], inc=[(vC, 1)])
                        p.on("tensor", None, wait=[self.w(vC)])
                        for cc in range(C):
                            p.on("tensor", lambda e, cc=cc: e.transpose(
                                psTB[:, cc * 128:(cc + 1) * 128],
                                xlt[:, cc * 128:(cc + 1) * 128], self.idb[:]),
                                inc=([(tC, 1)] if cc == C - 1 else []))
                        for cc in range(C):
                            p.on("tensor", lambda e, xg=xg, cc=cc: e.transpose(
                                psTF[:, cc * 128:(cc + 1) * 128],
                                xg[:, cc * 128:(cc + 1) * 128], self.idf[:]),
                                inc=([(tC, 1)] if cc == C - 1 else []))
                        tcw = tC.count
                        p.on("vector", lambda e, bi=bi, g=g: e.tensor_copy(
                            xlnf[bi][:].rearrange("p (c t) -> p c t", c=C)[:, :, g * 128:(g + 1) * 128],
                            psTB[:].rearrange("p (c t) -> p c t", c=C)),
                            wait=[(tC, tcw)], inc=[(vC, 1)])
                        p.on("vector", lambda e, bi=bi, g=g: e.tensor_copy(
                            x0f[bi][:].rearrange("p (c t) -> p c t", c=C)[:, :, g * 128:(g + 1) * 128],
                            psTF[:].rearrange("p (c t) -> p c t", c=C)),
                            inc=[(vC, 1)])
                    xready = vC.count
                    p.on("sync", lambda e, bi=bi, t0=t0: fm_store(e, d["XLN"], t0, 512, xlnf[bi]),
                         wait=[(vC, xready)], inc=[(spillS[bi], 16)])
                    p.on("sync", lambda e, bi=bi, t0=t0: fm_store(e, d["X0F"], t0, 512, x0f[bi]),
                         inc=[(spillS[bi], 16)])
                    x0f_free[bi] = (spillS[bi], spillS[bi].count)
                    p.on("tensor", None, wait=[(vC, xready), (qW, wld)]
                         + ([vfm_free] if vfm_free else []))
                    for ec in range(C):
                        kb, pb, wv = psH.acquire()
                        if wv:
                            p.on("tensor", None, wait=wv)
                        for dc in range(C):
                            p.on("tensor", lambda e, ec=ec, dc=dc, pb=pb, bi=bi: e.matmul(
                                pb[:], Wv[:, dc * D + ec * 128:dc * D + (ec + 1) * 128],
                                xlnf[bi][:, dc * 512:(dc + 1) * 512],
                                start=(dc == 0), stop=(dc == C - 1)),
                                inc=([(tC, 1)] if dc == C - 1 else []))
                        p.on("scalar", lambda e, ec=ec, pb=pb: e.activation(
                            vfm[:, ec * 512:(ec + 1) * 512], pb[:], AF.Sigmoid,
                            scale=self.AB["Av"][:, ec:ec + 1],
                            bias=self.AB["Bv"][:, ec:ec + 1]),
                            wait=[(tC, tC.count)], inc=[(sC, 1)])
                        psH.rel[kb] = (sC, sC.count)
                    xlnf_free[bi] = (tC, tC.count)
                    xin_free[bi] = (vC, xready)
                    sig_done = sC.count
                    for g2 in range(4):
                        p.on("tensor", None, wait=[(sC, sig_done)])
                        for cc in range(C):
                            p.on("tensor", lambda e, cc=cc, g2=g2: e.transpose(
                                psV[:, cc * 128:(cc + 1) * 128],
                                vfm[:, cc * 512 + g2 * 128:cc * 512 + (g2 + 1) * 128],
                                self.idb[:]), inc=([(tC, 1)] if cc == C - 1 else []))
                        p.on("scalar", lambda e: e.copy(vtm[:], psV[:]),
                             wait=[(tC, tC.count)], inc=[(sC, 1)])
                        p.on("tensor", None, wait=[(sC, sC.count), self.w(vC)])
                        for h in range(HEADS):
                            p.on("tensor", lambda e, h=h: e.matmul(
                                psVV[(h % 2) * 64:(h % 2) * 64 + 64,
                                     (h // 2) * 64:(h // 2) * 64 + 64],
                                vtm[:, h * HD:(h + 1) * HD], vtm[:, h * HD:(h + 1) * HD],
                                start=True, stop=True,
                                tile_position=(0, (h % 2) * 64)),
                                inc=([(tC, 1)] if h == HEADS - 1 else []))
                        first = (t == 0 and g2 == 0)
                        if first:
                            p.on("vector", lambda e: e.tensor_copy(vvacc[:], psVV[:]),
                                 wait=[self.w(tC)], inc=[(vC, 1)])
                        else:
                            p.on("vector", lambda e: e.tensor_add(vvacc[:], vvacc[:], psVV[:]),
                                 wait=[self.w(tC), self.w(vC)], inc=[(vC, 1)])
                    vfm_free = (tC, tC.count)
                p.on("scalar", lambda e, s=s: e.mul(self.vvs[s][:], vvacc[:], 0.125),
                     wait=[self.w(vC)], inc=[(sC, 1)])
                vv_read = (sC, sC.count)
            self.p.barrier()

    def phase_p23b(self):
        cx, p, d = self.cx, self.p, self.d
        qL, qS, qW, vC, sC, tC = cx.qL, cx.qS, cx.qW, cx.vC, cx.sC, cx.tC
        with contextlib.ExitStack() as st:
            sb = lambda n, sh, dt=F32: st.enter_context(self.nc.sbuf_tensor(n, sh, dt))
            ps = lambda n, sh, dt=F32: st.enter_context(self.nc.psum_tensor(n, sh, dt))
            Wq = sb("bWq", [128, C * D], BF16)
            Wk = sb("bWk", [128, C * D], BF16)
            Wp = sb("bWp", [128, C * D], BF16)
            Msb = sb("bM", [128, C * D], BF16)
            xl_in = [sb(f"bxl{i}", [128, C * 512], BF16) for i in range(2)]
            x0_in = [sb(f"bx0{i}", [128, C * 512], F32) for i in range(2)]
            qf = sb("bqf", [128, C * 512], BF16)
            kf = sb("bkf", [128, C * 512], BF16)
            qkt = sb("bqkt", [128, C * 512], BF16)
            x1t = sb("bx1t", [128, C * 512], F32)
            x1b = sb("bx1b", [128, C * 512], BF16)
            x1sq = sb("bx1sq", [128, C * 512], BF16)
            s1row = sb("bs1row", [1, L], F32)
            s2row = sb("bs2row", [1, L], F32)
            psH = Rot([ps(f"bpsH{i}", [128, 512], F32) for i in range(2)])
            psO = Rot([ps(f"bpsO{i}", [128, 512], F32) for i in range(2)])
            psM = ps("bpsM", [128, 1024], F32)
            psRS = ps("bpsRS", [1, 512], F32)
            psRS2 = ps("bpsRS2", [1, 512], F32)
            binS = [p.sem("bin0"), p.sem("bin1")]
            x1S = p.sem("bx1S")
            rowS = p.sem("browS")
            for w, dst in (("WqT", Wq), ("WkT", Wk), ("WpT", Wp)):
                p.on("gpsimd", lambda e, w=w, dst=dst: cast_w(e, dst, d[w]), inc=[(qW, 16)])
            wld = qW.count
            in_free = [None, None]
            x1b_free = None
            row_free = None
            for s in range(self.nseq):
                for j in range(C):
                    p.on("tensor", None, wait=[self.w(sC), (qW, wld)])
                    for half in range(2):
                        for hh in range(2):
                            h = 2 * j + hh
                            base = (h % 2) * 64
                            p.on("tensor", lambda e, s=s, j=j, half=half, hh=hh, base=base: e.matmul(
                                psM[base:base + 64, half * 512:(half + 1) * 512],
                                self.vvs[s][base:base + 64, j * 64:(j + 1) * 64],
                                Wp[base:base + 64,
                                   j * D + half * 512:j * D + (half + 1) * 512],
                                start=True, stop=True,
                                tile_position=(base, base)),
                                inc=([(tC, 1)] if half == 1 and hh == 1 else []))
                    p.on("scalar", lambda e, j=j: e.copy(Msb[:, j * D:(j + 1) * D], psM[:]),
                         wait=[self.w(tC)], inc=[(sC, 1)])
                mdone = self.w(sC)
                if self.debug and s == 0:
                    p.on("sync", lambda e: e.dma_start(out=d["DBGM"][:, :], in_=Msb[:]),
                         wait=[mdone], inc=[(qS, 16)])
                for t in range(4):
                    ti = s * 4 + t
                    bi = ti % 2
                    t0 = ti * 512
                    p.on("sync", lambda e, bi=bi, t0=t0: fm_load(e, d["XLN"], t0, 512, xl_in[bi]),
                         wait=([in_free[bi]] if in_free[bi] else []), inc=[(binS[bi], 16)])
                    p.on("sync", lambda e, bi=bi, t0=t0: fm_load(e, d["X0F"], t0, 512, x0_in[bi]),
                         inc=[(binS[bi], 16)])
                    ld_t = (binS[bi], binS[bi].count)
                    if getattr(self, "_dbgqk_free", None):
                        p.on("scalar", None, wait=[self._dbgqk_free])
                        self._dbgqk_free = None
                    p.on("tensor", None, wait=[ld_t, (qW, wld), self.w(vC)])
                    for which, Wt, dst, Acol, Bcol in (("q", Wq, qf, "Aq", "Bq"),
                                                       ("k", Wk, kf, "Ak", "Bk")):
                        for ec in range(C):
                            kb, pb, wv = psH.acquire()
                            if wv:
                                p.on("tensor", None, wait=wv)
                            for dc in range(C):
                                p.on("tensor", lambda e, Wt=Wt, ec=ec, dc=dc, pb=pb, bi=bi: e.matmul(
                                    pb[:], Wt[:, dc * D + ec * 128:dc * D + (ec + 1) * 128],
                                    xl_in[bi][:, dc * 512:(dc + 1) * 512],
                                    start=(dc == 0), stop=(dc == C - 1)),
                                    inc=([(tC, 1)] if dc == C - 1 else []))
                            p.on("scalar", lambda e, dst=dst, ec=ec, pb=pb, Acol=Acol, Bcol=Bcol:
                                 e.activation(dst[:, ec * 512:(ec + 1) * 512], pb[:], AF.Sigmoid,
                                              scale=self.AB[Acol][:, ec:ec + 1],
                                              bias=self.AB[Bcol][:, ec:ec + 1]),
                                 wait=[(tC, tC.count)], inc=[(sC, 1)])
                            psH.rel[kb] = (sC, sC.count)
                    if self.debug and s == 0 and t == 0:
                        p.on("sync", lambda e: e.dma_start(out=d["DBGQ"][:, :], in_=qf[:]),
                             wait=[self.w(sC)], inc=[(qS, 16)])
                        p.on("sync", lambda e: e.dma_start(out=d["DBGK"][:, :], in_=kf[:]),
                             inc=[(qS, 16)])
                        self._dbgqk_free = (qS, qS.count)
                    p.on("vector", lambda e: e.tensor_mul(qkt[:], qf[:], kf[:]),
                         wait=[self.w(sC)], inc=[(vC, 1)])
                    qk_ready = vC.count
                    if getattr(self, "_x1t_free", None):
                        p.on("vector", None, wait=self._x1t_free)
                    p.on("tensor", None, wait=[(vC, qk_ready), mdone])
                    for ec in range(C):
                        kb, pb, wv = psO.acquire()
                        if wv:
                            p.on("tensor", None, wait=wv)
                        for dc in range(C):
                            p.on("tensor", lambda e, ec=ec, dc=dc, pb=pb: e.matmul(
                                pb[:], Msb[:, dc * D + ec * 128:dc * D + (ec + 1) * 128],
                                qkt[:, dc * 512:(dc + 1) * 512],
                                start=(dc == 0), stop=(dc == C - 1)),
                                inc=([(tC, 1)] if dc == C - 1 else []))
                        p.on("vector", lambda e, ec=ec, pb=pb, bi=bi: e.scalar_tensor_tensor(
                            x1t[:, ec * 512:(ec + 1) * 512], pb[:],
                            self.bpc[:, ec:ec + 1],
                            x0_in[bi][:, ec * 512:(ec + 1) * 512], ALU.add, ALU.add),
                            wait=[(tC, tC.count)], inc=[(vC, 1)])
                        psO.rel[kb] = (vC, vC.count)
                    x1_ready = vC.count
                    in_free[bi] = (vC, x1_ready)
                    p.on("sync", lambda e, t0=t0: fm_store(e, d["X1"], t0, 512, x1t),
                         wait=[(vC, x1_ready)], inc=[(x1S, 16)])
                    p.on("vector", lambda e: e.tensor_copy(x1b[:], x1t[:]),
                         wait=[self.w(vC)] + ([x1b_free] if x1b_free else []), inc=[(vC, 1)])
                    p.on("scalar", lambda e: e.activation(x1sq[:], x1b[:], AF.Square),
                         wait=[self.w(vC)], inc=[(sC, 1)])
                    p.on("tensor", None, wait=[self.w(sC)] + ([row_free] if row_free else []))
                    for dc in range(C):
                        p.on("tensor", lambda e, dc=dc: e.matmul(
                            psRS[:], self.ones_c[:], x1b[:, dc * 512:(dc + 1) * 512],
                            start=(dc == 0), stop=(dc == C - 1)),
                            inc=([(tC, 1)] if dc == C - 1 else []))
                    for dc in range(C):
                        p.on("tensor", lambda e, dc=dc: e.matmul(
                            psRS2[:], self.ones_c[:], x1sq[:, dc * 512:(dc + 1) * 512],
                            start=(dc == 0), stop=(dc == C - 1)),
                            inc=([(tC, 1)] if dc == C - 1 else []))
                    rdone = tC.count
                    p.on("vector", lambda e, t=t: e.tensor_copy(
                        s1row[:, t * 512:(t + 1) * 512], psRS[:]), wait=[(tC, rdone)])
                    p.on("vector", lambda e, t=t: e.tensor_copy(
                        s2row[:, t * 512:(t + 1) * 512], psRS2[:]), inc=[(vC, 1)])
                    row_free = (vC, vC.count)
                    x1b_free = (vC, vC.count)
                    self._x1t_free = [(x1S, x1S.count), self.w(vC)]
                p.on("sync", lambda e, s=s: e.dma_start(
                    out=d["S1"][s * L:(s + 1) * L].rearrange("(a b) -> a b", a=1),
                    in_=s1row[:, :]), wait=[self.w(vC)], inc=[(rowS, 16)])
                p.on("sync", lambda e, s=s: e.dma_start(
                    out=d["S1"][TOK + s * L:TOK + (s + 1) * L].rearrange("(a b) -> a b", a=1),
                    in_=s2row[:, :]), inc=[(rowS, 16)])
                row_free = (rowS, rowS.count)
            self.p.barrier()

    def phase_fa(self, l):
        cx, p, d = self.cx, self.p, self.d
        qW, vC, sC, tC = cx.qW, cx.vC, cx.sC, cx.tC
        YIN = d["X1"] if l == 0 else d["Y2"]
        YOUT = d["Y1"] if l == 0 else d["Y3"]
        HOUT = d["HLN0"] if l == 0 else d["HLN1"]
        W = self.w
        with contextlib.ExitStack() as st:
            sb = lambda n, sh, dt=F32: st.enter_context(self.nc.sbuf_tensor(n, sh, dt))
            ps = lambda n, sh, dt=F32: st.enter_context(self.nc.psum_tensor(n, sh, dt))
            pre = f"f{l}"
            Wqkv = sb(pre + "Wqkv", [128, C * 1536], BF16)
            Wo = sb(pre + "Wo", [128, 4 * D], BF16)
            ones_rf = sb(pre + "onesrf", [1, 128], F32)
            tabs = {n: sb(pre + n, [128, 512], F32) for n in ("CSQ", "SSQ", "CSK", "SSK")}
            qscol = sb(pre + "qscol", [128, 1], F32)
            kscol = sb(pre + "kscol", [128, 1], F32)
            lgc = sb(pre + "lgc", [128, 8], F32)
            lbc = sb(pre + "lbc", [128, 8], F32)
            yin = [sb(pre + f"yin{i}", [128, C * 512], F32) for i in range(2)] if l == 0 else None
            ysb = [sb(pre + f"ysb{i}", [128, C * 513], F32) for i in range(2)]
            ybb = [sb(pre + f"ybb{i}", [128, C * 513], BF16) for i in range(2)]
            ybnd = sb(pre + "ybnd", [128, 4], F32)
            qraw = sb(pre + "qraw", [128, 4 * 512], F32)
            kraw = sb(pre + "kraw", [128, 4 * 512], F32)
            l2sq = [sb(pre + f"l2sq{i}", [128, 512], BF16) for i in range(2)]
            l2lr = [sb(pre + f"l2lr{i}", [2, 512], F32) for i in range(2)]
            l2rh = [sb(pre + f"l2rh{i}", [2, 512], BF16) for i in range(2)]
            l2rl = [sb(pre + f"l2rl{i}", [2, 512], BF16) for i in range(2)]
            rhi32 = sb(pre + "rhi32", [2, 512], F32)
            s1t = sb(pre + "s1t", [128, 512], F32)
            swp = sb(pre + "swp", [128, 512], F32)
            rt1 = sb(pre + "rt1", [128, 512], F32)
            qpair = sb(pre + "qpair", [128, 4 * 512], BF16)
            kring = sb(pre + "kring", [128, 4 * 1024], BF16)
            vring = [sb(pre + f"vr{i}", [128, 512], BF16) for i in range(9)]
            nmc = sb(pre + "nmc", [128, 8], F32)
            nmb = sb(pre + "nmb", [128, 8], BF16)
            nmf = sb(pre + "nmf", [128, 8], F32)
            nmrow0 = sb(pre + "nmrow0", [1, 512], BF16)
            onesm = sb(pre + "onesm", [128, 64], BF16)
            nmrow1 = sb(pre + "nmrow1", [1, 512], BF16)
            rdrow = sb(pre + "rdrow", [1, 512], F32)
            rdhi = sb(pre + "rdhi", [1, 512], BF16)
            rdhi32 = sb(pre + "rdhi32", [1, 512], F32)
            rdlo = sb(pre + "rdlo", [1, 512], BF16)
            rdsb = sb(pre + "rdsb", [128, 512], F32)
            attnT = [[sb(pre + f"attnT{i}{k}", [128, 512], BF16) for k in range(2)]
                     for i in range(2)]
            osb = sb(pre + "osb", [128, 4 * 512], BF16)
            hsq2 = [sb(pre + f"hsq{i}", [128, 512], BF16) for i in range(2)]
            rowa = sb(pre + "rowa", [1, 512], F32)
            rowb2 = sb(pre + "rowb2", [1, 512], F32)
            m3row = rowa
            v3row = rowb2
            psH = Rot([ps(pre + f"psH{i}", [128, 512], F32) for i in range(2)])
            psSA = [ps(pre + "psSAa", [128, 1024], F32), ps(pre + "psSAb", [128, 1024], F32)]
            psSB = [ps(pre + "psSBa", [128, 512], F32), ps(pre + "psSBb", [128, 512], F32)]
            psRR = psSA[1][0:1, 512:1024]
            psL2regs = [psSA[0][0:2, 512:1024], psSA[1][0:2, 512:1024]]

            inS = [p.sem(pre + "in0"), p.sem(pre + "in1")]
            outSp = [p.sem(pre + "out0"), p.sem(pre + "out1")]
            hSp = [p.sem(pre + "hS0"), p.sem(pre + "hS1")]
            faC = p.sem(pre + "faC")
            rowLS = p.sem(pre + "rowLS")
            self._tab_free = None

            p.on("gpsimd", lambda e: cast_w(e, Wqkv, d[f"wqkv{l}"]), inc=[(qW, 16)])
            p.on("gpsimd", lambda e: cast_w(e, Wo, d[f"wo{l}"]), inc=[(qW, 16)])
            for dst, src in ((ones_rf, "onesrow"), (qscol, f"qs{l}"), (kscol, f"ks{l}"),
                             (lgc, f"lng{l}"), (lbc, f"lnb{l}")):
                p.on("sync", lambda e, dst=dst, src=src: e.dma_start(
                    out=dst[:], in_=d[src][:, :]), inc=[(faC, 16)])
            tabS = p.sem(pre + "tabS")
            wld = (qW, qW.count)
            cl = (faC, faC.count)

            in_free = [None, None]
            p.on("vector", lambda e: e.memset(onesm[:], 1.0), inc=[(cx.vC, 1)])
            for bb2 in range(2):
                for cc in range(4):
                    p.on("vector", lambda e, bb2=bb2, cc=cc: e.memset(
                        ysb[bb2][:, cc * 513:cc * 513 + 1], 0.0), inc=[(vC, 1)])
            for s in range(self.nseq):
                for t in range(4):
                    ti = s * 4 + t
                    bi = ti % 2
                    cur, prv = ti % 2, 1 - ti % 2
                    t0 = ti * 512
                    pos0 = t * 512

                    p.scope(f"yld_{ti}" if ti in (8, 9) else None)
                    if l == 0:
                        p.on("sync", lambda e, bi=bi, t0=t0: fm_load(e, YIN, t0, 512, yin[bi]),
                             wait=([in_free[bi]] if in_free[bi] else []), inc=[(inS[bi], 16)])
                        ldw = (inS[bi], inS[bi].count)
                        p.on("sync", lambda e, t0=t0: e.dma_start(
                            out=rowa[:], in_=d["S1"][t0:t0 + 512].rearrange("(a b) -> a b", a=1)),
                            wait=[W(tC), W(vC)], inc=[(rowLS, 16)])
                        p.on("sync", lambda e, t0=t0: e.dma_start(
                            out=rowb2[:], in_=d["S1"][TOK + t0:TOK + t0 + 512].rearrange("(a b) -> a b", a=1)),
                            inc=[(rowLS, 16)])
                        lw = (rowLS, rowLS.count)
                        p.on("vector", lambda e: e.tensor_scalar_mul(rowa[:], rowa[:], 1.0 / D),
                             wait=[lw, W(sC), W(vC)], inc=[(vC, 1)])
                        p.on("vector", lambda e: e.tensor_scalar_mul(rowb2[:], rowb2[:], 1.0 / D),
                             wait=[W(vC)], inc=[(vC, 1)])
                        p.on("vector", lambda e: e.tensor_mul(swp[0:1, :], rowa[:], rowa[:]),
                             wait=[W(vC)], inc=[(vC, 1)])
                        p.on("vector", lambda e: e.tensor_sub(rowb2[:], rowb2[:], swp[0:1, :]),
                             wait=[W(vC)], inc=[(vC, 1)])
                        p.on("scalar", lambda e: e.activation(rowb2[:], rowb2[:], AF.Ln,
                                                              bias=self.epsc[0:1, 0:1]),
                             wait=[W(vC)], inc=[(sC, 1)])
                        p.on("scalar", lambda e: e.activation(rowb2[:], rowb2[:], AF.Exp,
                                                              scale=-0.5),
                             wait=[W(sC)], inc=[(sC, 1)])
                        p.on("vector", lambda e: e.tensor_mul(rowa[:], rowa[:], rowb2[:]),
                             wait=[W(sC)], inc=[(vC, 1)])
                        p.on("vector", lambda e: e.tensor_scalar_mul(rowa[:], rowa[:], -1.0),
                             wait=[W(vC)], inc=[(vC, 1)])
                        rows_rdy = W(vC)
                        kb0, pb0, wv0 = psH.acquire()
                        kb1, pb1, wv1 = psH.acquire()
                        p.on("tensor", None, wait=wv0 + wv1 + [rows_rdy, cl])
                        p.on("tensor", lambda e, pb0=pb0: e.matmul(
                            pb0[:], ones_rf[:, :], rowb2[0:1, :],
                            start=True, stop=True))
                        p.on("tensor", lambda e, pb1=pb1: e.matmul(
                            pb1[:], ones_rf[:, :], rowa[0:1, :],
                            start=True, stop=True), inc=[(tC, 1)])
                        bdone = W(tC)
                        if self._ysb_free[cur]:
                            p.on("vector", None, wait=self._ysb_free[cur])
                        for cc in range(C):
                            p.on("vector", lambda e, cc=cc, bi=bi, pb0=pb0: e.tensor_mul(
                                s1t[:], yin[bi][:, cc * 512:(cc + 1) * 512], pb0[:]),
                                wait=[bdone, ldw, W(vC)], inc=[(vC, 1)])
                            p.on("vector", lambda e, cc=cc, cur=cur, pb1=pb1: e.tensor_add(
                                ysb[cur][:, cc * 513 + 1:cc * 513 + 513], s1t[:], pb1[:]),
                                wait=[W(vC)], inc=[(vC, 1)])
                        psH.rel[kb0] = (vC, vC.count)
                        psH.rel[kb1] = (vC, vC.count)
                        in_free[bi] = (vC, vC.count)
                        yload = W(vC)
                    else:
                        if self._ysb_free[cur]:
                            p.on("vector", None, wait=self._ysb_free[cur])
                            p.on("sync", lambda e, cur=cur, t0=t0: e.dma_start(
                                out=ysb[cur][:].rearrange("p (c t) -> p c t", c=C)[:, :, 1:513],
                                in_=YIN[:, t0:t0 + 512].rearrange("(c p) t -> p c t", p=128)),
                                wait=self._ysb_free[cur], inc=[(inS[bi], 16)])
                        else:
                            p.on("sync", lambda e, cur=cur, t0=t0: e.dma_start(
                                out=ysb[cur][:].rearrange("p (c t) -> p c t", c=C)[:, :, 1:513],
                                in_=YIN[:, t0:t0 + 512].rearrange("(c p) t -> p c t", p=128)),
                                inc=[(inS[bi], 16)])
                        yload = (inS[bi], inS[bi].count)
                    if t == 0:
                        for cc in range(4, C):
                            p.on("vector", lambda e, cc=cc, cur=cur: e.memset(
                                ysb[cur][:, cc * 513:cc * 513 + 1], 0.0),
                                wait=[yload] + ([W(vC)] if cc == 4 else []), inc=[(vC, 1)])
                    else:
                        for cc in range(4, C):
                            p.on("vector", lambda e, cc=cc, cur=cur: e.tensor_copy(
                                ysb[cur][:, cc * 513:cc * 513 + 1], ybnd[:, cc - 4:cc - 3]),
                                wait=[yload] + ([W(vC)] if cc == 4 else []), inc=[(vC, 1)])
                    yrdy = W(vC)
                    for cc in range(C):
                        eng2 = "vector" if cc % 2 == 0 else "scalar"
                        sm = vC if cc % 2 == 0 else sC
                        p.on(eng2, lambda e, cc=cc, cur=cur: (
                            e.tensor_copy if cc % 2 == 0 else e.copy)(
                            ybb[cur][:, cc * 513:(cc + 1) * 513],
                            ysb[cur][:, cc * 513:(cc + 1) * 513]),
                            wait=[yrdy] + ([W(tC)] if cc < 2 else []), inc=[(sm, 1)])
                    ybrdy = [(vC, vC.count), (sC, sC.count)]
                    for n in ("CSQ", "SSQ", "CSK", "SSK"):
                        p.on("sync", lambda e, n=n, pos0=pos0: e.dma_start(
                            out=tabs[n][:], in_=d[n][:, pos0:pos0 + 512]),
                            wait=([self._tab_free] if self._tab_free else []),
                            inc=[(tabS, 16)])
                    tab_rdy = (tabS, tabS.count)

                    def ysl(cc, off, width, cur):
                        shf = 1 if cc >= 4 else 1
                        base = cc * 513 + (0 if cc >= 4 else 1)
                        return ybb[cur][:, base + off:base + off + width]

                    p.scope(f"qkv_{ti}" if ti in (8, 9) else None)
                    p.on("tensor", None, wait=ybrdy + [wld])
                    for which, dst in (("q", qraw), ("k", kraw)):
                        for ecl in range(4):
                            ec = ecl if which == "q" else 4 + ecl
                            kb, pb, wv = psH.acquire()
                            if wv:
                                p.on("tensor", None, wait=wv)
                            for dc in range(C):
                                p.on("tensor", lambda e, ec=ec, dc=dc, pb=pb, cur=cur: e.matmul(
                                    pb[:], Wqkv[:, dc * 1536 + ec * 128:dc * 1536 + (ec + 1) * 128],
                                    ysl(dc, 0, 512, cur),
                                    start=(dc == 0), stop=(dc == C - 1)),
                                    inc=([(tC, 1)] if dc == C - 1 else []))
                            p.on("scalar", lambda e, dst=dst, ecl=ecl, pb=pb: e.copy(
                                dst[:, ecl * 512:(ecl + 1) * 512], pb[:]),
                                wait=[W(tC), W(vC), W(hSp[0]), W(hSp[1])], inc=[(sC, 1)])
                            psH.rel[kb] = (sC, sC.count)
                    p.scope(f"vmm_{ti}" if ti in (8, 9) else None)
                    for g2 in range(4):
                        slot = (ti * 4 + g2) % 9
                        kb, pb, wv = psH.acquire()
                        if wv:
                            p.on("tensor", None, wait=wv)
                        for dc in range(C):
                            p.on("tensor", lambda e, dc=dc, g2=g2, pb=pb, cur=cur: e.matmul(
                                pb[:], ysl(dc, g2 * 128, 128, cur),
                                Wqkv[:, dc * 1536 + 1024:dc * 1536 + 1536],
                                start=(dc == 0), stop=(dc == C - 1)),
                                inc=([(tC, 1)] if dc == C - 1 else []))
                        p.on("scalar", lambda e, slot=slot, pb=pb: e.copy(
                            vring[slot][:], pb[:]), wait=[W(tC)], inc=[(sC, 1)])
                        psH.rel[kb] = (sC, sC.count)
                    p.scope(f"krs_{ti}" if ti in (8, 9) else None)
                    if t > 0:
                        for cc in range(4):
                            p.on("vector", lambda e, cc=cc: e.tensor_copy(
                                kring[:, cc * 1024:cc * 1024 + 512],
                                kring[:, cc * 1024 + 512:(cc + 1) * 1024]),
                                wait=([W(tC), W(vC)] if cc == 0 else [W(vC)]), inc=[(vC, 1)])
                    else:
                        p.on("vector", None, wait=[W(tC)])
                    p.scope(f"l2n_{ti}" if ti in (8, 9) else None)
                    qkevw = (sC, sC.count)
                    selw = [None, None]
                    psfree = [None, None]
                    minw = [None, None]
                    bcfree = [None, None]
                    for idx, (which, cc) in enumerate(
                            [(w2, c2) for w2 in ("q", "k") for c2 in range(4)]):
                        par = idx % 2
                        src = qraw if which == "q" else kraw
                        scol = qscol if which == "q" else kscol
                        CS = tabs["CSQ" if which == "q" else "CSK"]
                        SS = tabs["SSQ" if which == "q" else "SSK"]
                        psL2p = psL2regs[par]
                        p.on("scalar", lambda e, src=src, cc=cc, par=par: e.activation(
                            l2sq[par][:], src[:, cc * 512:(cc + 1) * 512], AF.Square),
                            wait=[qkevw] + ([selw[par]] if selw[par] else []), inc=[(sC, 1)])
                        sqw = (sC, sC.count)
                        p.on("tensor", lambda e, par=par, psL2p=psL2p: e.matmul(
                            psL2p, self.sel_b[:], l2sq[par][:], start=True, stop=True),
                            wait=[sqw] + ([psfree[par]] if psfree[par] else []),
                            inc=[(tC, 1)])
                        mmw = (tC, tC.count)
                        selw[par] = mmw
                        p.on("scalar", lambda e, par=par, psL2p=psL2p: e.activation(
                            l2lr[par][:], psL2p, AF.Ln),
                            wait=[mmw] + ([minw[par]] if minw[par] else []), inc=[(sC, 1)])
                        psfree[par] = (sC, sC.count)
                        p.on("scalar", lambda e, par=par: e.activation(
                            l2lr[par][:], l2lr[par][:], AF.Exp, scale=-0.5),
                            wait=[W(sC)], inc=[(sC, 1)])
                        lnw = (sC, sC.count)
                        p.on("vector", lambda e, par=par: e.tensor_scalar_min(
                            l2lr[par][:], l2lr[par][:], 1.0e12), wait=[lnw], inc=[(vC, 1)])
                        p.on("vector", lambda e, par=par: e.tensor_copy(
                            l2rh[par][:], l2lr[par][:]),
                            wait=[W(vC)] + ([bcfree[par]] if bcfree[par] else []),
                            inc=[(vC, 1)])
                        p.on("vector", lambda e, par=par: e.tensor_copy(
                            rhi32[:], l2rh[par][:]), wait=[W(vC)], inc=[(vC, 1)])
                        p.on("vector", lambda e, par=par: e.tensor_sub(
                            l2lr[par][:], l2lr[par][:], rhi32[:]), wait=[W(vC)], inc=[(vC, 1)])
                        p.on("vector", lambda e, par=par: e.tensor_copy(
                            l2rl[par][:], l2lr[par][:]), wait=[W(vC)], inc=[(vC, 1)])
                        rlw = (vC, vC.count)
                        minw[par] = rlw
                        kb, pb, wv = psH.acquire()
                        p.on("tensor", None, wait=wv + [rlw])
                        p.on("tensor", lambda e, pb=pb, par=par: e.matmul(
                            pb[:], self.sel2_b[:], l2rh[par][:], start=True, stop=False))
                        p.on("tensor", lambda e, pb=pb, par=par: e.matmul(
                            pb[:], self.sel2_b[:], l2rl[par][:], start=False, stop=True),
                            inc=[(tC, 1)])
                        bcw = (tC, tC.count)
                        bcfree[par] = bcw
                        p.on("vector", lambda e, src=src, cc=cc, pb=pb, scol=scol:
                             e.scalar_tensor_tensor(
                                 s1t[:], src[:, cc * 512:(cc + 1) * 512], scol[:, 0:1],
                                 pb[:], ALU.mult, ALU.mult),
                             wait=[bcw, cl, W(vC)], inc=[(vC, 1)])
                        psH.rel[kb] = (vC, vC.count)
                        for bb in range(4):
                            srcr = (bb // 2) * 64 + (1 - bb % 2) * 32
                            dstr = (bb // 2) * 64 + (bb % 2) * 32
                            p.on("vector", lambda e, srcr=srcr, dstr=dstr: e.tensor_copy(
                                swp[dstr:dstr + 32, :], s1t[srcr:srcr + 32, :]),
                                wait=[W(vC)], inc=[(vC, 1)])
                        p.on("vector", lambda e, CS=CS: e.tensor_mul(
                            rt1[:], s1t[:], CS[:, 0:512]),
                            wait=[W(vC), tab_rdy], inc=[(vC, 1)])
                        p.on("vector", lambda e, SS=SS: e.tensor_mul(
                            swp[:], swp[:], SS[:, 0:512]),
                            wait=[W(vC)], inc=[(vC, 1)])
                        if which == "q":
                            p.on("vector", lambda e, cc=cc: e.tensor_add(
                                qpair[:, cc * 512:(cc + 1) * 512], rt1[:], swp[:]),
                                wait=[W(vC)], inc=[(vC, 1)])
                        else:
                            p.on("vector", lambda e, cc=cc: e.tensor_add(
                                kring[:, cc * 1024 + 512:(cc + 1) * 1024], rt1[:], swp[:]),
                                wait=[W(vC)], inc=[(vC, 1)])
                    qk_rdy = (vC, vC.count)
                    self._tab_free = qk_rdy
                    p.scope(f"att_{ti}" if ti in (8, 9) else None)
                    for j in range(4):
                        SAa, SAb = psSA
                        for g in range(4):
                            if t == 0:
                                k0, wdt = 512, 128 * (g + 1)
                            else:
                                k0, wdt = 128 * g, 640
                            gw = [qk_rdy]
                            if g == 0:
                                gw += ([self._avfree] if self._avfree else [])
                            else:
                                gw += [(vC, vC.count)]
                            p.on("tensor", None, wait=gw)
                            for hh, (SA, tp) in enumerate(((SAa, (0, 0)), (SAb, (64, 0)))):
                                rb = hh * 64

                                def simmm(e, off, wpart, st, sp, SA=None, rb=None, j=None,
                                          g=None, k0=None, tp=None):
                                    return e.matmul(
                                        SA[:, off:off + wpart],
                                        qpair[rb:rb + 64, j * 512 + g * 128:j * 512 + (g + 1) * 128],
                                        kring[rb:rb + 64, j * 1024 + k0 + off:j * 1024 + k0 + off + wpart],
                                        start=st, stop=sp, tile_position=tp)
                                kw = dict(SA=SA, rb=rb, j=j, g=g, k0=k0, tp=tp)
                                lastinc = [(tC, 1)] if hh == 1 else []
                                if t == 0:
                                    if g > 0:
                                        p.on("tensor", lambda e, kw=kw, g=g: simmm(
                                            e, 0, g * 128, True, True, **kw))
                                    p.on("tensor", lambda e, kw=kw, g=g: simmm(
                                        e, g * 128, 128, True, False, **kw))
                                    p.on("tensor", lambda e, SA=SA, g=g: e.matmul(
                                        SA[:, g * 128:(g + 1) * 128], self.idb[:],
                                        self.mA["cur"][:, :], start=False, stop=True),
                                        inc=lastinc)
                                else:
                                    p.on("tensor", lambda e, kw=kw: simmm(
                                        e, 0, 128, True, False, **kw))
                                    p.on("tensor", lambda e, SA=SA: e.matmul(
                                        SA[:, 0:128], self.idb[:], self.mA["prev"][:, :],
                                        start=False, stop=True))
                                    p.on("tensor", lambda e, kw=kw: simmm(
                                        e, 128, 384, True, True, **kw))
                                    p.on("tensor", lambda e, kw=kw: simmm(
                                        e, 512, 128, True, False, **kw))
                                    p.on("tensor", lambda e, SA=SA: e.matmul(
                                        SA[:, 512:640], self.idb[:], self.mA["cur"][:, :],
                                        start=False, stop=True), inc=lastinc)
                            simw = W(tC)
                            for hh, SA in enumerate(psSA):
                                col = g if hh == 0 else 4 + g
                                mw = [simw] if hh == 0 else []
                                if g == 0 and hh == 0:
                                    mw += [self._nmfree] if self._nmfree else []
                                p.on("vector", lambda e, SA=SA, wdt=wdt, col=col: e.reduce_max(
                                    nmc[:, col:col + 1], SA[:, 0:wdt], axis=AX.X, negate=True),
                                    wait=mw, inc=[(vC, 1)])
                                p.on("vector", lambda e, col=col: e.tensor_copy(
                                    nmb[:, col:col + 1], nmc[:, col:col + 1]),
                                    wait=[W(vC)], inc=[(vC, 1)])
                                p.on("vector", lambda e, col=col: e.tensor_copy(
                                    nmf[:, col:col + 1], nmb[:, col:col + 1]),
                                    wait=[W(vC)], inc=[(vC, 1)])
                        maxdone = (vC, vC.count)
                        for hh in range(2):
                            cb = 0 if hh == 0 else 4
                            kb, pb, wv = psH.acquire()
                            p.on("tensor", None, wait=wv + [maxdone])
                            for g in range(4):
                                p.on("tensor", lambda e, cb=cb, g=g, pb=pb: e.transpose(
                                    pb[0:1, g * 128:(g + 1) * 128],
                                    nmf[:, cb + g:cb + g + 1], self.idf[:]),
                                    inc=([(tC, 1)] if g == 3 else []))
                            nr = nmrow0 if hh == 0 else nmrow1
                            p.on("vector", lambda e, nr=nr, pb=pb: e.tensor_copy(
                                nr[:], pb[0:1, 0:512]), wait=[W(tC)], inc=[(vC, 1)])
                            psH.rel[kb] = (vC, vC.count)
                        self._nmfree = (tC, tC.count)
                        rowsdone = (vC, vC.count)
                        kcs = [4, 0, 1, 2, 3, 5, 6, 7] if t > 0 else [4, 5, 6, 7]
                        for ki, kc in enumerate(kcs):
                            if kc < 4:
                                q0, qw = 0, 128 * (kc + 1)
                            else:
                                q0, qw = 128 * (kc - 4), 512 - 128 * (kc - 4)
                            if kc >= 4:
                                kslot = 4 * t + (kc - 4)
                            else:
                                kslot = 4 * (t - 1) + kc
                            vs = vring[(s * 16 + kslot) % 9]
                            first_av = ki == 0
                            is_last = ki == len(kcs) - 1
                            par = ki % 2
                            simc = [None, None]
                            for hh in range(2):
                                rb = hh * 64
                                base = 0 if par == 0 else 512
                                SBps = psSB[hh] if par == 0 else psSA[hh]
                                nr = nmrow0 if hh == 0 else nmrow1
                                bw = []
                                if self._expw2[hh][par]:
                                    bw += [self._expw2[hh][par]]
                                if ki == 0:
                                    bw += [rowsdone]
                                p.on("tensor", None, wait=bw)
                                p.on("tensor", lambda e, j=j, rb=rb, kc=kc, q0=q0, qw=qw,
                                     SBps=SBps, base=base: e.matmul(
                                    SBps[:, base + q0:base + q0 + qw],
                                    kring[rb:rb + 64, j * 1024 + kc * 128:j * 1024 + (kc + 1) * 128],
                                    qpair[rb:rb + 64, j * 512 + q0:j * 512 + q0 + qw],
                                    start=True, stop=False, tile_position=(rb, 0)))
                                if kc >= 4:
                                    p.on("tensor", lambda e, SBps=SBps, q0=q0, base=base: e.matmul(
                                        SBps[:, base + q0:base + q0 + 128], self.idb[:],
                                        self.mB["cur"][:, :], start=False, stop=False))
                                else:
                                    p.on("tensor", lambda e, SBps=SBps, kc=kc, base=base: e.matmul(
                                        SBps[:, base + kc * 128:base + (kc + 1) * 128], self.idb[:],
                                        self.mB["prev"][:, :], start=False, stop=False))
                                p.on("tensor", lambda e, SBps=SBps, q0=q0, qw=qw, nr=nr,
                                     base=base: e.matmul(
                                    SBps[:, base + q0:base + q0 + qw], self.ones_r[:, :],
                                    nr[0:1, q0:q0 + qw], start=False, stop=True),
                                    inc=[(tC, 1)])
                                simc[hh] = (tC, tC.count)
                            for hh in range(2):
                                att = attnT[hh][par]
                                base = 0 if par == 0 else 512
                                SBps = psSB[hh] if par == 0 else psSA[hh]
                                aw = [simc[hh]]
                                if self._attfree[hh][par]:
                                    aw += [self._attfree[hh][par]]
                                p.on("scalar", lambda e, SBps=SBps, att=att, q0=q0, qw=qw,
                                     base=base: e.activation(
                                    att[:, q0:q0 + qw], SBps[:, base + q0:base + q0 + qw],
                                    AF.Exp),
                                     wait=aw, inc=[(sC, 1)])
                                self._expw2[hh][par] = (sC, sC.count)
                            for hh in range(2):
                                att = attnT[hh][par]
                                h = 2 * j + hh
                                AVps = psSA[hh]
                                p.on("tensor", None, wait=[self._expw2[hh][par]])
                                p.on("tensor", lambda e, AVps=AVps, att=att, vs=vs, h=h,
                                     q0=q0, qw=qw, fa=first_av, sp=is_last: e.matmul(
                                    AVps[0:64, q0:q0 + qw], vs[:, h * 64:(h + 1) * 64],
                                    att[:, q0:q0 + qw], start=fa, stop=sp,
                                    tile_position=(0, 0)))
                                p.on("tensor", lambda e, AVps=AVps, att=att,
                                     q0=q0, qw=qw, fa=first_av, sp=is_last: e.matmul(
                                    AVps[64:128, q0:q0 + qw], onesm[:, :],
                                    att[:, q0:q0 + qw], start=fa, stop=sp,
                                    tile_position=(0, 64), skip_group_check=True),
                                    inc=[(tC, 1)])
                                self._attfree[hh][par] = (tC, tC.count)
                        avw = [self._attfree[0][(len(kcs) - 1) % 2],
                               self._attfree[1][(len(kcs) - 1) % 2]]
                        for hh in range(2):
                            rb = hh * 64
                            AVps = psSA[hh]
                            p.on("vector", lambda e, AVps=AVps: e.reciprocal(
                                rdrow[:], AVps[64:65, 0:512]),
                                wait=[avw[hh], W(tC), W(sC), W(vC)], inc=[(vC, 1)])
                            p.on("vector", lambda e: e.tensor_copy(rdhi[:], rdrow[:]),
                                 wait=[W(vC)] + ([W(tC)] if hh == 1 else []), inc=[(vC, 1)])
                            p.on("vector", lambda e: e.tensor_copy(rdhi32[:], rdhi[:]),
                                 wait=[W(vC)], inc=[(vC, 1)])
                            p.on("vector", lambda e: e.tensor_sub(rdrow[:], rdrow[:], rdhi32[:]),
                                 wait=[W(vC)], inc=[(vC, 1)])
                            p.on("vector", lambda e: e.tensor_copy(rdlo[:], rdrow[:]),
                                 wait=[W(vC)], inc=[(vC, 1)])
                            rowsw = W(vC)
                            kb, pb, wv = psH.acquire()
                            p.on("tensor", None, wait=wv + [rowsw])
                            p.on("tensor", lambda e, pb=pb: e.matmul(
                                pb[:], self.ones_r[:, :], rdhi[:, :], start=True, stop=False))
                            p.on("tensor", lambda e, pb=pb: e.matmul(
                                pb[:], self.ones_r[:, :], rdlo[:, :], start=False, stop=True),
                                inc=[(tC, 1)])
                            p.on("scalar", lambda e, pb=pb: e.copy(rdsb[:], pb[:]),
                                 wait=[W(tC)] + ([W(vC)] if hh == 1 else []), inc=[(sC, 1)])
                            psH.rel[kb] = (sC, sC.count)
                            p.on("vector", lambda e, AVps=AVps, j=j, rb=rb: e.tensor_mul(
                                osb[rb:rb + 64, j * 512:(j + 1) * 512],
                                AVps[0:64, 0:512], rdsb[0:64, 0:512]),
                                wait=[W(sC), W(vC)], inc=[(vC, 1)])
                        self._avfree = (vC, vC.count)
                    attn_done = (vC, vC.count)
                    p.scope(f"owo_{ti}" if ti in (8, 9) else None)
                    for cc in range(4, C):
                        p.on("vector", lambda e, cc=cc, cur=cur: e.tensor_copy(
                            ybnd[:, cc - 4:cc - 3], ysb[cur][:, cc * 513 + 512:cc * 513 + 513]),
                            wait=[W(vC)], inc=[(vC, 1)])
                    bnd_saved = (vC, vC.count)
                    p.on("tensor", None, wait=[attn_done])
                    for ec in range(C):
                        kb, pb, wv = psH.acquire()
                        if wv:
                            p.on("tensor", None, wait=wv)
                        for dc in range(4):
                            p.on("tensor", lambda e, ec=ec, dc=dc, pb=pb: e.matmul(
                                pb[:], Wo[:, dc * D + ec * 128:dc * D + (ec + 1) * 128],
                                osb[:, dc * 512:(dc + 1) * 512],
                                start=(dc == 0), stop=(dc == 3)),
                                inc=([(tC, 1)] if dc == 3 else []))
                        p.on("vector", lambda e, ec=ec, pb=pb, cur=cur: e.tensor_add(
                            ysb[cur][:, ec * 513 + 1:ec * 513 + 513],
                            ysb[cur][:, ec * 513 + 1:ec * 513 + 513], pb[:]),
                            wait=[W(tC), bnd_saved, W(vC)], inc=[(vC, 1)])
                        psH.rel[kb] = (vC, vC.count)
                    yp_done = (vC, vC.count)
                    if t == 0:
                        for cc in range(4, C):
                            p.on("vector", lambda e, cc=cc, cur=cur: e.memset(
                                ysb[cur][:, cc * 513:cc * 513 + 1], 0.0),
                                wait=[W(vC)], inc=[(vC, 1)])
                    else:
                        for cc in range(4, C):
                            p.on("vector", lambda e, cc=cc, cur=cur, prv=prv: e.tensor_copy(
                                ysb[cur][:, cc * 513:cc * 513 + 1],
                                ysb[prv][:, cc * 513 + 512:cc * 513 + 513]),
                                wait=[W(vC)], inc=[(vC, 1)])
                        if self._ysb_free[prv]:
                            self._ysb_free[prv] = self._ysb_free[prv] + [(vC, vC.count)]
                    for cc in range(C):
                        eng2 = "vector" if cc % 2 == 0 else "scalar"
                        sm = vC if cc % 2 == 0 else sC
                        p.on(eng2, lambda e, cc=cc, cur=cur: (
                            e.tensor_copy if cc % 2 == 0 else e.copy)(
                            ybb[cur][:, cc * 513:(cc + 1) * 513],
                            ysb[cur][:, cc * 513:(cc + 1) * 513]),
                            wait=[W(vC)] + ([W(sC)] if cc < 2 else []), inc=[(sm, 1)])
                    yb2 = [(vC, vC.count), (sC, sC.count)]
                    p.on("sync", lambda e, cur=cur, t0=t0: e.dma_start(
                        out=YOUT[:, t0:t0 + 512].rearrange("(c p) t -> p c t", p=128),
                        in_=ysb[cur][:].rearrange("p (c t) -> p c t", c=C)[:, :, 1:513]),
                        wait=[yp_done], inc=[(outSp[cur], 16)])
                    p.scope(f"hln_{ti}" if ti in (8, 9) else None)
                    def ypsl(cc, off, width, cur):
                        base = cc * 513 + (0 if cc >= 4 else 1)
                        return ybb[cur][:, base + off:base + off + width]
                    p.on("tensor", None, wait=yb2)
                    for dc in range(C):
                        p.on("tensor", lambda e, dc=dc, cur=cur: e.matmul(
                            psRR, self.ones_c[:], ypsl(dc, 0, 512, cur),
                            start=(dc == 0), stop=(dc == C - 1)),
                            inc=([(tC, 1)] if dc == C - 1 else []))
                    p.on("vector", lambda e: e.tensor_copy(m3row[:], psRR[0:1, 0:512]),
                         wait=[W(tC)], inc=[(vC, 1)])
                    sqmm = [None, None]
                    for dc in range(C):
                        hp = dc % 2
                        p.on("scalar", lambda e, dc=dc, cur=cur, hp=hp: e.activation(
                            hsq2[hp][:], ypsl(dc, 0, 512, cur), AF.Square),
                            wait=([W(vC), W(tC)] if dc == 0 else [])
                            + ([sqmm[hp]] if sqmm[hp] else []), inc=[(sC, 1)])
                        sqdone = (sC, sC.count)
                        p.on("tensor", lambda e, dc=dc, hp=hp: e.matmul(
                            psRR, self.ones_c[:],
                            hsq2[hp][:], start=(dc == 0), stop=(dc == C - 1)),
                            wait=[sqdone], inc=[(tC, 1)])
                        sqmm[hp] = (tC, tC.count)
                    p.on("vector", lambda e: e.tensor_copy(v3row[:], psRR[0:1, 0:512]),
                         wait=[W(tC)], inc=[(vC, 1)])
                    p.on("vector", lambda e: e.tensor_scalar_mul(m3row[:], m3row[:], 1.0 / D),
                         wait=[W(vC)], inc=[(vC, 1)])
                    p.on("vector", lambda e: e.tensor_scalar_mul(v3row[:], v3row[:], 1.0 / D),
                         wait=[W(vC)], inc=[(vC, 1)])
                    p.on("vector", lambda e: e.tensor_mul(s1t[0:1, :], m3row[:], m3row[:]),
                         wait=[W(vC)], inc=[(vC, 1)])
                    p.on("vector", lambda e: e.tensor_sub(v3row[:], v3row[:], s1t[0:1, :]),
                         wait=[W(vC)], inc=[(vC, 1)])
                    p.on("scalar", lambda e: e.activation(v3row[:], v3row[:], AF.Ln,
                                                          bias=self.epsc[0:1, 0:1]),
                         wait=[W(vC)], inc=[(sC, 1)])
                    p.on("scalar", lambda e: e.activation(v3row[:], v3row[:], AF.Exp,
                                                          scale=-0.5),
                         wait=[W(sC)], inc=[(sC, 1)])
                    p.on("vector", lambda e: e.tensor_mul(m3row[:], m3row[:], v3row[:]),
                         wait=[W(sC)], inc=[(vC, 1)])
                    p.on("vector", lambda e: e.tensor_scalar_mul(m3row[:], m3row[:], -1.0),
                         wait=[W(vC)], inc=[(vC, 1)])
                    rows3 = W(vC)
                    kb0, pb0, wv0 = psH.acquire()
                    kb1, pb1, wv1 = psH.acquire()
                    p.on("tensor", None, wait=wv0 + wv1 + [rows3])
                    p.on("tensor", lambda e, pb0=pb0: e.matmul(
                        pb0[:], ones_rf[:, :], v3row[:, :], start=True, stop=True))
                    p.on("tensor", lambda e, pb1=pb1: e.matmul(
                        pb1[:], ones_rf[:, :], m3row[:, :], start=True, stop=True),
                        inc=[(tC, 1)])
                    bc3 = W(tC)
                    for cc in range(C):
                        p.on("vector", lambda e, cc=cc, cur=cur, pb0=pb0: e.tensor_mul(
                            rt1[:], ysb[cur][:, cc * 513 + (0 if cc >= 4 else 1):
                                             cc * 513 + (0 if cc >= 4 else 1) + 512], pb0[:]),
                            wait=[bc3, W(vC)], inc=[(vC, 1)])
                        p.on("vector", lambda e, pb1=pb1: e.tensor_add(rt1[:], rt1[:], pb1[:]),
                             wait=[W(vC)], inc=[(vC, 1)])
                        hdst = qraw if cc < 4 else kraw
                        p.on("vector", lambda e, cc=cc, hdst=hdst: e.tensor_scalar(
                            hdst[:, (cc % 4) * 512:(cc % 4 + 1) * 512], rt1[:],
                            lgc[:, cc:cc + 1], lbc[:, cc:cc + 1], ALU.mult, ALU.add),
                            wait=[W(vC), cl], inc=[(vC, 1)])
                    psH.rel[kb0] = (vC, vC.count)
                    psH.rel[kb1] = (vC, vC.count)
                    hrdy = (vC, vC.count)
                    p.on("gpsimd", lambda e, t0=t0: e.dma_start(
                        out=HOUT[0:512, t0:t0 + 512].rearrange("(c p) t -> p c t", p=128),
                        in_=qraw[:].rearrange("p (c t) -> p c t", c=4)),
                        wait=[hrdy], inc=[(hSp[cur], 16)])
                    p.on("gpsimd", lambda e, t0=t0: e.dma_start(
                        out=HOUT[512:1024, t0:t0 + 512].rearrange("(c p) t -> p c t", p=128),
                        in_=kraw[:].rearrange("p (c t) -> p c t", c=4)),
                        inc=[(hSp[cur], 16)])
                    self._ysb_free[cur] = [(outSp[cur], outSp[cur].count),
                                           (hSp[cur], hSp[cur].count), hrdy]
            p.scope(None)
            self.p.barrier()
            self._ysb_free = [None, None]
            self._avfree = None
            self._nmfree = None
            self._expw2 = [[None, None], [None, None]]
            self._attfree = [[None, None], [None, None]]

    def phase_fbc(self, l):
        cx, p, d = self.cx, self.p, self.d
        qW, vC, sC, tC = cx.qW, cx.vC, cx.sC, cx.tC
        W = self.w
        YIN = d["Y1"] if l == 0 else d["Y3"]
        HIN = d["HLN0"] if l == 0 else d["HLN1"]
        T = 512
        NTF = (self.nseq * L) // T
        with contextlib.ExitStack() as st:
            sb = lambda n, sh, dt=F32: st.enter_context(self.nc.sbuf_tensor(n, sh, dt))
            ps = lambda n, sh, dt=F32: st.enter_context(self.nc.psum_tensor(n, sh, dt))
            pre = f"g{l}"
            W1 = sb(pre + "W1", [128, C * 5460], BF16)
            W2 = sb(pre + "W2", [128, 22 * D], BF16)
            b1a = sb(pre + "b1a", [128, 22], F32)
            b1g = sb(pre + "b1g", [128, 22], F32)
            b2c = sb(pre + "b2c", [128, 8], F32)
            hin = [sb(pre + f"hin{i}", [128, C * T], BF16) for i in range(2)]
            yin = sb(pre + "yin", [128, C * T], F32)
            x1in = sb(pre + "x1in", [128, C * T], F32) if l == 1 else None
            gel = sb(pre + "gel", [128, T], BF16)
            gated = sb(pre + "gated", [128, 22 * T], BF16)
            psA = Rot([ps(pre + f"psA{i}", [128, T], F32) for i in range(2)])
            psG = Rot([ps(pre + f"psG{i}", [128, T], F32) for i in range(2)])
            psO = Rot([ps(pre + f"psO{i}", [128, T], F32) for i in range(2)])
            inS = [p.sem(pre + "in0"), p.sem(pre + "in1")]
            yS = p.sem(pre + "yS")
            xS = p.sem(pre + "xS")
            outS = p.sem(pre + "outS")
            p.on("gpsimd", lambda e: cast_w(e, W1, d[f"w1_{l}"]), inc=[(qW, 16)])
            p.on("gpsimd", lambda e: cast_w(e, W2, d[f"w2_{l}"]), inc=[(qW, 16)])
            for dst, src in ((b1a, f"b1a{l}"), (b1g, f"b1g{l}"), (b2c, f"b2_{l}")):
                p.on("sync", lambda e, dst=dst, src=src: e.dma_start(
                    out=dst[:], in_=d[src][:, :]), inc=[(cx.qL, 16)])
            wld = (qW, qW.count)
            cl = (cx.qL, cx.qL.count)
            hin_free = [None, None]
            y_free = None
            x_free = None
            for ti in range(NTF):
                bi = ti % 2
                t0 = ti * T
                p.on("sync", lambda e, bi=bi, t0=t0: fm_load(e, HIN, t0, T, hin[bi]),
                     wait=([hin_free[bi]] if hin_free[bi] else []), inc=[(inS[bi], 16)])
                p.on("sync", lambda e, t0=t0: fm_load(e, YIN, t0, T, yin),
                     wait=([y_free] if y_free else []), inc=[(yS, 16)])
                if l == 1:
                    p.on("sync", lambda e, t0=t0: fm_load(e, d["X1"], t0, T, x1in),
                         wait=([x_free] if x_free else []), inc=[(xS, 16)])
                ldw = (inS[bi], inS[bi].count)
                ydw = (yS, yS.count)
                xdw = (xS, xS.count)
                p.on("tensor", None, wait=[ldw, wld, cl])
                for c in range(22):
                    rows = 128 if c < 21 else 42
                    ka, pa, wva = psA.acquire()
                    if wva:
                        p.on("tensor", None, wait=wva)
                    for dc in range(C):
                        p.on("tensor", lambda e, c=c, dc=dc, pa=pa, bi=bi, rows=rows: e.matmul(
                            pa[0:rows, :], W1[:, dc * 5460 + c * 128:dc * 5460 + c * 128 + rows],
                            hin[bi][:, dc * T:(dc + 1) * T],
                            start=(dc == 0), stop=(dc == C - 1)),
                            inc=([(tC, 1)] if dc == C - 1 else []))
                    kg, pg, wvg = psG.acquire()
                    if wvg:
                        p.on("tensor", None, wait=wvg)
                    for dc in range(C):
                        p.on("tensor", lambda e, c=c, dc=dc, pg=pg, bi=bi, rows=rows: e.matmul(
                            pg[0:rows, :],
                            W1[:, dc * 5460 + 2730 + c * 128:dc * 5460 + 2730 + c * 128 + rows],
                            hin[bi][:, dc * T:(dc + 1) * T],
                            start=(dc == 0), stop=(dc == C - 1)),
                            inc=([(tC, 1)] if dc == C - 1 else []))
                    mmd = W(tC)
                    p.on("scalar", lambda e, c=c, pg=pg, rows=rows: e.activation(
                        gel[0:rows, :], pg[0:rows, :], AF.Gelu, bias=b1g[0:rows, c:c + 1]),
                        wait=[mmd, W(vC)], inc=[(sC, 1)])
                    p.on("vector", lambda e, c=c, pa=pa, rows=rows: e.scalar_tensor_tensor(
                        gated[0:rows, c * T:(c + 1) * T], pa[0:rows, :],
                        b1a[0:rows, c:c + 1], gel[0:rows, :], ALU.add, ALU.mult),
                        wait=[W(sC), W(vC)], inc=[(vC, 1)])
                    psA.rel[ka] = (vC, vC.count)
                    psG.rel[kg] = (vC, vC.count)
                gdone = (vC, vC.count)
                hin_free[bi] = (tC, tC.count)
                p.on("tensor", None, wait=[gdone])
                for ec in range(C):
                    ko, po, wvo = psO.acquire()
                    if wvo:
                        p.on("tensor", None, wait=wvo)
                    for c in range(22):
                        rows = 128 if c < 21 else 42
                        p.on("tensor", lambda e, ec=ec, c=c, po=po, rows=rows: e.matmul(
                            po[:], W2[0:rows, c * D + ec * 128:c * D + (ec + 1) * 128],
                            gated[0:rows, c * T:(c + 1) * T],
                            start=(c == 0), stop=(c == 21)),
                            inc=([(tC, 1)] if c == 21 else []))
                    if l == 1:
                        p.on("vector", lambda e, ec=ec: e.tensor_add(
                            yin[:, ec * T:(ec + 1) * T], yin[:, ec * T:(ec + 1) * T],
                            x1in[:, ec * T:(ec + 1) * T]),
                            wait=[W(vC), ydw, xdw], inc=[(vC, 1)])
                    p.on("vector", lambda e, ec=ec, po=po: e.scalar_tensor_tensor(
                        yin[:, ec * T:(ec + 1) * T], po[:], b2c[:, ec:ec + 1],
                        yin[:, ec * T:(ec + 1) * T], ALU.add, ALU.add),
                        wait=[W(tC), W(vC)] + ([ydw] if l == 0 else []),
                        inc=[(vC, 1)])
                    psO.rel[ko] = (vC, vC.count)
                ydone = (vC, vC.count)
                x_free = ydone
                OUTD = d["Y2"] if l == 0 else d["out"]
                p.on("sync", lambda e, t0=t0, OUTD=OUTD: fm_store(e, OUTD, t0, T, yin, nch=C),
                     wait=[ydone], inc=[(outS, 16)])
                y_free = (outS, outS.count)
            self.p.barrier()

    def build(self):
        self.declare_io()
        self.persistent()
        if "S" in self.phases and not self.fused:
            self.phase_stats()
        if self.fused and "S" in self.phases:
            with contextlib.ExitStack() as sst:
                self.Ssb = sst.enter_context(
                    self.nc.sbuf_tensor("Ssb", [128, C * D], BF16))
                self.syrow = sst.enter_context(
                    self.nc.sbuf_tensor("syrow", [1, 1024], F32))
                self.phase_p1()
                self.phase_gramB()
                self.phase_stats_fused()
        if "P23" in self.phases:
            with contextlib.ExitStack() as pst:
                self.vvs = [pst.enter_context(
                    self.nc.sbuf_tensor(f"vvs{s}", [128, 512], BF16))
                    for s in range(SPC)]
                self.phase_p23a()
                self.phase_p23b()
        if "Fa0" in self.phases:
            self.phase_fa(0)
        if "Fbc0" in self.phases:
            self.phase_fbc(0)
        if "Fa1" in self.phases:
            self.phase_fa(1)
        if "Fbc1" in self.phases:
            self.phase_fbc(1)
        self.p.on("sync", None, wait=[self.w(self.cx.qS)])
        self.p.emit(self.nc)
        self.cx.close()
        return self.d


def kernel(**inputs):
    g = prep_inputs(inputs)
    X = g["X"]
    nc = bass.Bass("TRN2", target_bir_lowering=False)
    m = Main(nc, fused=True, debug=False)
    m.build()
    in_maps = []
    for c in range(NCORE):
        im = {k: g[k] for k in g if k != "X" and k in m.d}
        im["x"] = np.ascontiguousarray(np.concatenate(
            [X[c * TOK:(c + 1) * TOK], X[:c * TOK], X[(c + 1) * TOK:]], 0))
        in_maps.append(im)
    res = run_bass_kernel_spmd(nc, in_maps, core_ids=list(range(NCORE)))
    out = np.concatenate([np.asarray(res.results[c]["out"]).T
                          for c in range(NCORE)], 0)
    return np.ascontiguousarray(out).reshape(B, L, D)


# version 09
# speedup vs baseline: 1.2709x; 1.0180x over previous
import contextlib
import numpy as np
import concourse.bass as bass
import concourse.mybir as mybir
from concourse.bass_utils import run_bass_kernel_spmd

"""Shared kernel-building library (development copy; final code is inlined
into kernel.py). Implements the full megakernel program for
mk_seq_preln_d1_jaxref_1024_a72310.

Layouts:
- tm tile: [128 tokens, F feats] (tokens on partitions)
- fm tile: [128 feats(chunk), T tokens] (feature chunk on partitions)
- HBM fm arrays: [1024(or F), 8192] rows=features (contiguous along tokens)
- HBM tm arrays: [8192, F]
"""
import numpy as np
import concourse.bass as bass
import concourse.mybir as mybir

F32 = mybir.dt.float32
BF16 = mybir.dt.bfloat16
AF = mybir.ActivationFunctionType
ALU = mybir.AluOpType
AX = mybir.AxisListType

B, L, D = 32, 2048, 1024
NCORE = 8
SPC = 4
TOK = SPC * L
C = 8
NT = TOK // 512
HEADS, HD = 16, 64
INNER, FH, FHD = 512, 8, 64
HID = 2730
WIN = 512
EPS = 1e-5

ENGINES = ("sync", "scalar", "vector", "tensor", "gpsimd")


class Sem:
    __slots__ = ("name", "h", "count")

    def __init__(self, name):
        self.name = name
        self.h = None
        self.count = 0


class Prog:
    def __init__(self):
        self.q = {e: [] for e in ENGINES}
        self.sems = []
        self._scope = None

    def scope(self, name):
        self._scope = name

    def sem(self, name):
        s = Sem(name)
        self.sems.append(s)
        return s

    def on(self, eng, build=None, wait=(), inc=()):
        w = [(s, v) for (s, v) in wait]
        i = [(s, k) for (s, k) in inc]
        self.q[eng].append((build, w, i, self._scope))
        for s, k in i:
            s.count += k

    def barrier(self, sems=None, skip=()):
        sems = [s for s in (sems if sems is not None else self.sems)
                if s.count > 0 and s not in skip]
        for e in ENGINES:
            self.on(e, None, wait=[(s, s.count) for s in sems])

    def emit(self, nc):
        import contextlib
        with contextlib.ExitStack() as stack:
            for s in self.sems:
                s.h = stack.enter_context(nc.semaphore(s.name))
            block = stack.enter_context(nc.Block())

            def replay(name):
                def run(eng):
                    cur = [None]
                    cm = [None]
                    for build, waits, incs, scope in self.q[name]:
                        if scope != cur[0]:
                            if cm[0] is not None:
                                cm[0].__exit__(None, None, None)
                                cm[0] = None
                            if scope is not None:
                                cm[0] = nc.named_scope(scope)
                                cm[0].__enter__()
                            cur[0] = scope
                        for s, v in waits:
                            if v > 0:
                                eng.wait_ge(s.h, v)
                        if build is not None:
                            ins = build(eng)
                            for s, k in incs:
                                ins = ins.then_inc(s.h, k)
                        elif incs:
                            ins = eng.nop()
                            for s, k in incs:
                                ins = ins.then_inc(s.h, k)
                    if cm[0] is not None:
                        cm[0].__exit__(None, None, None)
                return run

            block.sync(replay("sync"))
            block.scalar(replay("scalar"))
            block.vector(replay("vector"))
            block.tensor(replay("tensor"))
            block.gpsimd(replay("gpsimd"))


def pack8(v):
    return np.ascontiguousarray(v.reshape(8, 128).T).astype(np.float32)


def make_consts():
    c = {}
    c["identb"] = np.eye(128, dtype=np.float32)
    c["onesrow"] = np.ones((1, 128), np.float32)
    c["onescol"] = np.ones((128, 1), np.float32)
    sel = np.zeros((128, 2), np.float32)
    sel[0:64, 0] = 1.0
    sel[64:128, 1] = 1.0
    c["sel"] = sel
    c["sel2"] = np.ascontiguousarray(sel.T)
    i_idx = np.arange(128)[:, None]
    j_idx = np.arange(128)[None, :]
    NEG = np.float32(-1e30)
    c["mA_prev"] = np.where(j_idx >= i_idx, 0.0, NEG).astype(np.float32)
    c["mA_cur"] = np.where(j_idx <= i_idx, 0.0, NEG).astype(np.float32)
    c["mB_prev"] = np.ascontiguousarray(c["mA_prev"].T)
    c["mB_cur"] = np.ascontiguousarray(c["mA_cur"].T)
    d = np.arange(64)
    inv_freq = 1.0 / 10000.0 ** (np.arange(0, 64, 2, dtype=np.float64) / 64.0)
    t = np.arange(L, dtype=np.float64)
    freqs = t[None, :] * inv_freq[d % 32][:, None]
    base = (np.arange(0, 64, 2, dtype=np.float64) + 0.4 * 64) / (1.4 * 64)
    power = (t - L // 2) / float(WIN // 2)
    scale = base[d % 32][:, None] ** power[None, :]
    cos, sin = np.cos(freqs), np.sin(freqs)
    sgn = np.where(d % 64 < 32, -1.0, 1.0)[:, None]
    CSQ = (cos * scale)
    SSQ = (sgn * sin * scale)
    CSK = (cos / scale)
    SSK = (sgn * sin / scale)
    for nm, a in (("CSQ", CSQ), ("SSQ", SSQ), ("CSK", CSK), ("SSK", SSK)):
        c[nm] = np.ascontiguousarray(np.tile(a, (2, 1))).astype(np.float32)
    return c


def prep_inputs(inp):
    g = {}
    X = np.ascontiguousarray(np.asarray(inp["L0_attn_x"], np.float32).reshape(B * L, D))
    g["X"] = X
    for w in ("Wq", "Wk", "Wv", "Wp"):
        g[w + "T"] = np.ascontiguousarray(np.asarray(inp["L0_attn_" + w], np.float32).T)
    for w in ("Wq", "Wk", "Wv"):
        g[w + "U"] = np.ascontiguousarray(np.asarray(inp["L0_attn_" + w], np.float32))
    for v in ("gq", "bq", "gk", "bk", "gv", "bv", "bp"):
        g[v] = pack8(np.asarray(inp["L0_attn_" + v], np.float32))
    wqkv = np.asarray(inp["L0_ffn_wqkv"], np.float32)
    g["wqkv0"], g["wqkv1"] = np.ascontiguousarray(wqkv[0]), np.ascontiguousarray(wqkv[1])
    wo = np.asarray(inp["L0_ffn_wo"], np.float32)
    g["wo0"], g["wo1"] = np.ascontiguousarray(wo[0]), np.ascontiguousarray(wo[1])
    w1 = np.asarray(inp["L0_ffn_w1"], np.float32)
    g["w1_0"], g["w1_1"] = np.ascontiguousarray(w1[0]), np.ascontiguousarray(w1[1])
    w2 = np.asarray(inp["L0_ffn_w2"], np.float32)
    w2p = np.zeros((2, 22 * 128, D), np.float32)
    w2p[:, :HID, :] = w2
    g["w2_0"], g["w2_1"] = np.ascontiguousarray(w2p[0]), np.ascontiguousarray(w2p[1])
    for l in range(2):
        qs = np.asarray(inp["L0_ffn_q_scale"], np.float32)[l]
        ks = np.asarray(inp["L0_ffn_k_scale"], np.float32)[l]
        g[f"qs{l}"] = np.tile(qs, 2)[:, None].astype(np.float32)
        g[f"ks{l}"] = np.tile(ks, 2)[:, None].astype(np.float32)
        g[f"lng{l}"] = pack8(np.asarray(inp["L0_ffn_ln_g"], np.float32)[l])
        g[f"lnb{l}"] = pack8(np.asarray(inp["L0_ffn_ln_b"], np.float32)[l])
        b1 = np.asarray(inp["L0_ffn_b1"], np.float32)[l]
        b1a = np.zeros(22 * 128, np.float32)
        b1a[:HID] = b1[:HID]
        b1g = np.zeros(22 * 128, np.float32)
        b1g[:HID] = b1[HID:]
        g[f"b1a{l}"] = np.ascontiguousarray(b1a.reshape(22, 128).T)
        g[f"b1g{l}"] = np.ascontiguousarray(b1g.reshape(22, 128).T)
        g[f"b2_{l}"] = pack8(np.asarray(inp["L0_ffn_b2"], np.float32)[l])
    g.update(make_consts())
    return g


"""Program builders for the megakernel (development copy).

Conventions:
- Prog.on(engine, build, wait=[(sem, val)], inc=[(sem, k)])
- PSUM bank rotation via Rot: acquire() returns (bank_tile, wait_list); after
  the last reader of the bank's contents, call rot.release(reader_sem_count).
"""
import contextlib
import numpy as np
import concourse.bass as bass
import concourse.mybir as mybir


class Rot:

    def __init__(self, banks):
        self.banks = banks
        self.rel = [None] * len(banks)
        self.i = 0

    def acquire(self):
        k = self.i % len(self.banks)
        self.i += 1
        w = [self.rel[k]] if self.rel[k] else []
        return k, self.banks[k], w

    def release(self, k, sem):
        self.rel[k] = (sem, sem.count)


class Ctx:
    def __init__(self, nc, p):
        self.nc = nc
        self.p = p
        self.stack = contextlib.ExitStack()
        for name in ("qL", "qS", "qW", "vC", "sC", "tC", "gC"):
            setattr(self, name, p.sem(name))

    def sb(self, name, shape, dt=F32):
        return self.stack.enter_context(self.nc.sbuf_tensor(name, shape, dt))

    def ps(self, name, shape, dt=F32):
        return self.stack.enter_context(self.nc.psum_tensor(name, shape, dt))

    def close(self):
        self.stack.close()


def cast_w(e, dst, src):
    R, F = src.shape
    nch = R // 128
    return e.dma_start(
        out=dst[:, 0:nch * F].rearrange("p (c f) -> p c f", c=nch),
        in_=src.rearrange("(c p) f -> p c f", p=128))


def fm_store(e, dram, t0, T, sb, nch=8):
    return e.dma_start(
        out=dram[:, t0:t0 + T].rearrange("(c p) t -> p c t", p=128),
        in_=sb[:, 0:nch * T].rearrange("p (c t) -> p c t", c=nch))


def fm_load(e, dram, t0, T, sb, nch=8):
    return e.dma_start(
        out=sb[:, 0:nch * T].rearrange("p (c t) -> p c t", c=nch),
        in_=dram[:, t0:t0 + T].rearrange("(c p) t -> p c t", p=128))


def build_launch_a(nc, stages="all", ntiles=NT):
    p = Prog()
    cx = Ctx(nc, p)
    d = {}
    d["x"] = nc.dram_tensor("x", [TOK, D], F32, kind="ExternalInput").ap()
    for w in ("WqT", "WkT", "WvT"):
        d[w] = nc.dram_tensor(w, [D, D], F32, kind="ExternalInput").ap()
    d["identb"] = nc.dram_tensor("identb", [128, 128], F32, kind="ExternalInput").ap()
    d["bnp"] = nc.dram_tensor("bnp", [128, 48], F32, kind="ExternalOutput").ap()
    d["rows"] = nc.dram_tensor("rows", [2 * TOK], F32, kind="ExternalOutput").ap()
    d["cols"] = nc.dram_tensor("cols", [128, 128], F32, kind="ExternalOutput").ap()

    idb = cx.sb("idb", [128, 128], BF16)
    idf = cx.sb("idf", [128, 128], F32)
    W = {w: cx.sb("s" + w, [128, C * D], BF16) for w in ("WqT", "WkT", "WvT")}
    xin = [cx.sb(f"xin{i}", [128, 4 * D], F32) for i in range(2)]
    xlnf = [cx.sb(f"xlnf{i}", [128, C * 512], BF16) for i in range(2)]
    bnbuf = cx.sb("bnbuf", [128, 12], F32)
    mv = cx.sb("mv", [128, 2], F32)
    rcol = cx.sb("rcol", [128, 1], F32)
    nmr = cx.sb("nmr", [128, 1], F32)
    stash = cx.sb("stash", [128, 8], F32)
    rowb = cx.sb("rowb", [8, 128], F32)
    acc = cx.sb("acc", [128, 48], F32)
    tmpc = cx.sb("tmpc", [128, 1], F32)
    sqc = cx.sb("sqc", [128, 1], F32)
    epsc = cx.sb("epsc", [128, 1], F32)
    trash = cx.sb("trash", [128, 512], BF16)
    xln_tm = cx.sb("xln_tm", [128, D], BF16)
    psT = cx.ps("psT", [128, D], BF16)
    psH = Rot([cx.ps(f"psH{i}", [128, 512], F32) for i in range(2)])
    psR = cx.ps("psR", [8, 128], F32)

    p.on("gpsimd", lambda e: e.dma_start(out=idb[:], in_=d["identb"][:, :]), inc=[(cx.qW, 16)])
    p.on("sync", lambda e: e.dma_start(out=idf[:], in_=d["identb"][:, :]), inc=[(cx.qL, 16)])
    for w in ("WqT", "WkT", "WvT"):
        p.on("gpsimd", lambda e, w=w: cast_w(e, W[w], d[w]), inc=[(cx.qW, 16)])
    wl = cx.qW.count
    p.on("vector", lambda e: e.memset(acc[:], 0.0))
    p.on("vector", lambda e: e.memset(epsc[:], EPS), inc=[(cx.vC, 1)])

    xin_free = [None, None]
    xlnf_free = [None, None]
    for i in range(ntiles):
        bi = i % 2
        t0 = i * 512
        p.on("sync", lambda e, t0=t0, bi=bi: e.dma_start(
            out=xin[bi][:].rearrange("p (g d) -> p g d", g=4),
            in_=d["x"][t0:t0 + 512, :].rearrange("(g p) d -> p g d", p=128)),
            wait=([xin_free[bi]] if xin_free[bi] else []), inc=[(cx.qL, 16)])
        ld_t = cx.qL.count
        if xlnf_free[bi]:
            p.on("vector", None, wait=[xlnf_free[bi]])
        for gi in range(4 if stages != "ld" else 0):
            xg = xin[bi][:, gi * D:(gi + 1) * D]
            p.on("vector", lambda e, xg=xg: e.bn_stats(bnbuf[:, 0:6], xg[:, 0:512]),
                 wait=[(cx.qL, ld_t)])
            p.on("vector", lambda e, xg=xg: e.bn_stats(bnbuf[:, 6:12], xg[:, 512:1024]),
                 inc=[(cx.vC, 1)])
            p.on("vector", lambda e: e.bn_aggr(mv[:, 0:2], bnbuf[:, 0:12]),
                 wait=[(cx.sC, cx.sC.count), (cx.vC, cx.vC.count)], inc=[(cx.vC, 1)])
            p.on("scalar", lambda e: e.activation(rcol[:], mv[:, 1:2], AF.Ln, bias=epsc[:, 0:1]),
                 wait=[(cx.vC, cx.vC.count)], inc=[(cx.sC, 1)])
            p.on("scalar", lambda e: e.activation(rcol[:], rcol[:], AF.Exp, scale=-0.5),
                 inc=[(cx.sC, 1)])
            p.on("vector", lambda e: e.tensor_scalar(nmr[:], mv[:, 0:1], rcol[:, 0:1], -1.0,
                                                     ALU.mult, ALU.mult),
                 wait=[(cx.sC, cx.sC.count)])
            p.on("vector", lambda e, gi=gi: e.tensor_copy(stash[:, gi:gi + 1], mv[:, 0:1]))
            p.on("vector", lambda e, gi=gi: e.tensor_copy(stash[:, 4 + gi:5 + gi],
                                                          rcol[:, 0:1]), inc=[(cx.vC, 1)])
            p.on("vector", lambda e, xg=xg: e.tensor_scalar(
                xln_tm[:], xg, rcol[:, 0:1], nmr[:, 0:1], ALU.mult, ALU.add),
                wait=[(cx.tC, cx.tC.count)], inc=[(cx.vC, 1)])
            if stages == "ln":
                continue
            p.on("tensor", None, wait=[(cx.vC, cx.vC.count)])
            for cc in range(C):
                last = cc == C - 1
                p.on("tensor", lambda e, cc=cc: e.transpose(
                    psT[:, cc * 128:(cc + 1) * 128], xln_tm[:, cc * 128:(cc + 1) * 128],
                    idb[:]), inc=([(cx.tC, 1)] if last else []))
            p.on("vector", lambda e, bi=bi, gi=gi: e.tensor_copy(
                xlnf[bi][:].rearrange("p (c t) -> p c t", c=C)[:, :, gi * 128:(gi + 1) * 128],
                psT[:].rearrange("p (c t) -> p c t", c=C)),
                wait=[(cx.tC, cx.tC.count)], inc=[(cx.vC, 1)])
        xvread = cx.vC.count
        p.on("tensor", None, wait=[(cx.vC, xvread), (cx.qW, wl)])
        for br, w in (enumerate(("WqT", "WkT", "WvT")) if stages in ("all", "mm", "mm_nosq") else []):
            for ec in range(C):
                kb, pb, wv = psH.acquire()
                if wv:
                    p.on("tensor", None, wait=wv)
                for dc in range(C):
                    p.on("tensor", lambda e, w=w, ec=ec, dc=dc, pb=pb, bi=bi: e.matmul(
                        pb[:], W[w][:, dc * D + ec * 128:dc * D + (ec + 1) * 128],
                        xlnf[bi][:, dc * 512:(dc + 1) * 512],
                        start=(dc == 0), stop=(dc == C - 1)),
                        inc=([(cx.tC, 1)] if dc == C - 1 else []))
                tw = cx.tC.count
                col = br * 16 + ec
                p.on("vector", lambda e, pb=pb: e.reduce_sum(tmpc[:], pb[:], axis=AX.X),
                     wait=[(cx.tC, tw)], inc=[(cx.vC, 1)])
                p.on("vector", lambda e, col=col: e.tensor_add(
                    acc[:, col:col + 1], acc[:, col:col + 1], tmpc[:]),
                    wait=[(cx.vC, cx.vC.count)], inc=[(cx.vC, 1)])
                p.on("scalar", lambda e, pb=pb: e.activation(
                    trash[:], pb[:], AF.Square, accum_out=sqc[:]),
                    wait=[(cx.tC, tw), (cx.vC, cx.vC.count)], inc=[(cx.sC, 1)])
                p.on("vector", lambda e, col=col: e.tensor_add(
                    acc[:, col + 8:col + 9], acc[:, col + 8:col + 9], sqc[:]),
                    wait=[(cx.sC, cx.sC.count)], inc=[(cx.vC, 1)])
                psH.release(kb, cx.vC)
                psH.rel[kb] = (cx.vC, cx.vC.count)
        xlnf_free[bi] = (cx.tC, cx.tC.count)
        xin_free[bi] = (cx.vC, xvread)
        if stages in ("ld",):
            xin_free[bi] = (cx.qL, cx.qL.count)
            continue
        p.on("tensor", lambda e: e.transpose(psR[:], stash[:], idf[:]),
             wait=[(cx.vC, cx.vC.count)], inc=[(cx.tC, 1)])
        p.on("vector", lambda e: e.tensor_copy(rowb[:], psR[:]),
             wait=[(cx.tC, cx.tC.count)], inc=[(cx.vC, 1)])
        rv = (cx.vC, cx.vC.count)
        p.on("sync", lambda e, t0=t0: e.dma_start(
            out=d["rows"][t0:t0 + 512].rearrange("(g q) -> g q", g=4),
            in_=rowb[0:4, :]), wait=[rv], inc=[(cx.qS, 16)])
        p.on("sync", lambda e, t0=t0: e.dma_start(
            out=d["rows"][TOK + t0:TOK + t0 + 512].rearrange("(g q) -> g q", g=4),
            in_=rowb[4:8, :]), inc=[(cx.qS, 16)])
        p.on("sync", lambda e, i=i: e.dma_start(
            out=d["cols"][:, i * 4:(i + 1) * 4], in_=stash[:, 0:4]),
            wait=[rv], inc=[(cx.qS, 16)])
        p.on("sync", lambda e, i=i: e.dma_start(
            out=d["cols"][:, 64 + i * 4:64 + (i + 1) * 4], in_=stash[:, 4:8]),
            inc=[(cx.qS, 16)])
        stash_free = (cx.qS, cx.qS.count)
        p.on("vector", None, wait=[stash_free])
    p.on("sync", lambda e: e.dma_start(out=d["bnp"][:, :], in_=acc[:]),
         wait=[(cx.vC, cx.vC.count)], inc=[(cx.qS, 16)])
    p.on("sync", None, wait=[(cx.qS, cx.qS.count)])
    p.emit(nc)
    cx.close()
    return d



class Main:

    def __init__(self, nc, fused=False, debug=False,
                 phases=("S", "P23", "Fa0", "Fbc0", "Fa1", "Fbc1"), nseq=SPC):
        self.nseq = nseq
        self.nc = nc
        self.fused = fused
        self.debug = debug
        self.phases = phases
        self.p = Prog()
        self.cx = Ctx(nc, self.p)
        self.d = {}
        self._ysb_free = [None, None]
        self._otm_free = None
        self._tab_free = None
        self._avfree = None
        self._nmfree = None
        self._expw2 = [[None, None], [None, None]]
        self._attfree = [[None, None], [None, None]]

    def sb(self, name, shape, dt=F32):
        return self.cx.sb(name, shape, dt)

    def ps(self, name, shape, dt=F32):
        return self.cx.ps(name, shape, dt)

    def on(self, eng, build=None, wait=(), inc=()):
        self.p.on(eng, build, wait=wait, inc=inc)

    def w(self, sem):
        return (sem, sem.count)

    def declare_io(self):
        nc, d = self.nc, self.d
        DI = lambda n, sh: nc.dram_tensor(n, sh, F32, kind="ExternalInput").ap()
        d["x"] = DI("x", [(B * L) if self.fused else TOK, D])
        for w in ("WqT", "WkT", "WvT", "WpT"):
            d[w] = DI(w, [D, D])
        for v in ("gq", "bq", "gk", "bk", "gv", "bv", "bp"):
            d[v] = DI(v, [128, 8])
        for l in range(2):
            d[f"wqkv{l}"] = DI(f"wqkv{l}", [D, 1536])
            d[f"wo{l}"] = DI(f"wo{l}", [512, D])
            d[f"w1_{l}"] = DI(f"w1_{l}", [D, 5460])
            d[f"w2_{l}"] = DI(f"w2_{l}", [22 * 128, D])
            d[f"qs{l}"] = DI(f"qs{l}", [128, 1])
            d[f"ks{l}"] = DI(f"ks{l}", [128, 1])
            d[f"lng{l}"] = DI(f"lng{l}", [128, 8])
            d[f"lnb{l}"] = DI(f"lnb{l}", [128, 8])
            d[f"b1a{l}"] = DI(f"b1a{l}", [128, 22])
            d[f"b1g{l}"] = DI(f"b1g{l}", [128, 22])
            d[f"b2_{l}"] = DI(f"b2_{l}", [128, 8])
        d["identb"] = DI("identb", [128, 128])
        d["onesrow"] = DI("onesrow", [1, 128])
        d["onescol"] = DI("onescol", [128, 1])
        d["sel"] = DI("sel", [128, 2])
        d["sel2"] = DI("sel2", [2, 128])
        for m in ("mA_prev", "mA_cur", "mB_prev", "mB_cur"):
            d[m] = DI(m, [128, 128])
        for t in ("CSQ", "SSQ", "CSK", "SSK"):
            d[t] = DI(t, [128, L])
        if not self.fused:
            d["bnps"] = DI("bnps", [8 * 128, 48])
            d["ln_rows"] = DI("ln_rows", [2 * TOK])
            d["ln_cols"] = DI("ln_cols", [128, 128])
        else:
            for wn in ("Wq", "Wk", "Wv"):
                d[wn + "U"] = DI(wn + "U", [D, D])
        d["out"] = nc.dram_tensor("out", [D, TOK], F32, kind="ExternalOutput").ap()
        kind = "ExternalOutput" if self.debug else "Internal"
        DSC = lambda n, sh, dt=F32: nc.dram_tensor(n, sh, dt, kind=kind).ap()
        d["XLN"] = DSC("XLN", [D, TOK], BF16)
        d["X0F"] = DSC("X0F", [D, TOK])
        d["X1"] = DSC("X1", [D, TOK])
        d["S1"] = DSC("S1", [2 * TOK])
        d["Y1"] = DSC("Y1", [D, TOK])
        d["HLN0"] = DSC("HLN0", [D, TOK], BF16)
        d["Y2"] = DSC("Y2", [D, TOK])
        d["Y3"] = DSC("Y3", [D, TOK])
        d["HLN1"] = DSC("HLN1", [D, TOK], BF16)
        if self.fused:
            d["YTM"] = DSC("YTM", [B * L, D], BF16)
        if self.debug:
            d["DBGA"] = DSC("DBGA", [128, 48])
            d["DBGS"] = DSC("DBGS", [128, 64])
            d["DBGSS"] = DSC("DBGSS", [128, C * D], BF16)
            d["DBGM"] = DSC("DBGM", [128, C * D], BF16)
            d["DBGQ"] = DSC("DBGQ", [128, C * 512], BF16)
            d["DBGK"] = DSC("DBGK", [128, C * 512], BF16)

    def persistent(self):
        cx, p, d = self.cx, self.p, self.d
        self.idb = self.sb("idb", [128, 128], BF16)
        self.idf = self.sb("idf", [128, 128], F32)
        self.ones_r = self.sb("ones_r", [1, 128], BF16)
        self.ones_c = self.sb("ones_c", [128, 1], BF16)
        self.sel_b = self.sb("sel_b", [128, 2], BF16)
        self.sel2_b = self.sb("sel2_b", [2, 128], BF16)
        self.mA = {"prev": self.sb("mAp", [128, 128], BF16),
                   "cur": self.sb("mAc", [128, 128], BF16)}
        self.mB = {"prev": self.sb("mBp", [128, 128], BF16),
                   "cur": self.sb("mBc", [128, 128], BF16)}
        self.epsc = self.sb("epsc", [128, 1], F32)
        self.AB = {n: self.sb("AB" + n, [128, 8], F32)
                   for n in ("Aq", "Bq", "Ak", "Bk", "Av", "Bv")}
        self.bpc = self.sb("bpc", [128, 8], F32)
        self.nmrc = self.sb("nmrc", [128, 64], F32)
        self.rc = self.sb("rc", [128, 128], F32)
        for dst, src in ((self.idb, "identb"), (self.ones_r, "onesrow"),
                         (self.ones_c, "onescol"), (self.sel_b, "sel"),
                         (self.sel2_b, "sel2"),
                         (self.mA["prev"], "mA_prev"), (self.mA["cur"], "mA_cur"),
                         (self.mB["prev"], "mB_prev"), (self.mB["cur"], "mB_cur")):
            p.on("gpsimd", lambda e, dst=dst, src=src: e.dma_start(
                out=dst[:], in_=d[src][:, :]), inc=[(cx.qW, 16)])
        p.on("sync", lambda e: e.dma_start(out=self.idf[:], in_=d["identb"][:, :]),
             inc=[(cx.qL, 16)])
        p.on("sync", lambda e: e.dma_start(out=self.bpc[:], in_=d["bp"][:, :]),
             inc=[(cx.qL, 16)])
        p.on("vector", lambda e: e.memset(self.epsc[:], EPS), inc=[(cx.vC, 1)])

    def phase_stats(self):
        cx, p, d = self.cx, self.p, self.d
        bn8 = self.sb("bn8", [128, 8 * 48], F32)
        accs = self.sb("accs", [128, 48], F32)
        t1 = self.sb("St1", [128, 8], F32)
        t2 = self.sb("St2", [128, 8], F32)
        t3 = self.sb("St3", [128, 8], F32)
        gcol = self.sb("gcol", [128, 8], F32)
        bcol = self.sb("bcol", [128, 8], F32)
        p.on("sync", lambda e: e.dma_start(
            out=bn8[:].rearrange("p (c f) -> p c f", c=8),
            in_=d["bnps"][:, :].rearrange("(c p) f -> p c f", p=128)), inc=[(cx.qL, 16)])
        p.on("sync", lambda e: e.dma_start(out=self.rc[:], in_=d["ln_cols"][:, :]),
             inc=[(cx.qL, 16)])
        lded = cx.qL.count
        p.on("vector", lambda e: e.tensor_add(accs[:], bn8[:, 0:48], bn8[:, 48:96]),
             wait=[(cx.qL, lded)], inc=[(cx.vC, 1)])
        for cc in range(2, 8):
            p.on("vector", lambda e, cc=cc: e.tensor_add(
                accs[:], accs[:], bn8[:, cc * 48:(cc + 1) * 48]),
                wait=[self.w(cx.vC)], inc=[(cx.vC, 1)])
        NTOT = float(B * L)
        for br, (gn, bnm, An, Bn) in enumerate((("gq", "bq", "Aq", "Bq"),
                                                ("gk", "bk", "Ak", "Bk"),
                                                ("gv", "bv", "Av", "Bv"))):
            s0, s1_ = br * 16, br * 16 + 8
            p.on("sync", lambda e, gn=gn: e.dma_start(out=gcol[:], in_=d[gn][:, :]),
                 wait=[self.w(cx.vC), self.w(cx.qS)], inc=[(cx.qL, 16)])
            p.on("sync", lambda e, bnm=bnm: e.dma_start(out=bcol[:], in_=d[bnm][:, :]),
                 inc=[(cx.qL, 16)])
            lb = cx.qL.count
            p.on("vector", lambda e, s0=s0: e.tensor_scalar_mul(
                t1[:], accs[:, s0:s0 + 8], 1.0 / NTOT),
                wait=[(cx.qL, lb), self.w(cx.vC), self.w(cx.sC)], inc=[(cx.vC, 1)])
            p.on("vector", lambda e, s1_=s1_: e.tensor_scalar_mul(
                t2[:], accs[:, s1_:s1_ + 8], 1.0 / NTOT), inc=[(cx.vC, 1)])
            p.on("vector", lambda e: e.tensor_mul(t3[:], t1[:], t1[:]),
                 wait=[self.w(cx.vC)], inc=[(cx.vC, 1)])
            p.on("vector", lambda e: e.tensor_sub(t2[:], t2[:], t3[:]),
                 wait=[self.w(cx.vC)], inc=[(cx.vC, 1)])
            p.on("scalar", lambda e: e.activation(t2[:], t2[:], AF.Ln, bias=self.epsc[:, 0:1]),
                 wait=[self.w(cx.vC)], inc=[(cx.sC, 1)])
            p.on("scalar", lambda e: e.activation(t2[:], t2[:], AF.Exp, scale=-0.5),
                 wait=[self.w(cx.sC)], inc=[(cx.sC, 1)])
            p.on("vector", lambda e, An=An: e.tensor_mul(self.AB[An][:], gcol[:], t2[:]),
                 wait=[self.w(cx.sC)], inc=[(cx.vC, 1)])
            p.on("vector", lambda e, An=An: e.tensor_scalar_mul(
                self.AB[An][:], self.AB[An][:], 4.0), wait=[self.w(cx.vC)], inc=[(cx.vC, 1)])
            p.on("vector", lambda e, An=An: e.tensor_mul(t3[:], t1[:], self.AB[An][:]),
                 wait=[self.w(cx.vC)], inc=[(cx.vC, 1)])
            p.on("vector", lambda e, Bn=Bn: e.tensor_scalar(
                self.AB[Bn][:], bcol[:], 4.0, -4.0, ALU.mult, ALU.add), inc=[(cx.vC, 1)])
            p.on("vector", lambda e, Bn=Bn: e.tensor_sub(self.AB[Bn][:], self.AB[Bn][:], t3[:]),
                 wait=[self.w(cx.vC)], inc=[(cx.vC, 1)])
            if self.debug and br == 0:
                p.on("sync", lambda e: e.dma_start(out=d["DBGS"][:, 48:56], in_=t2[:]),
                     wait=[self.w(cx.vC), self.w(cx.sC)], inc=[(cx.qS, 16)])
                p.on("sync", lambda e: e.dma_start(out=d["DBGS"][:, 56:64], in_=gcol[:]),
                     inc=[(cx.qS, 16)])
                p.on("sync", lambda e, An=An: e.dma_start(out=d["DBGS"][:, 40:48],
                     in_=self.AB[An][:]), inc=[(cx.qS, 16)])
        p.on("vector", lambda e: e.tensor_mul(self.nmrc[:], self.rc[:, 0:64],
                                              self.rc[:, 64:128]),
             wait=[(cx.qL, lded)], inc=[(cx.vC, 1)])
        p.on("vector", lambda e: e.tensor_scalar_mul(self.nmrc[:], self.nmrc[:], -1.0),
             wait=[self.w(cx.vC)], inc=[(cx.vC, 1)])
        if self.debug:
            p.on("sync", lambda e: e.dma_start(out=d["DBGS"][:, 0:48], in_=accs[:]),
                 wait=[self.w(cx.vC), self.w(cx.sC)], inc=[(cx.qS, 16)])
            for i, n in enumerate(("Aq", "Bq", "Ak", "Bk", "Av", "Bv")):
                p.on("sync", lambda e, i=i, n=n: e.dma_start(
                    out=d["DBGA"][:, i * 8:(i + 1) * 8], in_=self.AB[n][:]),
                    wait=[self.w(cx.vC), self.w(cx.sC)], inc=[(cx.qS, 16)])
        self.p.barrier()

    def phase_p1(self):
        cx, p, d = self.cx, self.p, self.d
        vC, sC, tC = cx.vC, cx.sC, cx.tC
        W = self.w
        NALL = (B * L) // 512
        with contextlib.ExitStack() as st:
            sb = lambda n, sh, dt=F32: st.enter_context(self.nc.sbuf_tensor(n, sh, dt))
            ps = lambda n, sh, dt=F32: st.enter_context(self.nc.psum_tensor(n, sh, dt))
            xin = [sb(f"p1xin{i}", [128, 4 * D], F32) for i in range(2)]
            ytile = [sb(f"p1y{i}", [128, 4 * D], BF16) for i in range(2)]
            bnbuf = [sb(f"p1bn{i}", [128, 12], F32) for i in range(2)]
            mv = [sb(f"p1mv{i}", [128, 2], F32) for i in range(2)]
            rcol = [sb(f"p1rcol{i}", [128, 1], F32) for i in range(2)]
            nmr = [sb(f"p1nmr{i}", [128, 1], F32) for i in range(2)]
            onebc = sb("p1oneb", [128, 1], BF16)
            psG = {mi: ps(f"p1g{mi}", [128, min(1024, ((8 - mi) * 128 + 511) // 512 * 512)])
                   for mi in (0, 1, 6, 7)}
            psMean = ps("p1mean", [1, 1024])
            inS = [p.sem("p1in0"), p.sem("p1in1")]
            ysS = [p.sem("p1ys0"), p.sem("p1ys1")]
            p.on("vector", lambda e: e.memset(onebc[:], 1.0), inc=[(vC, 1)])
            xin_free = [None, None]
            yt_free = [None, None]
            chain_free = [None, None]
            for i in range(NALL):
                bi = i % 2
                t0 = i * 512
                p.on("sync", lambda e, t0=t0, bi=bi: e.dma_start(
                    out=xin[bi][:].rearrange("p (g dd) -> p g dd", g=4),
                    in_=d["x"][t0:t0 + 512, :].rearrange("(g p) dd -> p g dd", p=128)),
                    wait=(xin_free[bi] or []), inc=[(inS[bi], 16)])
                ld_t = (inS[bi], inS[bi].count)
                if yt_free[bi]:
                    p.on("scalar", None, wait=yt_free[bi])
                    p.on("vector", None, wait=yt_free[bi])
                for gi in range(4):
                    par = gi % 2
                    xg = xin[bi][:, gi * D:(gi + 1) * D]
                    yg = ytile[bi][:, gi * D:(gi + 1) * D]
                    p.on("vector", lambda e, xg=xg, par=par: e.bn_stats(
                        bnbuf[par][:, 0:6], xg[:, 0:512]), wait=[ld_t])
                    p.on("vector", lambda e, xg=xg, par=par: e.bn_stats(
                        bnbuf[par][:, 6:12], xg[:, 512:1024]), inc=[(vC, 1)])
                    bnw = (vC, vC.count)
                    p.on("vector", lambda e, par=par: e.bn_aggr(
                        mv[par][:, 0:2], bnbuf[par][:, 0:12]),
                        wait=[bnw] + ([chain_free[par]] if chain_free[par] else []),
                        inc=[(vC, 1)])
                    aggw = (vC, vC.count)
                    p.on("scalar", lambda e, par=par: e.activation(
                        rcol[par][:], mv[par][:, 1:2], AF.Ln, bias=self.epsc[:, 0:1]),
                        wait=[aggw], inc=[(sC, 1)])
                    p.on("scalar", lambda e, par=par: e.activation(
                        rcol[par][:], rcol[par][:], AF.Exp, scale=-0.5),
                        wait=[W(sC)], inc=[(sC, 1)])
                    rw = (sC, sC.count)
                    p.on("vector", lambda e, par=par: e.tensor_scalar(
                        nmr[par][:], mv[par][:, 0:1], rcol[par][:, 0:1], -1.0,
                        ALU.mult, ALU.mult), wait=[rw], inc=[(vC, 1)])
                    nw = (vC, vC.count)
                    if i < NT:
                        idx = i * 4 + gi
                        p.on("vector", lambda e, idx=idx, par=par: e.tensor_copy(
                            self.rc[:, idx:idx + 1], mv[par][:, 0:1]),
                            wait=[aggw], inc=[(vC, 1)])
                        p.on("vector", lambda e, idx=idx, par=par: e.tensor_copy(
                            self.rc[:, 64 + idx:65 + idx], rcol[par][:, 0:1]),
                            wait=[rw], inc=[(vC, 1)])
                    if par == 0:
                        p.on("scalar", lambda e, xg=xg, yg=yg, par=par: e.activation(
                            yg, xg, AF.Identity, bias=nmr[par][:, 0:1],
                            scale=rcol[par][:, 0:1]), wait=[nw], inc=[(sC, 1)])
                        chain_free[par] = (sC, sC.count)
                    else:
                        p.on("vector", lambda e, xg=xg, yg=yg, par=par: e.tensor_scalar(
                            yg, xg, rcol[par][:, 0:1], nmr[par][:, 0:1],
                            ALU.mult, ALU.add), wait=[nw, rw], inc=[(vC, 1)])
                        chain_free[par] = (vC, vC.count)
                yrdy = [(sC, sC.count), (vC, vC.count)]
                p.on("gpsimd", lambda e, bi=bi, t0=t0: e.dma_start(
                    out=d["YTM"][t0:t0 + 512, :].rearrange("(g p) dd -> p g dd", p=128),
                    in_=ytile[bi][:].rearrange("p (g dd) -> p g dd", g=4)),
                    wait=yrdy, inc=[(ysS[bi], 16)])
                p.on("tensor", None, wait=yrdy + ([(vC, 1)] if i == 0 else []))
                for gi in range(4):
                    yg = ytile[bi][:, gi * D:(gi + 1) * D]
                    first = (i == 0 and gi == 0)
                    last = (i == NALL - 1 and gi == 3)
                    for mi in (0, 1, 6, 7):
                        w0 = mi * 128
                        wdt = 1024 - w0
                        for (off, wp) in ([(0, 512), (512, wdt - 512)] if wdt > 512
                                          else [(0, wdt)]):
                            p.on("tensor", lambda e, yg=yg, mi=mi, w0=w0, off=off, wp=wp,
                                 first=first, last=last: e.matmul(
                                psG[mi][:, off:off + wp],
                                yg[:, mi * 128:(mi + 1) * 128],
                                yg[:, w0 + off:w0 + off + wp],
                                start=first, stop=last))
                    for half in range(2):
                        p.on("tensor", lambda e, yg=yg, half=half, first=first, last=last:
                             e.matmul(psMean[0:1, half * 512:(half + 1) * 512],
                                      onebc[:], yg[:, half * 512:(half + 1) * 512],
                                      start=first, stop=last),
                             inc=([(tC, 1)] if (half == 1) else []))
                gdone = W(tC)
                xin_free[bi] = yrdy
                yt_free[bi] = [gdone, (ysS[bi], ysS[bi].count)]
            p.on("scalar", lambda e: e.copy(self.Ssb[:, 0 * D:1 * D], psG[0][:, 0:1024]),
                 wait=[gdone], inc=[(sC, 1)])
            p.on("scalar", lambda e: e.copy(self.Ssb[:, 1 * D + 128:2 * D], psG[1][:, 0:896]),
                 inc=[(sC, 1)])
            p.on("scalar", lambda e: e.copy(self.Ssb[:, 6 * D + 768:7 * D], psG[6][:, 0:256]),
                 inc=[(sC, 1)])
            p.on("scalar", lambda e: e.copy(self.Ssb[:, 7 * D + 896:8 * D], psG[7][:, 0:128]),
                 inc=[(sC, 1)])
            p.on("vector", lambda e: e.tensor_copy(self.syrow[:], psMean[0:1, :]),
                 wait=[gdone], inc=[(vC, 1)])
            self.p.barrier()

    def phase_gramB(self):
        cx, p, d = self.cx, self.p, self.d
        vC, sC, tC = cx.vC, cx.sC, cx.tC
        W = self.w
        NALL = (B * L) // 512
        with contextlib.ExitStack() as st:
            sb = lambda n, sh, dt=F32: st.enter_context(self.nc.sbuf_tensor(n, sh, dt))
            ps = lambda n, sh, dt=F32: st.enter_context(self.nc.psum_tensor(n, sh, dt))
            ytile = [sb(f"gBy{i}", [128, 4 * D], BF16) for i in range(2)]
            psG = {mi: ps(f"gBg{mi}", [128, min(1024, ((8 - mi) * 128 + 511) // 512 * 512)])
                   for mi in (2, 3, 4, 5)}
            inS = [p.sem("gBin0"), p.sem("gBin1")]
            yt_free = [None, None]
            for i in range(NALL):
                bi = i % 2
                t0 = i * 512
                p.on("sync", lambda e, t0=t0, bi=bi: e.dma_start(
                    out=ytile[bi][:].rearrange("p (g dd) -> p g dd", g=4),
                    in_=d["YTM"][t0:t0 + 512, :].rearrange("(g p) dd -> p g dd", p=128)),
                    wait=([yt_free[bi]] if yt_free[bi] else []), inc=[(inS[bi], 16)])
                ld_t = (inS[bi], inS[bi].count)
                p.on("tensor", None, wait=[ld_t])
                for gi in range(4):
                    yg = ytile[bi][:, gi * D:(gi + 1) * D]
                    first = (i == 0 and gi == 0)
                    last = (i == NALL - 1 and gi == 3)
                    for mi in (2, 3, 4, 5):
                        w0 = mi * 128
                        wdt = 1024 - w0
                        for (off, wp) in ([(0, 512), (512, wdt - 512)] if wdt > 512
                                          else [(0, wdt)]):
                            p.on("tensor", lambda e, yg=yg, mi=mi, w0=w0, off=off, wp=wp,
                                 first=first, last=last: e.matmul(
                                psG[mi][:, off:off + wp],
                                yg[:, mi * 128:(mi + 1) * 128],
                                yg[:, w0 + off:w0 + off + wp],
                                start=first, stop=last),
                                inc=([(tC, 1)] if (mi == 5 and gi == 3) else []))
                yt_free[bi] = W(tC)
            gdone = W(tC)
            p.on("scalar", lambda e: e.copy(self.Ssb[:, 2 * D + 256:3 * D], psG[2][:, 0:768]),
                 wait=[gdone], inc=[(sC, 1)])
            p.on("scalar", lambda e: e.copy(self.Ssb[:, 3 * D + 384:4 * D], psG[3][:, 0:640]),
                 inc=[(sC, 1)])
            p.on("scalar", lambda e: e.copy(self.Ssb[:, 4 * D + 512:5 * D], psG[4][:, 0:512]),
                 inc=[(sC, 1)])
            p.on("scalar", lambda e: e.copy(self.Ssb[:, 5 * D + 640:6 * D], psG[5][:, 0:384]),
                 inc=[(sC, 1)])
            self.p.barrier()

    def phase_stats_fused(self):
        cx, p, d = self.cx, self.p, self.d
        vC, sC, tC = cx.vC, cx.sC, cx.tC
        W = self.w
        NTOT = float(B * L)
        with contextlib.ExitStack() as st:
            sb = lambda n, sh, dt=F32: st.enter_context(self.nc.sbuf_tensor(n, sh, dt))
            ps = lambda n, sh, dt=F32: st.enter_context(self.nc.psum_tensor(n, sh, dt))
            Wq2 = sb("sfWq2", [128, C * D], BF16)
            WT = sb("sfWT", [128, C * D], BF16)
            sycol = sb("sfsycol", [128, 8], BF16)
            gcol = sb("sfgcol", [128, 8], F32)
            bcol = sb("sfbcol", [128, 8], F32)
            t1 = sb("sft1", [128, 8], F32)
            t2 = sb("sft2", [128, 8], F32)
            t3 = sb("sft3", [128, 8], F32)
            ucol = sb("sfu", [128, 1024], F32)
            psT = ps("sfpsT", [128, 1024])
            psR = ps("sfpsR", [128, 128])
            psRb = ps("sfpsRb", [128, 128], BF16)
            psE = ps("sfpsE", [1, 1024])
            ldS = p.sem("sfld")
            for dc in range(1, C):
                for dj in range(dc):
                    p.on("tensor", lambda e, dc=dc, dj=dj: e.transpose(
                        psRb[:], self.Ssb[:, dj * D + dc * 128:dj * D + (dc + 1) * 128],
                        self.idb[:]), wait=[W(sC), W(vC)], inc=[(tC, 1)])
                    p.on("vector", lambda e, dc=dc, dj=dj: e.tensor_copy(
                        self.Ssb[:, dc * D + dj * 128:dc * D + (dj + 1) * 128], psRb[:]),
                        wait=[W(tC)], inc=[(vC, 1)])
            mirr = [W(vC)]
            p.on("vector", lambda e: e.tensor_copy(ucol[0:1, :], self.syrow[:]),
                 inc=[(vC, 1)])
            for cc2 in range(C):
                p.on("tensor", lambda e, cc2=cc2: e.transpose(
                    psR[:, 0:1], ucol[0:1, cc2 * 128:(cc2 + 1) * 128], self.idf[0:1, 0:1]),
                    wait=[W(vC)], inc=[(tC, 1)])
                p.on("vector", lambda e, cc2=cc2: e.tensor_copy(
                    sycol[:, cc2:cc2 + 1], psR[:, 0:1]), wait=[W(tC)], inc=[(vC, 1)])
            for br, (wn, gn, bnm, An, Bn) in enumerate((
                    ("Wq", "gq", "bq", "Aq", "Bq"),
                    ("Wk", "gk", "bk", "Ak", "Bk"),
                    ("Wv", "gv", "bv", "Av", "Bv"))):
                p.on("gpsimd", lambda e, wn=wn: cast_w(e, Wq2, d[wn + "U"]),
                     wait=[W(sC), W(vC)], inc=[(cx.qW, 16)])
                p.on("gpsimd", lambda e, wn=wn: cast_w(e, WT, d[wn + "T"]),
                     inc=[(cx.qW, 16)])
                p.on("sync", lambda e, gn=gn: e.dma_start(out=gcol[:], in_=d[gn][:, :]),
                     wait=[W(vC)], inc=[(ldS, 16)])
                p.on("sync", lambda e, bnm=bnm: e.dma_start(out=bcol[:], in_=d[bnm][:, :]),
                     inc=[(ldS, 16)])
                wl = [(cx.qW, cx.qW.count), (ldS, ldS.count)]
                for ec in range(C):
                    p.on("tensor", None, wait=wl + mirr + [W(vC)])
                    for dc in range(C):
                        for half in range(2):
                            p.on("tensor", lambda e, ec=ec, dc=dc, half=half: e.matmul(
                                psT[:, half * 512:(half + 1) * 512],
                                WT[:, dc * D + ec * 128:dc * D + (ec + 1) * 128],
                                self.Ssb[:, dc * D + half * 512:dc * D + (half + 1) * 512],
                                start=(dc == 0), stop=(dc == C - 1)),
                                inc=([(tC, 1)] if dc == C - 1 and half == 1 else []))
                    p.on("vector", lambda e, ec=ec: e.tensor_mul(
                        ucol[:], psT[:, 0:1024], Wq2[:, ec * D:(ec + 1) * D]),
                        wait=[W(tC), W(vC)], inc=[(vC, 1)])
                    p.on("vector", lambda e, ec=ec: e.reduce_sum(
                        t2[:, ec:ec + 1], ucol[:], axis=AX.X), wait=[W(vC)], inc=[(vC, 1)])
                p.on("tensor", None, wait=[W(vC)])
                for dc in range(C):
                    for half in range(2):
                        p.on("tensor", lambda e, dc=dc, half=half: e.matmul(
                            psE[0:1, half * 512:(half + 1) * 512], sycol[:, dc:dc + 1],
                            WT[:, dc * D + half * 512:dc * D + (half + 1) * 512],
                            start=(dc == 0), stop=(dc == C - 1)),
                            inc=([(tC, 1)] if dc == C - 1 and half == 1 else []))
                p.on("vector", lambda e: e.tensor_copy(ucol[0:1, :], psE[0:1, :]),
                     wait=[W(tC)], inc=[(vC, 1)])
                for cc2 in range(C):
                    p.on("tensor", lambda e, cc2=cc2: e.transpose(
                        psR[:, 0:1], ucol[0:1, cc2 * 128:(cc2 + 1) * 128], self.idf[0:1, 0:1]),
                        wait=[W(vC)], inc=[(tC, 1)])
                    p.on("vector", lambda e, cc2=cc2: e.tensor_copy(
                        t1[:, cc2:cc2 + 1], psR[:, 0:1]), wait=[W(tC)], inc=[(vC, 1)])
                p.on("vector", lambda e: e.tensor_scalar_mul(t1[:], t1[:], 1.0 / NTOT),
                     wait=[W(vC)], inc=[(vC, 1)])
                p.on("vector", lambda e: e.tensor_scalar_mul(t2[:], t2[:], 1.0 / NTOT),
                     wait=[W(vC)], inc=[(vC, 1)])
                p.on("vector", lambda e: e.tensor_mul(t3[:], t1[:], t1[:]),
                     wait=[W(vC)], inc=[(vC, 1)])
                p.on("vector", lambda e: e.tensor_sub(t2[:], t2[:], t3[:]),
                     wait=[W(vC)], inc=[(vC, 1)])
                p.on("scalar", lambda e: e.activation(t2[:], t2[:], AF.Ln,
                                                      bias=self.epsc[:, 0:1]),
                     wait=[W(vC)], inc=[(sC, 1)])
                p.on("scalar", lambda e: e.activation(t2[:], t2[:], AF.Exp, scale=-0.5),
                     wait=[W(sC)], inc=[(sC, 1)])
                p.on("vector", lambda e, An=An: e.tensor_mul(self.AB[An][:], gcol[:], t2[:]),
                     wait=[W(sC)], inc=[(vC, 1)])
                p.on("vector", lambda e, An=An: e.tensor_scalar_mul(
                    self.AB[An][:], self.AB[An][:], 4.0), wait=[W(vC)], inc=[(vC, 1)])
                p.on("vector", lambda e, An=An: e.tensor_mul(t3[:], t1[:], self.AB[An][:]),
                     wait=[W(vC)], inc=[(vC, 1)])
                p.on("vector", lambda e, Bn=Bn: e.tensor_scalar(
                    self.AB[Bn][:], bcol[:], 4.0, -4.0, ALU.mult, ALU.add), inc=[(vC, 1)])
                p.on("vector", lambda e, Bn=Bn: e.tensor_sub(
                    self.AB[Bn][:], self.AB[Bn][:], t3[:]), wait=[W(vC)], inc=[(vC, 1)])
                if self.debug and br == 0:
                    p.on("sync", lambda e: e.dma_start(out=d["DBGS"][:, 0:8], in_=t1[:]),
                         wait=[W(vC), W(sC)], inc=[(cx.qS, 16)])
                    p.on("sync", lambda e: e.dma_start(out=d["DBGS"][:, 8:16], in_=t2[:]),
                         inc=[(cx.qS, 16)])
                    p.on("vector", None, wait=[(cx.qS, cx.qS.count)])
            if self.debug:
                p.on("gpsimd", lambda e: e.dma_start(out=d["DBGSS"][:, :], in_=self.Ssb[:]),
                     wait=[W(vC), W(sC)], inc=[(cx.qW, 16)])
                for i, n in enumerate(("Aq", "Bq", "Ak", "Bk", "Av", "Bv")):
                    p.on("sync", lambda e, i=i, n=n: e.dma_start(
                        out=d["DBGA"][:, i * 8:(i + 1) * 8], in_=self.AB[n][:]),
                        wait=[W(vC), W(sC)], inc=[(cx.qS, 16)])
            p.on("vector", lambda e: e.tensor_mul(self.nmrc[:], self.rc[:, 0:64],
                                                  self.rc[:, 64:128]),
                 wait=[W(vC)], inc=[(vC, 1)])
            p.on("vector", lambda e: e.tensor_scalar_mul(self.nmrc[:], self.nmrc[:], -1.0),
                 wait=[W(vC)], inc=[(vC, 1)])
            self.p.barrier()

    def phase_p23a(self):
        cx, p, d = self.cx, self.p, self.d
        qL, qS, qW, vC, sC, tC = cx.qL, cx.qS, cx.qW, cx.vC, cx.sC, cx.tC
        with contextlib.ExitStack() as st:
            sb = lambda n, sh, dt=F32: st.enter_context(self.nc.sbuf_tensor(n, sh, dt))
            ps = lambda n, sh, dt=F32: st.enter_context(self.nc.psum_tensor(n, sh, dt))
            Wv = sb("aWv", [128, C * D], BF16)
            xin = [sb(f"axin{i}", [128, 4 * D], F32) for i in range(2)]
            xlnf = [sb(f"axlnf{i}", [128, C * 512], BF16) for i in range(2)]
            x0f = [sb(f"ax0f{i}", [128, C * 512], F32) for i in range(2)]
            xlt = sb("axlt", [128, D], BF16)
            vfm = sb("avfm", [128, C * 512], BF16)
            vtm = sb("avtm", [128, 1024], BF16)
            psTB = ps("psTB", [128, 1024], BF16)
            psTF = ps("psTF", [128, 1024], F32)
            psH = Rot([ps(f"apsH{i}", [128, 512], F32) for i in range(2)])
            psV = ps("apsV", [128, 1024], BF16)
            psVV = ps("apsVV", [128, 512], F32)
            vvacc = sb("avvacc", [128, 512], F32)
            spillS = [p.sem("aspill0"), p.sem("aspill1")]
            p.on("gpsimd", lambda e: cast_w(e, Wv, d["WvT"]), inc=[(qW, 16)])
            wld = qW.count
            xin_free = [None, None]
            xlnf_free = [None, None]
            x0f_free = [None, None]
            vv_read = None
            vfm_free = None
            for s in range(self.nseq):
                for t in range(4):
                    ti = s * 4 + t
                    bi = ti % 2
                    t0 = ti * 512
                    p.on("sync", lambda e, t0=t0, bi=bi: e.dma_start(
                        out=xin[bi][:].rearrange("p (g d) -> p g d", g=4),
                        in_=d["x"][t0:t0 + 512, :].rearrange("(g p) d -> p g d", p=128)),
                        wait=([xin_free[bi]] if xin_free[bi] else []), inc=[(qL, 16)])
                    ld_t = qL.count
                    if xlnf_free[bi]:
                        p.on("vector", None, wait=[xlnf_free[bi], x0f_free[bi]])
                    for g in range(4):
                        xg = xin[bi][:, g * D:(g + 1) * D]
                        idx = ti * 4 + g
                        p.on("vector", lambda e, xg=xg, idx=idx: e.tensor_scalar(
                            xlt[:], xg, self.rc[:, 64 + idx:65 + idx],
                            self.nmrc[:, idx:idx + 1], ALU.mult, ALU.add),
                            wait=[(qL, ld_t), self.w(tC)], inc=[(vC, 1)])
                        p.on("tensor", None, wait=[self.w(vC)])
                        for cc in range(C):
                            p.on("tensor", lambda e, cc=cc: e.transpose(
                                psTB[:, cc * 128:(cc + 1) * 128],
                                xlt[:, cc * 128:(cc + 1) * 128], self.idb[:]),
                                inc=([(tC, 1)] if cc == C - 1 else []))
                        for cc in range(C):
                            p.on("tensor", lambda e, xg=xg, cc=cc: e.transpose(
                                psTF[:, cc * 128:(cc + 1) * 128],
                                xg[:, cc * 128:(cc + 1) * 128], self.idf[:]),
                                inc=([(tC, 1)] if cc == C - 1 else []))
                        tcw = tC.count
                        p.on("vector", lambda e, bi=bi, g=g: e.tensor_copy(
                            xlnf[bi][:].rearrange("p (c t) -> p c t", c=C)[:, :, g * 128:(g + 1) * 128],
                            psTB[:].rearrange("p (c t) -> p c t", c=C)),
                            wait=[(tC, tcw)], inc=[(vC, 1)])
                        p.on("vector", lambda e, bi=bi, g=g: e.tensor_copy(
                            x0f[bi][:].rearrange("p (c t) -> p c t", c=C)[:, :, g * 128:(g + 1) * 128],
                            psTF[:].rearrange("p (c t) -> p c t", c=C)),
                            inc=[(vC, 1)])
                    xready = vC.count
                    p.on("sync", lambda e, bi=bi, t0=t0: fm_store(e, d["XLN"], t0, 512, xlnf[bi]),
                         wait=[(vC, xready)], inc=[(spillS[bi], 16)])
                    p.on("sync", lambda e, bi=bi, t0=t0: fm_store(e, d["X0F"], t0, 512, x0f[bi]),
                         inc=[(spillS[bi], 16)])
                    x0f_free[bi] = (spillS[bi], spillS[bi].count)
                    p.on("tensor", None, wait=[(vC, xready), (qW, wld)]
                         + ([vfm_free] if vfm_free else []))
                    for ec in range(C):
                        kb, pb, wv = psH.acquire()
                        if wv:
                            p.on("tensor", None, wait=wv)
                        for dc in range(C):
                            p.on("tensor", lambda e, ec=ec, dc=dc, pb=pb, bi=bi: e.matmul(
                                pb[:], Wv[:, dc * D + ec * 128:dc * D + (ec + 1) * 128],
                                xlnf[bi][:, dc * 512:(dc + 1) * 512],
                                start=(dc == 0), stop=(dc == C - 1)),
                                inc=([(tC, 1)] if dc == C - 1 else []))
                        p.on("scalar", lambda e, ec=ec, pb=pb: e.activation(
                            vfm[:, ec * 512:(ec + 1) * 512], pb[:], AF.Sigmoid,
                            scale=self.AB["Av"][:, ec:ec + 1],
                            bias=self.AB["Bv"][:, ec:ec + 1]),
                            wait=[(tC, tC.count)], inc=[(sC, 1)])
                        psH.rel[kb] = (sC, sC.count)
                    xlnf_free[bi] = (tC, tC.count)
                    xin_free[bi] = (vC, xready)
                    sig_done = sC.count
                    for g2 in range(4):
                        p.on("tensor", None, wait=[(sC, sig_done)])
                        for cc in range(C):
                            p.on("tensor", lambda e, cc=cc, g2=g2: e.transpose(
                                psV[:, cc * 128:(cc + 1) * 128],
                                vfm[:, cc * 512 + g2 * 128:cc * 512 + (g2 + 1) * 128],
                                self.idb[:]), inc=([(tC, 1)] if cc == C - 1 else []))
                        p.on("scalar", lambda e: e.copy(vtm[:], psV[:]),
                             wait=[(tC, tC.count)], inc=[(sC, 1)])
                        p.on("tensor", None, wait=[(sC, sC.count), self.w(vC)])
                        for h in range(HEADS):
                            p.on("tensor", lambda e, h=h: e.matmul(
                                psVV[(h % 2) * 64:(h % 2) * 64 + 64,
                                     (h // 2) * 64:(h // 2) * 64 + 64],
                                vtm[:, h * HD:(h + 1) * HD], vtm[:, h * HD:(h + 1) * HD],
                                start=True, stop=True,
                                tile_position=(0, (h % 2) * 64)),
                                inc=([(tC, 1)] if h == HEADS - 1 else []))
                        first = (t == 0 and g2 == 0)
                        if first:
                            p.on("vector", lambda e: e.tensor_copy(vvacc[:], psVV[:]),
                                 wait=[self.w(tC)], inc=[(vC, 1)])
                        else:
                            p.on("vector", lambda e: e.tensor_add(vvacc[:], vvacc[:], psVV[:]),
                                 wait=[self.w(tC), self.w(vC)], inc=[(vC, 1)])
                    vfm_free = (tC, tC.count)
                p.on("scalar", lambda e, s=s: e.mul(self.vvs[s][:], vvacc[:], 0.125),
                     wait=[self.w(vC)], inc=[(sC, 1)])
                vv_read = (sC, sC.count)
            self.p.barrier()

    def phase_p23b(self):
        cx, p, d = self.cx, self.p, self.d
        qL, qS, qW, vC, sC, tC = cx.qL, cx.qS, cx.qW, cx.vC, cx.sC, cx.tC
        with contextlib.ExitStack() as st:
            sb = lambda n, sh, dt=F32: st.enter_context(self.nc.sbuf_tensor(n, sh, dt))
            ps = lambda n, sh, dt=F32: st.enter_context(self.nc.psum_tensor(n, sh, dt))
            Wq = sb("bWq", [128, C * D], BF16)
            Wk = sb("bWk", [128, C * D], BF16)
            Wp = sb("bWp", [128, C * D], BF16)
            Msb = sb("bM", [128, C * D], BF16)
            xl_in = [sb(f"bxl{i}", [128, C * 512], BF16) for i in range(2)]
            x0_in = [sb(f"bx0{i}", [128, C * 512], F32) for i in range(2)]
            qf = sb("bqf", [128, C * 512], BF16)
            kf = sb("bkf", [128, C * 512], BF16)
            qkt = sb("bqkt", [128, C * 512], BF16)
            x1t = sb("bx1t", [128, C * 512], F32)
            x1b = sb("bx1b", [128, C * 512], BF16)
            x1sq = sb("bx1sq", [128, C * 512], BF16)
            s1row = sb("bs1row", [1, L], F32)
            s2row = sb("bs2row", [1, L], F32)
            psH = Rot([ps(f"bpsH{i}", [128, 512], F32) for i in range(2)])
            psO = Rot([ps(f"bpsO{i}", [128, 512], F32) for i in range(2)])
            psM = ps("bpsM", [128, 1024], F32)
            psRS = ps("bpsRS", [1, 512], F32)
            psRS2 = ps("bpsRS2", [1, 512], F32)
            binS = [p.sem("bin0"), p.sem("bin1")]
            x1S = p.sem("bx1S")
            rowS = p.sem("browS")
            for w, dst in (("WqT", Wq), ("WkT", Wk), ("WpT", Wp)):
                p.on("gpsimd", lambda e, w=w, dst=dst: cast_w(e, dst, d[w]), inc=[(qW, 16)])
            wld = qW.count
            in_free = [None, None]
            x1b_free = None
            row_free = None
            for s in range(self.nseq):
                for j in range(C):
                    p.on("tensor", None, wait=[self.w(sC), (qW, wld)])
                    for half in range(2):
                        for hh in range(2):
                            h = 2 * j + hh
                            base = (h % 2) * 64
                            p.on("tensor", lambda e, s=s, j=j, half=half, hh=hh, base=base: e.matmul(
                                psM[base:base + 64, half * 512:(half + 1) * 512],
                                self.vvs[s][base:base + 64, j * 64:(j + 1) * 64],
                                Wp[base:base + 64,
                                   j * D + half * 512:j * D + (half + 1) * 512],
                                start=True, stop=True,
                                tile_position=(base, base)),
                                inc=([(tC, 1)] if half == 1 and hh == 1 else []))
                    p.on("scalar", lambda e, j=j: e.copy(Msb[:, j * D:(j + 1) * D], psM[:]),
                         wait=[self.w(tC)], inc=[(sC, 1)])
                mdone = self.w(sC)
                if self.debug and s == 0:
                    p.on("sync", lambda e: e.dma_start(out=d["DBGM"][:, :], in_=Msb[:]),
                         wait=[mdone], inc=[(qS, 16)])
                for t in range(4):
                    ti = s * 4 + t
                    bi = ti % 2
                    t0 = ti * 512
                    p.on("sync", lambda e, bi=bi, t0=t0: fm_load(e, d["XLN"], t0, 512, xl_in[bi]),
                         wait=([in_free[bi]] if in_free[bi] else []), inc=[(binS[bi], 16)])
                    p.on("sync", lambda e, bi=bi, t0=t0: fm_load(e, d["X0F"], t0, 512, x0_in[bi]),
                         inc=[(binS[bi], 16)])
                    ld_t = (binS[bi], binS[bi].count)
                    if getattr(self, "_dbgqk_free", None):
                        p.on("scalar", None, wait=[self._dbgqk_free])
                        self._dbgqk_free = None
                    p.on("tensor", None, wait=[ld_t, (qW, wld), self.w(vC)])
                    for which, Wt, dst, Acol, Bcol in (("q", Wq, qf, "Aq", "Bq"),
                                                       ("k", Wk, kf, "Ak", "Bk")):
                        for ec in range(C):
                            kb, pb, wv = psH.acquire()
                            if wv:
                                p.on("tensor", None, wait=wv)
                            for dc in range(C):
                                p.on("tensor", lambda e, Wt=Wt, ec=ec, dc=dc, pb=pb, bi=bi: e.matmul(
                                    pb[:], Wt[:, dc * D + ec * 128:dc * D + (ec + 1) * 128],
                                    xl_in[bi][:, dc * 512:(dc + 1) * 512],
                                    start=(dc == 0), stop=(dc == C - 1)),
                                    inc=([(tC, 1)] if dc == C - 1 else []))
                            p.on("scalar", lambda e, dst=dst, ec=ec, pb=pb, Acol=Acol, Bcol=Bcol:
                                 e.activation(dst[:, ec * 512:(ec + 1) * 512], pb[:], AF.Sigmoid,
                                              scale=self.AB[Acol][:, ec:ec + 1],
                                              bias=self.AB[Bcol][:, ec:ec + 1]),
                                 wait=[(tC, tC.count)], inc=[(sC, 1)])
                            psH.rel[kb] = (sC, sC.count)
                    if self.debug and s == 0 and t == 0:
                        p.on("sync", lambda e: e.dma_start(out=d["DBGQ"][:, :], in_=qf[:]),
                             wait=[self.w(sC)], inc=[(qS, 16)])
                        p.on("sync", lambda e: e.dma_start(out=d["DBGK"][:, :], in_=kf[:]),
                             inc=[(qS, 16)])
                        self._dbgqk_free = (qS, qS.count)
                    p.on("vector", lambda e: e.tensor_mul(qkt[:], qf[:], kf[:]),
                         wait=[self.w(sC)], inc=[(vC, 1)])
                    qk_ready = vC.count
                    if getattr(self, "_x1t_free", None):
                        p.on("vector", None, wait=self._x1t_free)
                    p.on("tensor", None, wait=[(vC, qk_ready), mdone])
                    for ec in range(C):
                        kb, pb, wv = psO.acquire()
                        if wv:
                            p.on("tensor", None, wait=wv)
                        for dc in range(C):
                            p.on("tensor", lambda e, ec=ec, dc=dc, pb=pb: e.matmul(
                                pb[:], Msb[:, dc * D + ec * 128:dc * D + (ec + 1) * 128],
                                qkt[:, dc * 512:(dc + 1) * 512],
                                start=(dc == 0), stop=(dc == C - 1)),
                                inc=([(tC, 1)] if dc == C - 1 else []))
                        p.on("vector", lambda e, ec=ec, pb=pb, bi=bi: e.scalar_tensor_tensor(
                            x1t[:, ec * 512:(ec + 1) * 512], pb[:],
                            self.bpc[:, ec:ec + 1],
                            x0_in[bi][:, ec * 512:(ec + 1) * 512], ALU.add, ALU.add),
                            wait=[(tC, tC.count)], inc=[(vC, 1)])
                        psO.rel[kb] = (vC, vC.count)
                    x1_ready = vC.count
                    in_free[bi] = (vC, x1_ready)
                    p.on("sync", lambda e, t0=t0: fm_store(e, d["X1"], t0, 512, x1t),
                         wait=[(vC, x1_ready)], inc=[(x1S, 16)])
                    p.on("vector", lambda e: e.tensor_copy(x1b[:], x1t[:]),
                         wait=[self.w(vC)] + ([x1b_free] if x1b_free else []), inc=[(vC, 1)])
                    p.on("scalar", lambda e: e.activation(x1sq[:], x1b[:], AF.Square),
                         wait=[self.w(vC)], inc=[(sC, 1)])
                    p.on("tensor", None, wait=[self.w(sC)] + ([row_free] if row_free else []))
                    for dc in range(C):
                        p.on("tensor", lambda e, dc=dc: e.matmul(
                            psRS[:], self.ones_c[:], x1b[:, dc * 512:(dc + 1) * 512],
                            start=(dc == 0), stop=(dc == C - 1)),
                            inc=([(tC, 1)] if dc == C - 1 else []))
                    for dc in range(C):
                        p.on("tensor", lambda e, dc=dc: e.matmul(
                            psRS2[:], self.ones_c[:], x1sq[:, dc * 512:(dc + 1) * 512],
                            start=(dc == 0), stop=(dc == C - 1)),
                            inc=([(tC, 1)] if dc == C - 1 else []))
                    rdone = tC.count
                    p.on("vector", lambda e, t=t: e.tensor_copy(
                        s1row[:, t * 512:(t + 1) * 512], psRS[:]), wait=[(tC, rdone)])
                    p.on("vector", lambda e, t=t: e.tensor_copy(
                        s2row[:, t * 512:(t + 1) * 512], psRS2[:]), inc=[(vC, 1)])
                    row_free = (vC, vC.count)
                    x1b_free = (vC, vC.count)
                    self._x1t_free = [(x1S, x1S.count), self.w(vC)]
                p.on("sync", lambda e, s=s: e.dma_start(
                    out=d["S1"][s * L:(s + 1) * L].rearrange("(a b) -> a b", a=1),
                    in_=s1row[:, :]), wait=[self.w(vC)], inc=[(rowS, 16)])
                p.on("sync", lambda e, s=s: e.dma_start(
                    out=d["S1"][TOK + s * L:TOK + (s + 1) * L].rearrange("(a b) -> a b", a=1),
                    in_=s2row[:, :]), inc=[(rowS, 16)])
                row_free = (rowS, rowS.count)
            self.p.barrier()

    def phase_fa(self, l):
        cx, p, d = self.cx, self.p, self.d
        qW, vC, sC, tC = cx.qW, cx.vC, cx.sC, cx.tC
        YIN = d["X1"] if l == 0 else d["Y2"]
        YOUT = d["Y1"] if l == 0 else d["Y3"]
        HOUT = d["HLN0"] if l == 0 else d["HLN1"]
        W = self.w
        with contextlib.ExitStack() as st:
            sb = lambda n, sh, dt=F32: st.enter_context(self.nc.sbuf_tensor(n, sh, dt))
            ps = lambda n, sh, dt=F32: st.enter_context(self.nc.psum_tensor(n, sh, dt))
            pre = f"f{l}"
            Wqkv = sb(pre + "Wqkv", [128, C * 1536], BF16)
            Wo = sb(pre + "Wo", [128, 4 * D], BF16)
            ones_rf = sb(pre + "onesrf", [1, 128], F32)
            tabs = {n: sb(pre + n, [128, 512], F32) for n in ("CSQ", "SSQ", "CSK", "SSK")}
            qscol = sb(pre + "qscol", [128, 1], F32)
            kscol = sb(pre + "kscol", [128, 1], F32)
            lgc = sb(pre + "lgc", [128, 8], F32)
            lbc = sb(pre + "lbc", [128, 8], F32)
            yin = [sb(pre + f"yin{i}", [128, C * 512], F32) for i in range(2)] if l == 0 else None
            ysb = [sb(pre + f"ysb{i}", [128, C * 513], F32) for i in range(2)]
            ybb = [sb(pre + f"ybb{i}", [128, C * 513], BF16) for i in range(2)]
            ybnd = sb(pre + "ybnd", [128, 4], F32)
            qraw = sb(pre + "qraw", [128, 4 * 512], F32)
            kraw = sb(pre + "kraw", [128, 4 * 512], F32)
            l2sq = [sb(pre + f"l2sq{i}", [128, 512], BF16) for i in range(2)]
            l2lr = [sb(pre + f"l2lr{i}", [2, 512], F32) for i in range(2)]
            l2rh = [sb(pre + f"l2rh{i}", [2, 512], BF16) for i in range(2)]
            l2rl = [sb(pre + f"l2rl{i}", [2, 512], BF16) for i in range(2)]
            s1t = sb(pre + "s1t", [128, 512], F32)
            swp = sb(pre + "swp", [128, 512], F32)
            rt1 = sb(pre + "rt1", [128, 512], F32)
            qpair = sb(pre + "qpair", [128, 4 * 512], BF16)
            kring = sb(pre + "kring", [128, 4 * 1024], BF16)
            vring = [sb(pre + f"vr{i}", [128, 512], BF16) for i in range(9)]
            nmc = sb(pre + "nmc", [128, 8], F32)
            nmb = sb(pre + "nmb", [128, 8], BF16)
            nmf = sb(pre + "nmf", [128, 8], F32)
            nmrow0 = sb(pre + "nmrow0", [1, 512], BF16)
            onesm = sb(pre + "onesm", [128, 64], BF16)
            nmrow1 = sb(pre + "nmrow1", [1, 512], BF16)
            rdrow2 = [sb(pre + f"rdrow{i}", [1, 512], F32) for i in range(2)]
            pmx = sb(pre + "pmx", [128, 4], F32)
            avsb = [sb(pre + f"avsb{i}", [64, 512], F32) for i in range(2)]
            rdhi = sb(pre + "rdhi", [1, 512], BF16)
            rdlo = sb(pre + "rdlo", [1, 512], BF16)
            rdsb = sb(pre + "rdsb", [128, 512], F32)
            attnT = [[sb(pre + f"attnT{i}{k}", [128, 512], BF16) for k in range(2)]
                     for i in range(2)]
            osb = sb(pre + "osb", [128, 4 * 512], BF16)
            hsq2 = [sb(pre + f"hsq{i}", [128, 512], BF16) for i in range(2)]
            rowa = sb(pre + "rowa", [1, 512], F32)
            rowb2 = sb(pre + "rowb2", [1, 512], F32)
            m3row = rowa
            v3row = rowb2
            psH = Rot([ps(pre + f"psH{i}", [128, 512], F32) for i in range(2)])
            psSA = [ps(pre + "psSAa", [128, 1024], F32), ps(pre + "psSAb", [128, 1024], F32)]
            psSB = [ps(pre + "psSBa", [128, 512], F32), ps(pre + "psSBb", [128, 512], F32)]
            psRR = psSA[1][0:1, 512:1024]
            psL2regs = [psSA[0][0:2, 512:1024], psSA[1][0:2, 512:1024]]

            inS = [p.sem(pre + "in0"), p.sem(pre + "in1")]
            outSp = [p.sem(pre + "out0"), p.sem(pre + "out1")]
            hSp = [p.sem(pre + "hS0"), p.sem(pre + "hS1")]
            faC = p.sem(pre + "faC")
            rowLS = p.sem(pre + "rowLS")
            self._tab_free = None

            p.on("gpsimd", lambda e: cast_w(e, Wqkv, d[f"wqkv{l}"]), inc=[(qW, 16)])
            p.on("gpsimd", lambda e: cast_w(e, Wo, d[f"wo{l}"]), inc=[(qW, 16)])
            for dst, src in ((ones_rf, "onesrow"), (qscol, f"qs{l}"), (kscol, f"ks{l}"),
                             (lgc, f"lng{l}"), (lbc, f"lnb{l}")):
                p.on("sync", lambda e, dst=dst, src=src: e.dma_start(
                    out=dst[:], in_=d[src][:, :]), inc=[(faC, 16)])
            tabS = p.sem(pre + "tabS")
            wld = (qW, qW.count)
            cl = (faC, faC.count)

            in_free = [None, None]
            p.on("vector", lambda e: e.memset(onesm[:], 1.0), inc=[(cx.vC, 1)])
            for bb2 in range(2):
                for cc in range(4):
                    p.on("vector", lambda e, bb2=bb2, cc=cc: e.memset(
                        ysb[bb2][:, cc * 513:cc * 513 + 1], 0.0), inc=[(vC, 1)])
            for s in range(self.nseq):
                for t in range(4):
                    ti = s * 4 + t
                    bi = ti % 2
                    cur, prv = ti % 2, 1 - ti % 2
                    t0 = ti * 512
                    pos0 = t * 512

                    p.scope(f"yld_{ti}" if ti in (8, 9) else None)
                    if l == 0:
                        p.on("sync", lambda e, bi=bi, t0=t0: fm_load(e, YIN, t0, 512, yin[bi]),
                             wait=([in_free[bi]] if in_free[bi] else []), inc=[(inS[bi], 16)])
                        ldw = (inS[bi], inS[bi].count)
                        p.on("sync", lambda e, t0=t0: e.dma_start(
                            out=rowa[:], in_=d["S1"][t0:t0 + 512].rearrange("(a b) -> a b", a=1)),
                            wait=[W(tC), W(vC)], inc=[(rowLS, 16)])
                        p.on("sync", lambda e, t0=t0: e.dma_start(
                            out=rowb2[:], in_=d["S1"][TOK + t0:TOK + t0 + 512].rearrange("(a b) -> a b", a=1)),
                            inc=[(rowLS, 16)])
                        lw = (rowLS, rowLS.count)
                        p.on("vector", lambda e: e.tensor_scalar_mul(rowa[:], rowa[:], 1.0 / D),
                             wait=[lw, W(sC), W(vC)], inc=[(vC, 1)])
                        p.on("vector", lambda e: e.tensor_scalar_mul(rowb2[:], rowb2[:], 1.0 / D),
                             wait=[W(vC)], inc=[(vC, 1)])
                        p.on("vector", lambda e: e.tensor_mul(swp[0:1, :], rowa[:], rowa[:]),
                             wait=[W(vC)], inc=[(vC, 1)])
                        p.on("vector", lambda e: e.tensor_sub(rowb2[:], rowb2[:], swp[0:1, :]),
                             wait=[W(vC)], inc=[(vC, 1)])
                        p.on("scalar", lambda e: e.activation(rowb2[:], rowb2[:], AF.Ln,
                                                              bias=self.epsc[0:1, 0:1]),
                             wait=[W(vC)], inc=[(sC, 1)])
                        p.on("scalar", lambda e: e.activation(rowb2[:], rowb2[:], AF.Exp,
                                                              scale=-0.5),
                             wait=[W(sC)], inc=[(sC, 1)])
                        p.on("vector", lambda e: e.tensor_mul(rowa[:], rowa[:], rowb2[:]),
                             wait=[W(sC)], inc=[(vC, 1)])
                        p.on("vector", lambda e: e.tensor_scalar_mul(rowa[:], rowa[:], -1.0),
                             wait=[W(vC)], inc=[(vC, 1)])
                        rows_rdy = W(vC)
                        kb0, pb0, wv0 = psH.acquire()
                        kb1, pb1, wv1 = psH.acquire()
                        p.on("tensor", None, wait=wv0 + wv1 + [rows_rdy, cl])
                        p.on("tensor", lambda e, pb0=pb0: e.matmul(
                            pb0[:], ones_rf[:, :], rowb2[0:1, :],
                            start=True, stop=True))
                        p.on("tensor", lambda e, pb1=pb1: e.matmul(
                            pb1[:], ones_rf[:, :], rowa[0:1, :],
                            start=True, stop=True), inc=[(tC, 1)])
                        bdone = W(tC)
                        if self._ysb_free[cur]:
                            p.on("vector", None, wait=self._ysb_free[cur])
                        for cc in range(C):
                            p.on("vector", lambda e, cc=cc, bi=bi, pb0=pb0: e.tensor_mul(
                                s1t[:], yin[bi][:, cc * 512:(cc + 1) * 512], pb0[:]),
                                wait=[bdone, ldw, W(vC)], inc=[(vC, 1)])
                            p.on("vector", lambda e, cc=cc, cur=cur, pb1=pb1: e.tensor_add(
                                ysb[cur][:, cc * 513 + 1:cc * 513 + 513], s1t[:], pb1[:]),
                                wait=[W(vC)], inc=[(vC, 1)])
                        psH.rel[kb0] = (vC, vC.count)
                        psH.rel[kb1] = (vC, vC.count)
                        in_free[bi] = (vC, vC.count)
                        yload = W(vC)
                    else:
                        if self._ysb_free[cur]:
                            p.on("vector", None, wait=self._ysb_free[cur])
                            p.on("sync", lambda e, cur=cur, t0=t0: e.dma_start(
                                out=ysb[cur][:].rearrange("p (c t) -> p c t", c=C)[:, :, 1:513],
                                in_=YIN[:, t0:t0 + 512].rearrange("(c p) t -> p c t", p=128)),
                                wait=self._ysb_free[cur], inc=[(inS[bi], 16)])
                        else:
                            p.on("sync", lambda e, cur=cur, t0=t0: e.dma_start(
                                out=ysb[cur][:].rearrange("p (c t) -> p c t", c=C)[:, :, 1:513],
                                in_=YIN[:, t0:t0 + 512].rearrange("(c p) t -> p c t", p=128)),
                                inc=[(inS[bi], 16)])
                        yload = (inS[bi], inS[bi].count)
                    if t == 0:
                        for cc in range(4, C):
                            p.on("vector", lambda e, cc=cc, cur=cur: e.memset(
                                ysb[cur][:, cc * 513:cc * 513 + 1], 0.0),
                                wait=[yload] + ([W(vC)] if cc == 4 else []), inc=[(vC, 1)])
                    else:
                        for cc in range(4, C):
                            p.on("vector", lambda e, cc=cc, cur=cur: e.tensor_copy(
                                ysb[cur][:, cc * 513:cc * 513 + 1], ybnd[:, cc - 4:cc - 3]),
                                wait=[yload] + ([W(vC)] if cc == 4 else []), inc=[(vC, 1)])
                    yrdy = W(vC)
                    for cc in range(C):
                        eng2 = "vector" if cc % 2 == 0 else "scalar"
                        sm = vC if cc % 2 == 0 else sC
                        p.on(eng2, lambda e, cc=cc, cur=cur: (
                            e.tensor_copy if cc % 2 == 0 else e.copy)(
                            ybb[cur][:, cc * 513:(cc + 1) * 513],
                            ysb[cur][:, cc * 513:(cc + 1) * 513]),
                            wait=[yrdy] + ([W(tC)] if cc < 2 else []), inc=[(sm, 1)])
                    ybrdy = [(vC, vC.count), (sC, sC.count)]
                    for n in ("CSQ", "SSQ", "CSK", "SSK"):
                        p.on("sync", lambda e, n=n, pos0=pos0: e.dma_start(
                            out=tabs[n][:], in_=d[n][:, pos0:pos0 + 512]),
                            wait=([self._tab_free] if self._tab_free else []),
                            inc=[(tabS, 16)])
                    tab_rdy = (tabS, tabS.count)

                    def ysl(cc, off, width, cur):
                        shf = 1 if cc >= 4 else 1
                        base = cc * 513 + (0 if cc >= 4 else 1)
                        return ybb[cur][:, base + off:base + off + width]

                    p.scope(f"qkv_{ti}" if ti in (8, 9) else None)
                    p.on("tensor", None, wait=ybrdy + [wld])
                    for which, dst in (("q", qraw), ("k", kraw)):
                        for ecl in range(4):
                            ec = ecl if which == "q" else 4 + ecl
                            kb, pb, wv = psH.acquire()
                            if wv:
                                p.on("tensor", None, wait=wv)
                            for dc in range(C):
                                p.on("tensor", lambda e, ec=ec, dc=dc, pb=pb, cur=cur: e.matmul(
                                    pb[:], Wqkv[:, dc * 1536 + ec * 128:dc * 1536 + (ec + 1) * 128],
                                    ysl(dc, 0, 512, cur),
                                    start=(dc == 0), stop=(dc == C - 1)),
                                    inc=([(tC, 1)] if dc == C - 1 else []))
                            p.on("scalar", lambda e, dst=dst, ecl=ecl, pb=pb: e.copy(
                                dst[:, ecl * 512:(ecl + 1) * 512], pb[:]),
                                wait=[W(tC), W(vC), W(hSp[0]), W(hSp[1])], inc=[(sC, 1)])
                            psH.rel[kb] = (sC, sC.count)
                    p.scope(f"vmm_{ti}" if ti in (8, 9) else None)
                    for g2 in range(4):
                        slot = (ti * 4 + g2) % 9
                        kb, pb, wv = psH.acquire()
                        if wv:
                            p.on("tensor", None, wait=wv)
                        for dc in range(C):
                            p.on("tensor", lambda e, dc=dc, g2=g2, pb=pb, cur=cur: e.matmul(
                                pb[:], ysl(dc, g2 * 128, 128, cur),
                                Wqkv[:, dc * 1536 + 1024:dc * 1536 + 1536],
                                start=(dc == 0), stop=(dc == C - 1)),
                                inc=([(tC, 1)] if dc == C - 1 else []))
                        p.on("scalar", lambda e, slot=slot, pb=pb: e.copy(
                            vring[slot][:], pb[:]), wait=[W(tC)], inc=[(sC, 1)])
                        psH.rel[kb] = (sC, sC.count)
                    p.scope(f"krs_{ti}" if ti in (8, 9) else None)
                    if t > 0:
                        for cc in range(4):
                            p.on("vector", lambda e, cc=cc: e.tensor_copy(
                                kring[:, cc * 1024:cc * 1024 + 512],
                                kring[:, cc * 1024 + 512:(cc + 1) * 1024]),
                                wait=([W(tC), W(vC)] if cc == 0 else [W(vC)]), inc=[(vC, 1)])
                    else:
                        p.on("vector", None, wait=[W(tC)])
                    p.scope(f"l2n_{ti}" if ti in (8, 9) else None)
                    qkevw = (sC, sC.count)
                    selw = [None, None]
                    psfree = [None, None]
                    minw = [None, None]
                    bcfree = [None, None]
                    for idx, (which, cc) in enumerate(
                            [(w2, c2) for w2 in ("q", "k") for c2 in range(4)]):
                        par = idx % 2
                        src = qraw if which == "q" else kraw
                        scol = qscol if which == "q" else kscol
                        CS = tabs["CSQ" if which == "q" else "CSK"]
                        SS = tabs["SSQ" if which == "q" else "SSK"]
                        psL2p = psL2regs[par]
                        p.on("scalar", lambda e, src=src, cc=cc, par=par: e.activation(
                            l2sq[par][:], src[:, cc * 512:(cc + 1) * 512], AF.Square),
                            wait=[qkevw] + ([selw[par]] if selw[par] else []), inc=[(sC, 1)])
                        sqw = (sC, sC.count)
                        p.on("tensor", lambda e, par=par, psL2p=psL2p: e.matmul(
                            psL2p, self.sel_b[:], l2sq[par][:], start=True, stop=True),
                            wait=[sqw] + ([psfree[par]] if psfree[par] else []),
                            inc=[(tC, 1)])
                        mmw = (tC, tC.count)
                        selw[par] = mmw
                        p.on("scalar", lambda e, par=par, psL2p=psL2p: e.activation(
                            l2lr[par][:], psL2p, AF.Ln),
                            wait=[mmw] + ([minw[par]] if minw[par] else []), inc=[(sC, 1)])
                        psfree[par] = (sC, sC.count)
                        p.on("scalar", lambda e, par=par: e.activation(
                            l2lr[par][:], l2lr[par][:], AF.Exp, scale=-0.5),
                            wait=[W(sC)], inc=[(sC, 1)])
                        lnw = (sC, sC.count)
                        p.on("vector", lambda e, par=par: e.tensor_scalar_min(
                            l2lr[par][:], l2lr[par][:], 1.0e12), wait=[lnw], inc=[(vC, 1)])
                        p.on("vector", lambda e, par=par: e.tensor_copy(
                            l2rh[par][:], l2lr[par][:]),
                            wait=[W(vC)] + ([bcfree[par]] if bcfree[par] else []),
                            inc=[(vC, 1)])
                        p.on("vector", lambda e, par=par: e.tensor_sub(
                            l2lr[par][:], l2lr[par][:], l2rh[par][:]),
                            wait=[W(vC)], inc=[(vC, 1)])
                        p.on("vector", lambda e, par=par: e.tensor_copy(
                            l2rl[par][:], l2lr[par][:]), wait=[W(vC)], inc=[(vC, 1)])
                        rlw = (vC, vC.count)
                        minw[par] = rlw
                        kb, pb, wv = psH.acquire()
                        p.on("tensor", None, wait=wv + [rlw])
                        p.on("tensor", lambda e, pb=pb, par=par: e.matmul(
                            pb[:], self.sel2_b[:], l2rh[par][:], start=True, stop=False))
                        p.on("tensor", lambda e, pb=pb, par=par: e.matmul(
                            pb[:], self.sel2_b[:], l2rl[par][:], start=False, stop=True),
                            inc=[(tC, 1)])
                        bcw = (tC, tC.count)
                        bcfree[par] = bcw
                        p.on("vector", lambda e, src=src, cc=cc, pb=pb, scol=scol:
                             e.scalar_tensor_tensor(
                                 s1t[:], src[:, cc * 512:(cc + 1) * 512], scol[:, 0:1],
                                 pb[:], ALU.mult, ALU.mult),
                             wait=[bcw, cl, W(vC)], inc=[(vC, 1)])
                        psH.rel[kb] = (vC, vC.count)
                        for bb in range(4):
                            srcr = (bb // 2) * 64 + (1 - bb % 2) * 32
                            dstr = (bb // 2) * 64 + (bb % 2) * 32
                            p.on("vector", lambda e, srcr=srcr, dstr=dstr: e.tensor_copy(
                                swp[dstr:dstr + 32, :], s1t[srcr:srcr + 32, :]),
                                wait=[W(vC)], inc=[(vC, 1)])
                        p.on("vector", lambda e, CS=CS: e.tensor_mul(
                            rt1[:], s1t[:], CS[:, 0:512]),
                            wait=[W(vC), tab_rdy], inc=[(vC, 1)])
                        p.on("vector", lambda e, SS=SS: e.tensor_mul(
                            swp[:], swp[:], SS[:, 0:512]),
                            wait=[W(vC)], inc=[(vC, 1)])
                        if which == "q":
                            p.on("vector", lambda e, cc=cc: e.tensor_add(
                                qpair[:, cc * 512:(cc + 1) * 512], rt1[:], swp[:]),
                                wait=[W(vC)], inc=[(vC, 1)])
                        else:
                            p.on("vector", lambda e, cc=cc: e.tensor_add(
                                kring[:, cc * 1024 + 512:(cc + 1) * 1024], rt1[:], swp[:]),
                                wait=[W(vC)], inc=[(vC, 1)])
                    qk_rdy = (vC, vC.count)
                    self._tab_free = qk_rdy
                    for j in range(4):
                        SAa, SAb = psSA
                        p1free = [None, None]
                        p2free = None
                        for g in range(4):
                            sl = g % 2
                            if t == 0:
                                k0, wdt = 512, 128 * (g + 1)
                            else:
                                k0, wdt = 128 * g, 640
                            gw = [qk_rdy]
                            if g < 2:
                                if sl == 0:
                                    gw += ([self._avfree] if self._avfree else [])
                                else:
                                    for hh2 in range(2):
                                        if self._expw2[hh2][0]:
                                            gw += [self._expw2[hh2][0]]
                            else:
                                gw += [p1free[sl]]
                            p.on("tensor", None, wait=gw)
                            for hh, (SA, tp) in enumerate(((SAa, (0, 0)), (SAb, (64, 0)))):
                                rb = hh * 64
                                P1 = SA if sl == 0 else psSB[hh]

                                def s1mm(e, off, wpart, st, sp, P1=None, rb=None, j=None,
                                         g=None, k0=None, tp=None):
                                    return e.matmul(
                                        P1[:, off:off + wpart],
                                        qpair[rb:rb + 64, j * 512 + g * 128:j * 512 + (g + 1) * 128],
                                        kring[rb:rb + 64, j * 1024 + k0 + off:j * 1024 + k0 + off + wpart],
                                        start=st, stop=sp, tile_position=tp)
                                kw = dict(P1=P1, rb=rb, j=j, g=g, k0=k0, tp=tp)
                                if t == 0:
                                    if g > 0:
                                        p.on("tensor", lambda e, kw=kw, g=g: s1mm(
                                            e, 0, g * 128, True, True, **kw))
                                    p.on("tensor", lambda e, kw=kw, g=g: s1mm(
                                        e, g * 128, 128, True, False, **kw))
                                    p.on("tensor", lambda e, P1=P1, g=g: e.matmul(
                                        P1[:, g * 128:(g + 1) * 128], self.idb[:],
                                        self.mA["cur"][:, :], start=False, stop=True),
                                        inc=([(tC, 1)] if hh == 1 else []))
                                else:
                                    p.on("tensor", lambda e, kw=kw: s1mm(
                                        e, 0, 128, True, False, **kw))
                                    p.on("tensor", lambda e, P1=P1: e.matmul(
                                        P1[:, 0:128], self.idb[:], self.mA["prev"][:, :],
                                        start=False, stop=True))
                                    p.on("tensor", lambda e, kw=kw: s1mm(
                                        e, 128, 384, True, True, **kw))
                            if t > 0:
                                p.on("tensor", None,
                                     wait=([p2free] if p2free else
                                           [w3 for hh2 in range(2)
                                            for w3 in ([self._expw2[hh2][1]]
                                                       if self._expw2[hh2][1] else [])]))
                                for hh, (SA, tp) in enumerate(((SAa, (0, 0)), (SAb, (64, 0)))):
                                    rb = hh * 64
                                    p.on("tensor", lambda e, SA=SA, rb=rb, j=j, g=g,
                                         k0=k0, tp=tp: e.matmul(
                                        SA[:, 512:640],
                                        qpair[rb:rb + 64, j * 512 + g * 128:j * 512 + (g + 1) * 128],
                                        kring[rb:rb + 64, j * 1024 + k0 + 512:j * 1024 + k0 + 640],
                                        start=True, stop=False, tile_position=tp,
                                        skip_group_check=True))
                                    p.on("tensor", lambda e, SA=SA: e.matmul(
                                        SA[:, 512:640], self.idb[:],
                                        self.mA["cur"][:, :], start=False, stop=True,
                                        skip_group_check=True),
                                        inc=([(tC, 1)] if hh == 1 else []))
                            simw = W(tC)
                            for hh, SA in enumerate(psSA):
                                col = g if hh == 0 else 4 + g
                                P1 = SA if sl == 0 else psSB[hh]
                                mw = [simw] if hh == 0 else []
                                if g == 0 and hh == 0:
                                    mw += [self._nmfree] if self._nmfree else []
                                if t == 0:
                                    p.on("vector", lambda e, P1=P1, wdt=wdt, col=col:
                                         e.reduce_max(nmc[:, col:col + 1], P1[:, 0:wdt],
                                                      axis=AX.X, negate=True),
                                         wait=mw, inc=[(vC, 1)])
                                elif sl == 0:
                                    p.on("vector", lambda e, SA=SA, col=col: e.reduce_max(
                                        nmc[:, col:col + 1], SA[:, 0:640], axis=AX.X,
                                        negate=True), wait=mw, inc=[(vC, 1)])
                                else:
                                    p.on("vector", lambda e, SA=SA, hh=hh: e.reduce_max(
                                        pmx[:, 2 + hh:3 + hh], SA[:, 512:640], axis=AX.X),
                                        wait=mw, inc=[(vC, 1)])
                                    p.on("vector", lambda e, P1=P1, hh=hh: e.reduce_max(
                                        pmx[:, hh:hh + 1], P1[:, 0:512], axis=AX.X),
                                        inc=[(vC, 1)])
                                    p.on("vector", lambda e, hh=hh: e.tensor_max(
                                        pmx[:, hh:hh + 1], pmx[:, hh:hh + 1],
                                        pmx[:, 2 + hh:3 + hh]), wait=[W(vC)], inc=[(vC, 1)])
                                    p.on("vector", lambda e, hh=hh, col=col: e.tensor_scalar_mul(
                                        nmc[:, col:col + 1], pmx[:, hh:hh + 1], -1.0),
                                        wait=[W(vC)], inc=[(vC, 1)])
                                p.on("vector", lambda e, col=col: e.tensor_copy(
                                    nmb[:, col:col + 1], nmc[:, col:col + 1]),
                                    wait=[W(vC)], inc=[(vC, 1)])
                                p.on("vector", lambda e, col=col: e.tensor_copy(
                                    nmf[:, col:col + 1], nmb[:, col:col + 1]),
                                    wait=[W(vC)], inc=[(vC, 1)])
                                if t > 0 and sl == 0 and hh == 1:
                                    pass
                            p2free = (vC, vC.count)
                            p1free[sl] = (vC, vC.count)
                        maxdone = (vC, vC.count)
                        maxdone = (vC, vC.count)
                        for hh in range(2):
                            cb = 0 if hh == 0 else 4
                            kb, pb, wv = psH.acquire()
                            p.on("tensor", None, wait=wv + [maxdone])
                            for g in range(4):
                                p.on("tensor", lambda e, cb=cb, g=g, pb=pb: e.transpose(
                                    pb[0:1, g * 128:(g + 1) * 128],
                                    nmf[:, cb + g:cb + g + 1], self.idf[:]),
                                    inc=([(tC, 1)] if g == 3 else []))
                            nr = nmrow0 if hh == 0 else nmrow1
                            p.on("vector", lambda e, nr=nr, pb=pb: e.tensor_copy(
                                nr[:], pb[0:1, 0:512]), wait=[W(tC)], inc=[(vC, 1)])
                            psH.rel[kb] = (vC, vC.count)
                        self._nmfree = (tC, tC.count)
                        rowsdone = (vC, vC.count)
                        kcs = [4, 0, 1, 2, 3, 5, 6, 7] if t > 0 else [4, 5, 6, 7]
                        for ki, kc in enumerate(kcs):
                            if kc < 4:
                                q0, qw = 0, 128 * (kc + 1)
                            else:
                                q0, qw = 128 * (kc - 4), 512 - 128 * (kc - 4)
                            if kc >= 4:
                                kslot = 4 * t + (kc - 4)
                            else:
                                kslot = 4 * (t - 1) + kc
                            vs = vring[(s * 16 + kslot) % 9]
                            first_av = ki == 0
                            is_last = ki == len(kcs) - 1
                            par = ki % 2
                            simc = [None, None]
                            for hh in range(2):
                                rb = hh * 64
                                base = 0 if par == 0 else 512
                                SBps = psSB[hh] if par == 0 else psSA[hh]
                                nr = nmrow0 if hh == 0 else nmrow1
                                bw = []
                                if self._expw2[hh][par]:
                                    bw += [self._expw2[hh][par]]
                                if ki == 0:
                                    bw += [rowsdone]
                                p.on("tensor", None, wait=bw)
                                p.on("tensor", lambda e, j=j, rb=rb, kc=kc, q0=q0, qw=qw,
                                     SBps=SBps, base=base: e.matmul(
                                    SBps[:, base + q0:base + q0 + qw],
                                    kring[rb:rb + 64, j * 1024 + kc * 128:j * 1024 + (kc + 1) * 128],
                                    qpair[rb:rb + 64, j * 512 + q0:j * 512 + q0 + qw],
                                    start=True, stop=False, tile_position=(rb, 0)))
                                if kc >= 4:
                                    p.on("tensor", lambda e, SBps=SBps, q0=q0, base=base: e.matmul(
                                        SBps[:, base + q0:base + q0 + 128], self.idb[:],
                                        self.mB["cur"][:, :], start=False, stop=False))
                                else:
                                    p.on("tensor", lambda e, SBps=SBps, kc=kc, base=base: e.matmul(
                                        SBps[:, base + kc * 128:base + (kc + 1) * 128], self.idb[:],
                                        self.mB["prev"][:, :], start=False, stop=False))
                                p.on("tensor", lambda e, SBps=SBps, q0=q0, qw=qw, nr=nr,
                                     base=base: e.matmul(
                                    SBps[:, base + q0:base + q0 + qw], self.ones_r[:, :],
                                    nr[0:1, q0:q0 + qw], start=False, stop=True),
                                    inc=[(tC, 1)])
                                simc[hh] = (tC, tC.count)
                            for hh in range(2):
                                att = attnT[hh][par]
                                base = 0 if par == 0 else 512
                                SBps = psSB[hh] if par == 0 else psSA[hh]
                                aw = [simc[hh]]
                                if self._attfree[hh][par]:
                                    aw += [self._attfree[hh][par]]
                                p.on("scalar", lambda e, SBps=SBps, att=att, q0=q0, qw=qw,
                                     base=base: e.activation(
                                    att[:, q0:q0 + qw], SBps[:, base + q0:base + q0 + qw],
                                    AF.Exp),
                                     wait=aw, inc=[(sC, 1)])
                                self._expw2[hh][par] = (sC, sC.count)
                            for hh in range(2):
                                att = attnT[hh][par]
                                h = 2 * j + hh
                                AVps = psSA[hh]
                                p.on("tensor", None, wait=[self._expw2[hh][par]])
                                p.on("tensor", lambda e, AVps=AVps, att=att, vs=vs, h=h,
                                     q0=q0, qw=qw, fa=first_av, sp=is_last: e.matmul(
                                    AVps[0:64, q0:q0 + qw], vs[:, h * 64:(h + 1) * 64],
                                    att[:, q0:q0 + qw], start=fa, stop=sp,
                                    tile_position=(0, 0)))
                                p.on("tensor", lambda e, AVps=AVps, att=att,
                                     q0=q0, qw=qw, fa=first_av, sp=is_last: e.matmul(
                                    AVps[64:128, q0:q0 + qw], onesm[:, :],
                                    att[:, q0:q0 + qw], start=fa, stop=sp,
                                    tile_position=(0, 64), skip_group_check=True),
                                    inc=[(tC, 1)])
                                self._attfree[hh][par] = (tC, tC.count)
                        avw = [self._attfree[0][(len(kcs) - 1) % 2],
                               self._attfree[1][(len(kcs) - 1) % 2]]
                        for hh in range(2):
                            rb = hh * 64
                            AVps = psSA[hh]
                            p.on("vector", lambda e, AVps=AVps, hh=hh: e.tensor_copy(
                                avsb[hh][0:64, :], AVps[0:64, 0:512]),
                                wait=[avw[hh]] + ([W(sC), W(vC)] if hh == 0 else []),
                                inc=[(vC, 1)])
                            p.on("vector", lambda e, AVps=AVps, hh=hh: e.reciprocal(
                                rdrow2[hh][:], AVps[64:65, 0:512]),
                                wait=[W(vC)], inc=[(vC, 1)])
                        self._avfree = (vC, vC.count)
                        for hh in range(2):
                            rb = hh * 64
                            p.on("vector", lambda e, hh=hh: e.tensor_copy(
                                rdhi[:], rdrow2[hh][:]),
                                wait=[W(vC)] + ([W(tC)] if hh == 1 else []), inc=[(vC, 1)])
                            p.on("vector", lambda e, hh=hh: e.tensor_sub(
                                rdrow2[hh][:], rdrow2[hh][:], rdhi[:]),
                                wait=[W(vC)], inc=[(vC, 1)])
                            p.on("vector", lambda e, hh=hh: e.tensor_copy(
                                rdlo[:], rdrow2[hh][:]), wait=[W(vC)], inc=[(vC, 1)])
                            rowsw = W(vC)
                            kb, pb, wv = psH.acquire()
                            p.on("tensor", None, wait=wv + [rowsw])
                            p.on("tensor", lambda e, pb=pb: e.matmul(
                                pb[:], self.ones_r[:, :], rdhi[:, :], start=True, stop=False))
                            p.on("tensor", lambda e, pb=pb: e.matmul(
                                pb[:], self.ones_r[:, :], rdlo[:, :], start=False, stop=True),
                                inc=[(tC, 1)])
                            p.on("scalar", lambda e, pb=pb: e.copy(rdsb[:], pb[:]),
                                 wait=[W(tC)] + ([W(vC)] if hh == 1 else []), inc=[(sC, 1)])
                            psH.rel[kb] = (sC, sC.count)
                            p.on("vector", lambda e, j=j, rb=rb, hh=hh: e.tensor_mul(
                                osb[rb:rb + 64, j * 512:(j + 1) * 512],
                                avsb[hh][0:64, 0:512], rdsb[0:64, 0:512]),
                                wait=[W(sC), W(vC)], inc=[(vC, 1)])
                    attn_done = (vC, vC.count)
                    p.scope(f"owo_{ti}" if ti in (8, 9) else None)
                    for cc in range(4, C):
                        p.on("vector", lambda e, cc=cc, cur=cur: e.tensor_copy(
                            ybnd[:, cc - 4:cc - 3], ysb[cur][:, cc * 513 + 512:cc * 513 + 513]),
                            wait=[W(vC)], inc=[(vC, 1)])
                    bnd_saved = (vC, vC.count)
                    p.on("tensor", None, wait=[attn_done])
                    for ec in range(C):
                        kb, pb, wv = psH.acquire()
                        if wv:
                            p.on("tensor", None, wait=wv)
                        for dc in range(4):
                            p.on("tensor", lambda e, ec=ec, dc=dc, pb=pb: e.matmul(
                                pb[:], Wo[:, dc * D + ec * 128:dc * D + (ec + 1) * 128],
                                osb[:, dc * 512:(dc + 1) * 512],
                                start=(dc == 0), stop=(dc == 3)),
                                inc=([(tC, 1)] if dc == 3 else []))
                        p.on("vector", lambda e, ec=ec, pb=pb, cur=cur: e.tensor_add(
                            ysb[cur][:, ec * 513 + 1:ec * 513 + 513],
                            ysb[cur][:, ec * 513 + 1:ec * 513 + 513], pb[:]),
                            wait=[W(tC), bnd_saved, W(vC)], inc=[(vC, 1)])
                        psH.rel[kb] = (vC, vC.count)
                    yp_done = (vC, vC.count)
                    if t == 0:
                        for cc in range(4, C):
                            p.on("vector", lambda e, cc=cc, cur=cur: e.memset(
                                ysb[cur][:, cc * 513:cc * 513 + 1], 0.0),
                                wait=[W(vC)], inc=[(vC, 1)])
                    else:
                        for cc in range(4, C):
                            p.on("vector", lambda e, cc=cc, cur=cur, prv=prv: e.tensor_copy(
                                ysb[cur][:, cc * 513:cc * 513 + 1],
                                ysb[prv][:, cc * 513 + 512:cc * 513 + 513]),
                                wait=[W(vC)], inc=[(vC, 1)])
                        if self._ysb_free[prv]:
                            self._ysb_free[prv] = self._ysb_free[prv] + [(vC, vC.count)]
                    for cc in range(C):
                        eng2 = "vector" if cc % 2 == 0 else "scalar"
                        sm = vC if cc % 2 == 0 else sC
                        p.on(eng2, lambda e, cc=cc, cur=cur: (
                            e.tensor_copy if cc % 2 == 0 else e.copy)(
                            ybb[cur][:, cc * 513:(cc + 1) * 513],
                            ysb[cur][:, cc * 513:(cc + 1) * 513]),
                            wait=[W(vC)] + ([W(sC)] if cc < 2 else []), inc=[(sm, 1)])
                    yb2 = [(vC, vC.count), (sC, sC.count)]
                    p.on("sync", lambda e, cur=cur, t0=t0: e.dma_start(
                        out=YOUT[:, t0:t0 + 512].rearrange("(c p) t -> p c t", p=128),
                        in_=ysb[cur][:].rearrange("p (c t) -> p c t", c=C)[:, :, 1:513]),
                        wait=[yp_done], inc=[(outSp[cur], 16)])
                    p.scope(f"hln_{ti}" if ti in (8, 9) else None)
                    def ypsl(cc, off, width, cur):
                        base = cc * 513 + (0 if cc >= 4 else 1)
                        return ybb[cur][:, base + off:base + off + width]
                    p.on("tensor", None, wait=yb2)
                    for dc in range(C):
                        p.on("tensor", lambda e, dc=dc, cur=cur: e.matmul(
                            psRR, self.ones_c[:], ypsl(dc, 0, 512, cur),
                            start=(dc == 0), stop=(dc == C - 1)),
                            inc=([(tC, 1)] if dc == C - 1 else []))
                    p.on("vector", lambda e: e.tensor_copy(m3row[:], psRR[0:1, 0:512]),
                         wait=[W(tC)], inc=[(vC, 1)])
                    sqmm = [None, None]
                    for dc in range(C):
                        hp = dc % 2
                        p.on("scalar", lambda e, dc=dc, cur=cur, hp=hp: e.activation(
                            hsq2[hp][:], ypsl(dc, 0, 512, cur), AF.Square),
                            wait=([W(vC), W(tC)] if dc == 0 else [])
                            + ([sqmm[hp]] if sqmm[hp] else []), inc=[(sC, 1)])
                        sqdone = (sC, sC.count)
                        p.on("tensor", lambda e, dc=dc, hp=hp: e.matmul(
                            psRR, self.ones_c[:],
                            hsq2[hp][:], start=(dc == 0), stop=(dc == C - 1)),
                            wait=[sqdone], inc=[(tC, 1)])
                        sqmm[hp] = (tC, tC.count)
                    p.on("vector", lambda e: e.tensor_copy(v3row[:], psRR[0:1, 0:512]),
                         wait=[W(tC)], inc=[(vC, 1)])
                    p.on("vector", lambda e: e.tensor_scalar_mul(m3row[:], m3row[:], 1.0 / D),
                         wait=[W(vC)], inc=[(vC, 1)])
                    p.on("vector", lambda e: e.tensor_scalar_mul(v3row[:], v3row[:], 1.0 / D),
                         wait=[W(vC)], inc=[(vC, 1)])
                    p.on("vector", lambda e: e.tensor_mul(s1t[0:1, :], m3row[:], m3row[:]),
                         wait=[W(vC)], inc=[(vC, 1)])
                    p.on("vector", lambda e: e.tensor_sub(v3row[:], v3row[:], s1t[0:1, :]),
                         wait=[W(vC)], inc=[(vC, 1)])
                    p.on("scalar", lambda e: e.activation(v3row[:], v3row[:], AF.Ln,
                                                          bias=self.epsc[0:1, 0:1]),
                         wait=[W(vC)], inc=[(sC, 1)])
                    p.on("scalar", lambda e: e.activation(v3row[:], v3row[:], AF.Exp,
                                                          scale=-0.5),
                         wait=[W(sC)], inc=[(sC, 1)])
                    p.on("vector", lambda e: e.tensor_mul(m3row[:], m3row[:], v3row[:]),
                         wait=[W(sC)], inc=[(vC, 1)])
                    p.on("vector", lambda e: e.tensor_scalar_mul(m3row[:], m3row[:], -1.0),
                         wait=[W(vC)], inc=[(vC, 1)])
                    rows3 = W(vC)
                    kb0, pb0, wv0 = psH.acquire()
                    kb1, pb1, wv1 = psH.acquire()
                    p.on("tensor", None, wait=wv0 + wv1 + [rows3])
                    p.on("tensor", lambda e, pb0=pb0: e.matmul(
                        pb0[:], ones_rf[:, :], v3row[:, :], start=True, stop=True))
                    p.on("tensor", lambda e, pb1=pb1: e.matmul(
                        pb1[:], ones_rf[:, :], m3row[:, :], start=True, stop=True),
                        inc=[(tC, 1)])
                    bc3 = W(tC)
                    for cc in range(C):
                        p.on("vector", lambda e, cc=cc, cur=cur, pb0=pb0: e.tensor_mul(
                            rt1[:], ysb[cur][:, cc * 513 + (0 if cc >= 4 else 1):
                                             cc * 513 + (0 if cc >= 4 else 1) + 512], pb0[:]),
                            wait=[bc3, W(vC)], inc=[(vC, 1)])
                        p.on("vector", lambda e, pb1=pb1: e.tensor_add(rt1[:], rt1[:], pb1[:]),
                             wait=[W(vC)], inc=[(vC, 1)])
                        hdst = qraw if cc < 4 else kraw
                        p.on("vector", lambda e, cc=cc, hdst=hdst: e.tensor_scalar(
                            hdst[:, (cc % 4) * 512:(cc % 4 + 1) * 512], rt1[:],
                            lgc[:, cc:cc + 1], lbc[:, cc:cc + 1], ALU.mult, ALU.add),
                            wait=[W(vC), cl], inc=[(vC, 1)])
                    psH.rel[kb0] = (vC, vC.count)
                    psH.rel[kb1] = (vC, vC.count)
                    hrdy = (vC, vC.count)
                    p.on("gpsimd", lambda e, t0=t0: e.dma_start(
                        out=HOUT[0:512, t0:t0 + 512].rearrange("(c p) t -> p c t", p=128),
                        in_=qraw[:].rearrange("p (c t) -> p c t", c=4)),
                        wait=[hrdy], inc=[(hSp[cur], 16)])
                    p.on("gpsimd", lambda e, t0=t0: e.dma_start(
                        out=HOUT[512:1024, t0:t0 + 512].rearrange("(c p) t -> p c t", p=128),
                        in_=kraw[:].rearrange("p (c t) -> p c t", c=4)),
                        inc=[(hSp[cur], 16)])
                    self._ysb_free[cur] = [(outSp[cur], outSp[cur].count),
                                           (hSp[cur], hSp[cur].count), hrdy]
            p.scope(None)
            self.p.barrier()
            self._ysb_free = [None, None]
            self._avfree = None
            self._nmfree = None
            self._expw2 = [[None, None], [None, None]]
            self._attfree = [[None, None], [None, None]]

    def phase_fbc(self, l):
        cx, p, d = self.cx, self.p, self.d
        qW, vC, sC, tC = cx.qW, cx.vC, cx.sC, cx.tC
        W = self.w
        YIN = d["Y1"] if l == 0 else d["Y3"]
        HIN = d["HLN0"] if l == 0 else d["HLN1"]
        T = 512
        NTF = (self.nseq * L) // T
        with contextlib.ExitStack() as st:
            sb = lambda n, sh, dt=F32: st.enter_context(self.nc.sbuf_tensor(n, sh, dt))
            ps = lambda n, sh, dt=F32: st.enter_context(self.nc.psum_tensor(n, sh, dt))
            pre = f"g{l}"
            W1 = sb(pre + "W1", [128, C * 5460], BF16)
            W2 = sb(pre + "W2", [128, 22 * D], BF16)
            b1a = sb(pre + "b1a", [128, 22], F32)
            b1g = sb(pre + "b1g", [128, 22], F32)
            b2c = sb(pre + "b2c", [128, 8], F32)
            hin = [sb(pre + f"hin{i}", [128, C * T], BF16) for i in range(2)]
            yin = sb(pre + "yin", [128, C * T], F32)
            x1in = sb(pre + "x1in", [128, C * T], F32) if l == 1 else None
            gel = sb(pre + "gel", [128, T], BF16)
            gated = sb(pre + "gated", [128, 22 * T], BF16)
            psA = Rot([ps(pre + f"psA{i}", [128, T], F32) for i in range(2)])
            psG = Rot([ps(pre + f"psG{i}", [128, T], F32) for i in range(2)])
            psO = Rot([ps(pre + f"psO{i}", [128, T], F32) for i in range(2)])
            inS = [p.sem(pre + "in0"), p.sem(pre + "in1")]
            yS = p.sem(pre + "yS")
            xS = p.sem(pre + "xS")
            outS = p.sem(pre + "outS")
            p.on("gpsimd", lambda e: cast_w(e, W1, d[f"w1_{l}"]), inc=[(qW, 16)])
            p.on("gpsimd", lambda e: cast_w(e, W2, d[f"w2_{l}"]), inc=[(qW, 16)])
            for dst, src in ((b1a, f"b1a{l}"), (b1g, f"b1g{l}"), (b2c, f"b2_{l}")):
                p.on("sync", lambda e, dst=dst, src=src: e.dma_start(
                    out=dst[:], in_=d[src][:, :]), inc=[(cx.qL, 16)])
            wld = (qW, qW.count)
            cl = (cx.qL, cx.qL.count)
            hin_free = [None, None]
            y_free = None
            x_free = None
            for ti in range(NTF):
                bi = ti % 2
                t0 = ti * T
                p.on("sync", lambda e, bi=bi, t0=t0: fm_load(e, HIN, t0, T, hin[bi]),
                     wait=([hin_free[bi]] if hin_free[bi] else []), inc=[(inS[bi], 16)])
                p.on("sync", lambda e, t0=t0: fm_load(e, YIN, t0, T, yin),
                     wait=([y_free] if y_free else []), inc=[(yS, 16)])
                if l == 1:
                    p.on("sync", lambda e, t0=t0: fm_load(e, d["X1"], t0, T, x1in),
                         wait=([x_free] if x_free else []), inc=[(xS, 16)])
                ldw = (inS[bi], inS[bi].count)
                ydw = (yS, yS.count)
                xdw = (xS, xS.count)
                p.on("tensor", None, wait=[ldw, wld, cl])
                for c in range(22):
                    rows = 128 if c < 21 else 42
                    ka, pa, wva = psA.acquire()
                    if wva:
                        p.on("tensor", None, wait=wva)
                    for dc in range(C):
                        p.on("tensor", lambda e, c=c, dc=dc, pa=pa, bi=bi, rows=rows: e.matmul(
                            pa[0:rows, :], W1[:, dc * 5460 + c * 128:dc * 5460 + c * 128 + rows],
                            hin[bi][:, dc * T:(dc + 1) * T],
                            start=(dc == 0), stop=(dc == C - 1)),
                            inc=([(tC, 1)] if dc == C - 1 else []))
                    kg, pg, wvg = psG.acquire()
                    if wvg:
                        p.on("tensor", None, wait=wvg)
                    for dc in range(C):
                        p.on("tensor", lambda e, c=c, dc=dc, pg=pg, bi=bi, rows=rows: e.matmul(
                            pg[0:rows, :],
                            W1[:, dc * 5460 + 2730 + c * 128:dc * 5460 + 2730 + c * 128 + rows],
                            hin[bi][:, dc * T:(dc + 1) * T],
                            start=(dc == 0), stop=(dc == C - 1)),
                            inc=([(tC, 1)] if dc == C - 1 else []))
                    mmd = W(tC)
                    p.on("scalar", lambda e, c=c, pg=pg, rows=rows: e.activation(
                        gel[0:rows, :], pg[0:rows, :], AF.Gelu, bias=b1g[0:rows, c:c + 1]),
                        wait=[mmd, W(vC)], inc=[(sC, 1)])
                    p.on("vector", lambda e, c=c, pa=pa, rows=rows: e.scalar_tensor_tensor(
                        gated[0:rows, c * T:(c + 1) * T], pa[0:rows, :],
                        b1a[0:rows, c:c + 1], gel[0:rows, :], ALU.add, ALU.mult),
                        wait=[W(sC), W(vC)], inc=[(vC, 1)])
                    psA.rel[ka] = (vC, vC.count)
                    psG.rel[kg] = (vC, vC.count)
                gdone = (vC, vC.count)
                hin_free[bi] = (tC, tC.count)
                p.on("tensor", None, wait=[gdone])
                for ec in range(C):
                    ko, po, wvo = psO.acquire()
                    if wvo:
                        p.on("tensor", None, wait=wvo)
                    for c in range(22):
                        rows = 128 if c < 21 else 42
                        p.on("tensor", lambda e, ec=ec, c=c, po=po, rows=rows: e.matmul(
                            po[:], W2[0:rows, c * D + ec * 128:c * D + (ec + 1) * 128],
                            gated[0:rows, c * T:(c + 1) * T],
                            start=(c == 0), stop=(c == 21)),
                            inc=([(tC, 1)] if c == 21 else []))
                    if l == 1:
                        p.on("vector", lambda e, ec=ec: e.tensor_add(
                            yin[:, ec * T:(ec + 1) * T], yin[:, ec * T:(ec + 1) * T],
                            x1in[:, ec * T:(ec + 1) * T]),
                            wait=[W(vC), ydw, xdw], inc=[(vC, 1)])
                    p.on("vector", lambda e, ec=ec, po=po: e.scalar_tensor_tensor(
                        yin[:, ec * T:(ec + 1) * T], po[:], b2c[:, ec:ec + 1],
                        yin[:, ec * T:(ec + 1) * T], ALU.add, ALU.add),
                        wait=[W(tC), W(vC)] + ([ydw] if l == 0 else []),
                        inc=[(vC, 1)])
                    psO.rel[ko] = (vC, vC.count)
                ydone = (vC, vC.count)
                x_free = ydone
                OUTD = d["Y2"] if l == 0 else d["out"]
                p.on("sync", lambda e, t0=t0, OUTD=OUTD: fm_store(e, OUTD, t0, T, yin, nch=C),
                     wait=[ydone], inc=[(outS, 16)])
                y_free = (outS, outS.count)
            self.p.barrier()

    def build(self):
        self.declare_io()
        self.persistent()
        if "S" in self.phases and not self.fused:
            self.phase_stats()
        if self.fused and "S" in self.phases:
            with contextlib.ExitStack() as sst:
                self.Ssb = sst.enter_context(
                    self.nc.sbuf_tensor("Ssb", [128, C * D], BF16))
                self.syrow = sst.enter_context(
                    self.nc.sbuf_tensor("syrow", [1, 1024], F32))
                self.phase_p1()
                self.phase_gramB()
                self.phase_stats_fused()
        if "P23" in self.phases:
            with contextlib.ExitStack() as pst:
                self.vvs = [pst.enter_context(
                    self.nc.sbuf_tensor(f"vvs{s}", [128, 512], BF16))
                    for s in range(SPC)]
                self.phase_p23a()
                self.phase_p23b()
        if "Fa0" in self.phases:
            self.phase_fa(0)
        if "Fbc0" in self.phases:
            self.phase_fbc(0)
        if "Fa1" in self.phases:
            self.phase_fa(1)
        if "Fbc1" in self.phases:
            self.phase_fbc(1)
        self.p.on("sync", None, wait=[self.w(self.cx.qS)])
        self.p.emit(self.nc)
        self.cx.close()
        return self.d


def kernel(**inputs):
    g = prep_inputs(inputs)
    X = g["X"]
    nc = bass.Bass("TRN2", target_bir_lowering=False)
    m = Main(nc, fused=True, debug=False)
    m.build()
    in_maps = []
    for c in range(NCORE):
        im = {k: g[k] for k in g if k != "X" and k in m.d}
        im["x"] = np.ascontiguousarray(np.concatenate(
            [X[c * TOK:(c + 1) * TOK], X[:c * TOK], X[(c + 1) * TOK:]], 0))
        in_maps.append(im)
    res = run_bass_kernel_spmd(nc, in_maps, core_ids=list(range(NCORE)))
    out = np.concatenate([np.asarray(res.results[c]["out"]).T
                          for c in range(NCORE)], 0)
    return np.ascontiguousarray(out).reshape(B, L, D)
